# version 13
# baseline (speedup 1.0000x reference)
import math
import os
import sys
from contextlib import ExitStack

if "/opt/trn_rl_repo" not in sys.path:
    sys.path.insert(0, "/opt/trn_rl_repo")

import numpy as np
import ml_dtypes

import concourse.bass as bass
import concourse.mybir as mybir
from concourse.bass_utils import run_bass_kernel_spmd

F32 = mybir.dt.float32
BF16 = mybir.dt.bfloat16
I32 = mybir.dt.int32
AF = mybir.ActivationFunctionType
ALU = mybir.AluOpType
AXX = mybir.AxisListType.X

D = 1024
T = 2048
BC = 2
DB = D // 128
TC = 512
NCH = T // TC
DI = 2048
DIB = DI // 128
NST = 16
RK = 64
HF = 4096
HB = HF // 128
FFT = 256
C30 = math.exp(-30.0)
LNEPS = 1e-5
WKVEPS = 1e-9

STAGES = int(os.environ.get("KERN_STAGES", "15"))
_UN = [0]


def _un(nm):
    _UN[0] += 1
    return f"{nm}_{_UN[0]}"



def seg(ap, nblk, width, lead=0):
    r = ap.rearrange("p (b w) -> p b w", b=nblk)
    return r[:, :, lead:lead + width] if lead else r


def fbc(ap, nblk, width):
    return ap.rearrange("p w -> p () w").broadcast_to([128, nblk, width])


def sbc(ap, nblk, width):
    return ap.rearrange("p b -> p b ()").broadcast_to([128, nblk, width])


class _FakeIns:
    __slots__ = ("incs", "post_fence")

    def __init__(self):
        self.incs = []
        self.post_fence = False

    def then_inc(self, sem, val):
        self.incs.append((sem, val))
        return self


class _Rec:

    def __init__(self, name, ctx):
        self._name = name
        self._ctx = ctx

    OPTOUT = ("X", "mXX", "mXXb", "mUB", "mDT", "mWT", "mYA", "mSZ", "mXCc",
              "mHH", "mYB", "fT0", "fHG", "fHQ", "fXQ", "aK", "aV", "aR",
              "aXKb", "aXVb", "aXRb", "aY", "aXN", "HST0", "HST1", "CCAR0",
              "CCAR1", "XPC", "PQS")

    @staticmethod
    def _base(nm):
        parts = nm.rsplit("_", 1)
        return parts[0] if len(parts) == 2 and parts[1].isdigit() else nm

    def __getattr__(self, meth):
        nm, ctx = self._name, self._ctx

        def call(*a, **kw):
            if meth in ("wait_ge", "nop", "dma_start", "engine_nop"):
                if meth == "dma_start":
                    pass
                fi = _FakeIns()
                ctx.log.append((nm, meth, a, kw, fi))
                return fi
            outs, ins_ = set(), set()
            for k, v_ in kw.items():
                t = getattr(v_, "tensor", None)
                if t is None or _Rec._base(t.name) in _Rec.OPTOUT:
                    continue
                (outs if k in ("out", "accum_out") else ins_).add(t.name)
            for v_ in a:
                t = getattr(v_, "tensor", None)
                if t is not None and _Rec._base(t.name) not in _Rec.OPTOUT:
                    ins_.add(t.name)
            special = meth in ("tensor_reduce", "reduce_max", "reduce_sum",
                               "reciprocal")
            if not special:
                cmp_ops = (ALU.is_gt, ALU.is_ge, ALU.is_lt, ALU.is_le,
                           ALU.is_equal, ALU.not_equal)
                for opk in ("op", "op0", "op1"):
                    if kw.get(opk) in cmp_ops:
                        special = True
                        break
            if not special:
                odt = getattr(kw.get("out"), "dtype", None)
                if odt is not None:
                    for k, v_ in kw.items():
                        if k in ("out", "accum_out"):
                            continue
                        dt_ = getattr(v_, "dtype", None)
                        if dt_ is not None and dt_ != odt:
                            special = True
                            break
            if not special:
                for v_ in list(kw.values()) + list(a):
                    apl = getattr(v_, "ap", None)
                    if apl is not None and len(apl) >= 3:
                        special = True
                        break
            pw, pr = ctx.pend_w[nm], ctx.pend_r[nm]
            if nm in ("v", "s", "g") and (
                    special or (ins_ & pw) or (outs & pw) or (outs & pr)):
                ctx.fence(nm)
            fi = _FakeIns()
            fi.post_fence = special and nm in ("v", "s", "g")
            ctx.log.append((nm, meth, a, kw, fi))
            ctx.pend_w[nm] |= outs
            ctx.pend_r[nm] |= ins_
            return fi
        return call


class Ctx:
    ENGS = ("v", "s", "t", "g", "q")

    def __init__(self, nc, block, stack):
        self.nc = nc
        self.block = block
        self.stack = stack
        self.log = []
        self.e = {k: _Rec(k, self) for k in self.ENGS}
        self.sem = {k: stack.enter_context(nc.semaphore(f"sem_{k}")) for k in self.ENGS}
        self.dsem = {}
        self.cnt = {k: 0 for k in self.ENGS}
        self.dcnt = {}
        self.lastw = {k: dict() for k in self.ENGS}
        self.pend_w = {k: set() for k in self.ENGS}
        self.pend_r = {k: set() for k in self.ENGS}
        self.fcnt = {k: -1 for k in self.ENGS}

    def finalize(self):
        def replay(eng, name):
            for (nm, meth, a, kw, fi) in self.log:
                if nm != name:
                    continue
                ins = getattr(eng, meth)(*a, **kw)
                for sem, val in fi.incs:
                    ins.then_inc(sem, val)

        block = self.block
        @block.vector
        def _(e):
            replay(e, "v")
        @block.scalar
        def _(e):
            replay(e, "s")
        @block.tensor
        def _(e):
            replay(e, "t")
        @block.gpsimd
        def _(e):
            replay(e, "g")
        @block.sync
        def _(e):
            replay(e, "q")

    def inc(self, e, ins):
        ins.then_inc(self.sem[e], 1)
        self.cnt[e] += 1
        tok = (e, self.cnt[e])
        if getattr(ins, "post_fence", False):
            self.fence(e)
        return tok

    def tok(self, e):
        return (e, self.cnt[e])

    def fence(self, e):
        self.pend_w[e].clear()
        self.pend_r[e].clear()
        if self.fcnt[e] == self.cnt[e]:
            return
        self.e[e].wait_ge(self.sem[e], self.cnt[e])
        self.fcnt[e] = self.cnt[e]

    def inc_fence(self, e, ins):
        tk = self.inc(e, ins)
        self.e[e].wait_ge(self.sem[e], self.cnt[e])
        return tk

    def dma(self, qe, stream, out, in_, **kw):
        if stream not in self.dsem:
            self.dsem[stream] = self.stack.enter_context(
                self.nc.semaphore(f"ds_{stream}"))
            self.dcnt[stream] = 0
        ins = self.e[qe].dma_start(out=out, in_=in_, **kw)
        ins.then_inc(self.dsem[stream], 16)
        self.dcnt[stream] += 16
        return ("D:" + stream, self.dcnt[stream])

    def wait(self, e, *tokens):
        for tk in tokens:
            if tk is None:
                continue
            pe, val = tk
            if pe == e:
                continue
            if pe.startswith("D:"):
                val = self.dcnt[pe[2:]]
            if self.lastw[e].get(pe, 0) >= val:
                continue
            sem = self.dsem[pe[2:]] if pe.startswith("D:") else self.sem[pe]
            self.e[e].wait_ge(sem, val)
            self.lastw[e][pe] = val

    def barrier(self, engines="vstgq"):
        toks = [(e, self.cnt[e]) for e in engines]
        dt_ = [("D:" + k, self.dcnt[k]) for k in self.dcnt]
        for e in engines:
            self.fence(e)
            self.wait(e, *toks, *dt_)


def build_program():
    nc = bass.Bass("TRN2", target_bir_lowering=False)

    xin = nc.dram_tensor("xin", [BC, D, T], F32, kind="ExternalInput").ap()
    consts = nc.dram_tensor("consts", [128, 8], F32, kind="ExternalInput").ap()
    identb = nc.dram_tensor("identb", [128, 128], BF16, kind="ExternalInput").ap()
    identf = nc.dram_tensor("identf", [128, 128], F32, kind="ExternalInput").ap()
    onesb = nc.dram_tensor("onesb", [128, 128], BF16, kind="ExternalInput").ap()
    onesf = nc.dram_tensor("onesf", [1, 512], F32, kind="ExternalInput").ap()
    selb = nc.dram_tensor("selb", [16, 16 * 128], BF16, kind="ExternalInput").ap()
    wsrc = {}
    for nm in ("wk", "wv", "wr", "wo"):
        wsrc[nm] = nc.dram_tensor(nm, [D, D], F32, kind="ExternalInput").ap()
    avec = nc.dram_tensor("avec", [128, 8 * 5], F32, kind="ExternalInput").ap()
    w1t = nc.dram_tensor("w1t", [D, HF], F32, kind="ExternalInput").ap()
    w2t = nc.dram_tensor("w2t", [HF, D], F32, kind="ExternalInput").ap()
    fvec = nc.dram_tensor("fvec", [128, HB + DB], F32, kind="ExternalInput").ap()
    inwt, outwt, xpwt, dtwt, mvec, alog = [], [], [], [], [], []
    for l in range(2):
        inwt.append(nc.dram_tensor(f"inwt{l}", [D, 2 * DI], F32, kind="ExternalInput").ap())
        outwt.append(nc.dram_tensor(f"outwt{l}", [DI, D], F32, kind="ExternalInput").ap())
        xpwt.append(nc.dram_tensor(f"xpwt{l}", [DI, 128], F32, kind="ExternalInput").ap())
        dtwt.append(nc.dram_tensor(f"dtwt{l}", [RK, DI], F32, kind="ExternalInput").ap())
        mvec.append(nc.dram_tensor(f"mvec{l}", [128, 128], F32, kind="ExternalInput").ap())
        alog.append(nc.dram_tensor(f"alog{l}", [128, DIB * NST], F32, kind="ExternalInput").ap())
    wbf = {nm: nc.dram_tensor(nm + "b", [D, D], BF16, kind="Internal").ap()
           for nm in ("wk", "wv", "wr", "wo")}
    w1q = nc.dram_tensor("w1q", [D, HF], BF16, kind="Internal").ap()
    w2q = nc.dram_tensor("w2q", [HF, D], BF16, kind="Internal").ap()
    inwb = [nc.dram_tensor(f"inwb{l}", [D, 2 * DI], BF16, kind="Internal").ap() for l in range(2)]
    outwb = [nc.dram_tensor(f"outwb{l}", [DI, D], BF16, kind="Internal").ap() for l in range(2)]
    xpwb = [nc.dram_tensor(f"xpwb{l}", [DI, 128], BF16, kind="Internal").ap() for l in range(2)]
    dtwb = [nc.dram_tensor(f"dtwb{l}", [RK, DI], BF16, kind="Internal").ap() for l in range(2)]
    xout = nc.dram_tensor("xout", [BC, D, T], F32, kind="ExternalOutput").ap()

    stack = ExitStack()
    with stack:
        block = stack.enter_context(nc.Block())
        c = Ctx(nc, block, stack)
        v, s, t_, g, q = (c.e[k] for k in Ctx.ENGS)
        sb = lambda nm, shp, dt=F32: stack.enter_context(nc.sbuf_tensor(nm, shp, dt))

        CT = sb("CT", [128, 8])
        for _j, _cv in enumerate([0.0, 1.0, LNEPS, 0.5, -0.5, -1.0, C30, 127.0]):
            nc.const_aps.register(F32, _cv, CT[:, _j:_j + 1])
        IDE = sb("IDE", [128, 128], BF16)
        IDF = sb("IDF", [128, 128])
        ONE = sb("ONE", [128, 128], BF16)
        ONF = sb("ONF", [1, 512])
        SEL = sb("SEL", [16, 16 * 128], BF16)
        AV = sb("AV", [128, 8 * 5])
        OMV = sb("OMV", [128, 8 * 3])
        FV = sb("FV", [128, HB + DB])
        MV = [sb(f"MV{l}", [128, 128]) for l in range(2)]
        ATL = [sb(f"ATL{l}", [128, DIB * NST]) for l in range(2)]
        WS = sb("WS", [128, 2])
        X = sb("X", [128, DB * TC])
        XPC = sb("XPC", [128, DB])
        PQS = sb("PQS", [128, 2 * DB])
        HSTT = [sb(f"HST{l}", [128, NST * DIB], BF16) for l in range(2)]
        CCAR = [sb(f"CCAR{l}", [128, DIB * 3]) for l in range(2)]

        env = dict(nc=nc, c=c, CT=CT, IDE=IDE, IDF=IDF, ONE=ONE, ONF=ONF, AV=AV,
                   OMV=OMV, FV=FV, MV=MV, ATL=ATL, WS=WS, X=X, XPC=XPC, PQS=PQS,
                   HSTT=HSTT, CCAR=CCAR, wbf=wbf, w1q=w1q, w2q=w2q, inwb=inwb,
                   outwb=outwb, xpwb=xpwb, dtwb=dtwb, SEL=SEL)

        ld0 = [c.dma("q", "init", CT[:, :], consts[:, :]),
               c.dma("q", "init", IDE[:, :], identb[:, :]),
               c.dma("q", "init", IDF[:, :], identf[:, :]),
               c.dma("q", "init", ONE[:, :], onesb[:, :]),
               c.dma("q", "init", ONF[:, :], onesf[:, :]),
               c.dma("q", "init", SEL[:, :], selb[:, :]),
               c.dma("q", "init", AV[:, :], avec[:, :]),
               c.dma("q", "init", FV[:, :], fvec[:, :])]
        for l in range(2):
            ld0.append(c.dma("q", "init", MV[l][:, :], mvec[l][:, :]))
            ld0.append(c.dma("q", "init", ATL[l][:, :], alog[l][:, :]))
        c.wait("v", *ld0)
        c.wait("s", *ld0)
        for j in range(3):
            c.inc("v", v.tensor_scalar(out=OMV[:, j * 8:(j + 1) * 8], in0=AV[:, j * 8:(j + 1) * 8],
                                       scalar1=-1.0, scalar2=1.0, op0=ALU.mult, op1=ALU.add))
        for l in range(2):
            c.inc("s", s.activation(out=ATL[l][:, :], in_=ATL[l][:, :], func=AF.Exp))
            c.wait("v", c.tok("s"))
            c.inc("v", v.tensor_scalar_mul(out=ATL[l][:, :], in0=ATL[l][:, :], scalar1=-1.0))
        c.barrier()

        for args in ((wsrc["wk"], wbf["wk"], D, D), (wsrc["wv"], wbf["wv"], D, D),
                     (wsrc["wr"], wbf["wr"], D, D), (wsrc["wo"], wbf["wo"], D, D),
                     (inwt[0], inwb[0], D, 2 * DI), (outwt[0], outwb[0], DI, D),
                     (xpwt[0], xpwb[0], DI, 128), (dtwt[0], dtwb[0], RK, DI),
                     (inwt[1], inwb[1], D, 2 * DI), (outwt[1], outwb[1], DI, D),
                     (xpwt[1], xpwb[1], DI, 128), (dtwt[1], dtwb[1], RK, DI)):
            emit_wcast(c, *args)
            c.barrier("qg")
        emit_wquant(c, env, w1t, w1q, 0, D, HF)
        c.barrier()
        emit_wquant(c, env, w2t, w2q, 1, HF, D)
        c.barrier()

        for b in range(BC):
            zc = CT[:, 0:1]
            c.inc("v", v.tensor_copy(out=XPC[:, :], in_=zc.broadcast_to([128, DB])))
            c.inc("v", v.tensor_scalar(out=PQS[:, 0:DB], in0=zc.broadcast_to([128, DB]),
                                       scalar1=0.0, scalar2=-1.0e30, op0=ALU.mult, op1=ALU.add))
            c.inc("v", v.tensor_copy(out=PQS[:, DB:2 * DB], in_=zc.broadcast_to([128, DB])))
            for l in range(2):
                c.inc("v", v.tensor_copy(out=HSTT[l][:, :], in_=zc.broadcast_to([128, NST * DIB])))
                c.inc("v", v.tensor_copy(out=CCAR[l][:, :], in_=zc.broadcast_to([128, DIB * 3])))
            c.barrier()

            for ch in range(NCH):
                t0 = ch * TC
                ldx = c.dma("q", "xload", X[:, :].rearrange("p (db t) -> p db t", db=DB),
                            xin[b].rearrange("(db p) t -> p db t", p=128)[:, :, t0:t0 + TC])
                for e in ("v", "s", "t"):
                    c.wait(e, ldx)
                if STAGES & 1:
                    emit_attn_a(c, env)
                    c.barrier()
                if STAGES & 2:
                    emit_ffn(c, env)
                    c.barrier()
                if STAGES & 4:
                    emit_mamba(c, env)
                    c.barrier()
                if STAGES & 8:
                    emit_ffn(c, env)
                    c.barrier()
                c.dma("q", "xstore", xout[b].rearrange("(db p) t -> p db t", p=128)[:, :, t0:t0 + TC],
                      X[:, :].rearrange("p (db t) -> p db t", db=DB))
                c.barrier()
        c.barrier()
        c.finalize()
    return nc


def emit_wcast(c, src, dst, K, M):
    nc = c.nc
    colchunk = min(M, 2048)
    with ExitStack() as st:
        wt = [st.enter_context(nc.sbuf_tensor(_un(f"wc_t{i}"), [128, colchunk], BF16))
              for i in range(2)]
        prev_store = [None, None]
        idx = 0
        for i in range((K + 127) // 128):
            pb = min(128, K - i * 128)
            for j in range(M // colchunk):
                p = idx % 2
                idx += 1
                c.wait("g", prev_store[p])
                tk = c.dma("g", f"wci{p}", wt[p][0:pb, :],
                           src[i * 128:i * 128 + pb, j * colchunk:(j + 1) * colchunk])
                c.wait("q", tk)
                prev_store[p] = c.dma(
                    "q", f"wco{p}",
                    dst[i * 128:i * 128 + pb, j * colchunk:(j + 1) * colchunk],
                    wt[p][0:pb, :])


def emit_wquant(c, env, src, dst, widx, K, M):
    nc = c.nc
    v, s, t_, g, q = (c.e[k] for k in Ctx.ENGS)
    ONE, WS = env["ONE"], env["WS"]
    nch, mch = K // 128, M // 512
    with ExitStack() as st:
        sbt = lambda nm, shp, dt=F32: st.enter_context(nc.sbuf_tensor(_un(nm), shp, dt))
        wt = [sbt(f"wq_in{i}", [128, 512]) for i in range(2)]
        acc = sbt("wq_acc", [128, 1])
        red = sbt("wq_red", [128, 1])
        accb = sbt("wq_accb", [128, 1], BF16)
        rws = sbt("wq_rws", [128, 1])
        wqt = [sbt(f"wq_out{i}", [128, 512], BF16) for i in range(2)]
        uq = sbt("wq_u", [128, 512])
        iq = sbt("wq_i", [128, 512], I32)
        cq = sbt("wq_c", [128, 512])
        vq = sbt("wq_v", [128, 512])
        ps = st.enter_context(nc.psum_tensor(_un("wq_ps"), [128, 1], F32))
        first = True
        prev_red = [None, None]
        for idx in range(nch * mch):
            i, j = divmod(idx, mch)
            p = idx % 2
            c.wait("q", prev_red[p])
            tk = c.dma("q", f"wq{p}", wt[p][:, :], src[i * 128:(i + 1) * 128, j * 512:(j + 1) * 512])
            c.wait("v", tk)
            c.fence("v")
            c.inc_fence("v", v.tensor_reduce(out=red[:, :], in_=wt[p][:, :], axis=AXX,
                                             op=ALU.add, apply_absolute_value=True))
            if first:
                rt = c.inc("v", v.tensor_copy(out=acc[:, :], in_=red[:, :]))
                first = False
            else:
                rt = c.inc("v", v.tensor_add(out=acc[:, :], in0=acc[:, :], in1=red[:, :]))
            prev_red[p] = rt
        c.fence("v")
        cb = c.inc("v", v.tensor_copy(out=accb[:, :], in_=acc[:, :]))
        c.wait("t", cb)
        mm = c.inc("t", t_.matmul(ps[:, :], lhsT=ONE[:, :], rhs=accb[:, :],
                                  start=True, stop=True))
        c.wait("v", mm)
        c.inc("v", v.tensor_scalar(out=WS[:, widx:widx + 1], in0=ps[:, :],
                                   scalar1=1.0 / (K * M), scalar2=LNEPS,
                                   op0=ALU.mult, op1=ALU.max))
        c.fence("v")
        rv = c.inc_fence("v", v.reciprocal(out=rws[:, :], in_=WS[:, widx:widx + 1]))
        prev_st = [rv, rv]
        prev_wq = [None, None]
        for idx in range(nch * mch):
            i, j = divmod(idx, mch)
            p = idx % 2
            c.wait("q", prev_st[p])
            tk = c.dma("q", f"wq{p}", wt[p][:, :], src[i * 128:(i + 1) * 128, j * 512:(j + 1) * 512])
            c.wait("v", tk, prev_wq[p])
            c.inc("v", v.tensor_scalar(out=vq[:, :], in0=wt[p][:, :], scalar1=rws[:, 0:1],
                                       scalar2=0.5, op0=ALU.mult, op1=ALU.add))
            c.inc("v", v.tensor_copy(out=iq[:, :], in_=vq[:, :]))
            c.inc("v", v.tensor_copy(out=uq[:, :], in_=iq[:, :]))
            c.inc("v", v.tensor_tensor(out=cq[:, :], in0=uq[:, :], in1=vq[:, :], op=ALU.is_gt))
            c.inc("v", v.tensor_tensor(out=uq[:, :], in0=uq[:, :], in1=cq[:, :], op=ALU.subtract))
            wq = c.inc("v", v.tensor_scalar(out=wqt[p][:, :], in0=uq[:, :], scalar1=-1.0,
                                            scalar2=1.0, op0=ALU.max, op1=ALU.min))
            c.wait("q", wq)
            stt = c.dma("q", f"ws{p}", dst[i * 128:(i + 1) * 128, j * 512:(j + 1) * 512], wqt[p][:, :])
            prev_st[p] = wq
            prev_wq[p] = stt
        c.barrier("vqt")


def emit_ln_stats(c, env, src_fn, nblk, MU, RSTD, XB2, SQ2, ps_s, ps_q, T1, T2, invn,
                  src_big=None):
    nc = c.nc
    v, s, t_, g, q = (c.e[k] for k in Ctx.ENGS)
    ONE = env["ONE"]
    if src_big is not None:
        c.inc("s", s.copy(out=XB2[:, 0:nblk * TC], in_=src_big))
        sq = c.inc("s", s.activation(out=SQ2[:, 0:nblk * TC], in_=src_big,
                                     func=AF.Square))
        c.wait("t", sq)
        for i in range(nblk):
            c.inc("t", t_.matmul(ps_s[:, :], lhsT=ONE[:, :],
                                 rhs=XB2[:, i * TC:(i + 1) * TC],
                                 start=(i == 0), stop=(i == nblk - 1)))
            mm = c.inc("t", t_.matmul(ps_q[:, :], lhsT=ONE[:, :],
                                      rhs=SQ2[:, i * TC:(i + 1) * TC],
                                      start=(i == 0), stop=(i == nblk - 1)))
        c.wait("v", mm)
    else:
        prev_mm = [None, None]
        for i in range(nblk):
            p = i % 2
            c.wait("s", prev_mm[p])
            cb = c.inc("s", s.copy(out=XB2[:, p * TC:(p + 1) * TC], in_=src_fn(i)))
            sq = c.inc("s", s.activation(out=SQ2[:, p * TC:(p + 1) * TC], in_=src_fn(i),
                                         func=AF.Square))
            c.wait("t", sq)
            c.inc("t", t_.matmul(ps_s[:, :], lhsT=ONE[:, :],
                                 rhs=XB2[:, p * TC:(p + 1) * TC],
                                 start=(i == 0), stop=(i == nblk - 1)))
            mm = c.inc("t", t_.matmul(ps_q[:, :],
                                      lhsT=ONE[:, :], rhs=SQ2[:, p * TC:(p + 1) * TC],
                                      start=(i == 0), stop=(i == nblk - 1)))
            prev_mm[p] = mm
        c.wait("v", prev_mm[(nblk - 1) % 2])
    c.inc("v", v.tensor_scalar_mul(out=MU[:, :], in0=ps_s[:, :], scalar1=invn))
    m2 = c.inc("v", v.tensor_scalar_mul(out=T1[:, :], in0=ps_q[:, :], scalar1=invn))
    c.wait("s", m2)
    sq2 = c.inc("s", s.activation(out=T2[:, :], in_=MU[:, :], func=AF.Square))
    c.wait("v", sq2)
    vv = c.inc("v", v.tensor_tensor(out=T1[:, :], in0=T1[:, :], in1=T2[:, :], op=ALU.subtract))
    c.wait("s", vv)
    c.inc("s", s.activation(out=T1[:, :], in_=T1[:, :], func=AF.Ln, bias=LNEPS))
    rs = c.inc("s", s.activation(out=RSTD[:, :], in_=T1[:, :], func=AF.Exp, scale=-0.5))
    c.wait("v", rs)
    return rs


def emit_floor_round(c, VQ, IQ, UQ, CQ, out_ap, n):
    v = c.e["v"]
    c.inc("v", v.tensor_copy(out=IQ[:, 0:n], in_=VQ[:, 0:n]))
    c.inc("v", v.tensor_copy(out=UQ[:, 0:n], in_=IQ[:, 0:n]))
    c.inc("v", v.tensor_tensor(out=CQ[:, 0:n], in0=UQ[:, 0:n], in1=VQ[:, 0:n], op=ALU.is_gt))
    return c.inc("v", v.tensor_tensor(out=out_ap, in0=UQ[:, 0:n], in1=CQ[:, 0:n],
                                      op=ALU.subtract))


def emit_bridge(c, env, col_src, ncols, ps_row, ROW, ps_rep, width, prev=None):
    nc = c.nc
    v, s, t_, g, q = (c.e[k] for k in Ctx.ENGS)
    IDF, ONF = env["IDF"], env["ONF"]
    ev = prev
    for i in range(ncols):
        c.wait("t", ev)
        tr = c.inc("t", t_.transpose(ps_row[0:1, :], col_src[:, i:i + 1], IDF[:, :]))
        c.wait("s", tr)
        ev = c.inc("s", s.copy(out=ROW[0:1, i * 128:(i + 1) * 128], in_=ps_row[0:1, :]))
    c.wait("t", ev, prev)
    mm = c.inc("t", t_.matmul(ps_rep[:, 0:width], lhsT=ONF[0:1, 0:128],
                              rhs=ROW[0:1, 0:width], start=True, stop=True))
    return mm


def emit_attn_a(c, env):
    nc = c.nc
    v, s, t_, g, q = (c.e[k] for k in Ctx.ENGS)
    X, CT, ONE, AV, OMV, XPC, PQS, wbf = (env[k] for k in
                                          ("X", "CT", "ONE", "AV", "OMV", "XPC", "PQS", "wbf"))
    tmax = {"k": 0, "v": 1, "r": 2}

    def av(j, db):
        return AV[:, j * 8 + db: j * 8 + db + 1]

    def omv(j, db):
        return OMV[:, j * 8 + db: j * 8 + db + 1]

    WDEC = lambda db: AV[:, 24 + db: 24 + db + 1]
    UFIR = lambda db: AV[:, 32 + db: 32 + db + 1]

    def xdb(db):
        return X[:, db * TC:(db + 1) * TC]

    with ExitStack() as st:
        sbt = lambda nm, shp, dt=F32: st.enter_context(nc.sbuf_tensor(_un(nm), shp, dt))
        XN = sbt("aXN", [128, DB * (TC + 1)])
        XB2 = sbt("aXB2", [128, DB * TC], BF16)
        SQ2 = sbt("aSQ2", [128, DB * TC], BF16)
        MU = sbt("aMU", [128, TC])
        RSTD = sbt("aRSTD", [128, TC])
        T1 = sbt("aT1", [128, TC])
        T2 = sbt("aT2", [128, TC])
        XKb = sbt("aXKb", [128, DB * TC], BF16)
        XVb = sbt("aXVb", [128, DB * TC], BF16)
        XRb = sbt("aXRb", [128, DB * TC], BF16)
        K_ = sbt("aK", [128, DB * TC], BF16)
        V_ = sbt("aV", [128, DB * TC], BF16)
        R_ = sbt("aR", [128, DB * TC], BF16)
        Y_ = sbt("aY", [128, DB * TC], BF16)
        WT = sbt("aWT", [128, DB * 1024], BF16)
        WT2 = sbt("aWT2", [128, DB * 1024], BF16)
        WT3 = sbt("aWT3", [128, DB * 1024], BF16)
        P_ = sbt("aP", [128, TC + 1])
        Q_ = sbt("aQ", [128, TC + 1])
        SA = sbt("aSA", [128, TC])
        SB = sbt("aSB", [128, TC])
        EA = sbt("aEA", [128, TC])
        EB = sbt("aEB", [128, TC])
        AQ = sbt("aAQ", [128, TC])
        BQ = sbt("aBQ", [128, TC])
        MM_ = sbt("aM", [128, TC])
        SO = sbt("aSO", [128, TC])
        SC = sbt("aSC", [128, TC])
        N1 = sbt("aN1", [128, TC])
        N2 = sbt("aN2", [128, TC])
        DEN = sbt("aDEN", [128, TC])
        ps_s = st.enter_context(nc.psum_tensor(_un("aps_s"), [128, TC], F32))
        ps_q = st.enter_context(nc.psum_tensor(_un("aps_q"), [128, TC], F32))
        ps_m = [st.enter_context(nc.psum_tensor(_un(f"aps_m{i}"), [128, TC], F32))
                for i in range(2)]

        wk_tok = c.dma("q", "awt", WT[:, :].rearrange("p (db e) -> p db e", db=DB),
                       wbf["wk"].rearrange("(db p) e -> p db e", p=128))
        wv_tok = c.dma("q", "awt2", WT2[:, :].rearrange("p (db e) -> p db e", db=DB),
                       wbf["wv"].rearrange("(db p) e -> p db e", p=128))
        wr_tok = c.dma("q", "awt3", WT3[:, :].rearrange("p (db e) -> p db e", db=DB),
                       wbf["wr"].rearrange("(db p) e -> p db e", p=128))

        emit_ln_stats(c, env, xdb, DB, MU, RSTD, XB2, SQ2, ps_s, ps_q, T1, T2, 1.0 / D,
                      src_big=X[:, :])
        c.inc("v", v.tensor_copy(
            out=XN[:, :].rearrange("p (db t) -> p db t", t=TC + 1)[:, :, 0:1],
            in_=XPC[:, :].rearrange("p db -> p db ()")))
        XNs = seg(XN[:, :], DB, TC, lead=1)
        c.inc("v", v.tensor_tensor(out=XNs, in0=seg(X[:, :], DB, TC),
                                   in1=fbc(MU[:, :], DB, TC), op=ALU.subtract))
        c.inc("v", v.tensor_tensor(out=XNs, in0=XNs,
                                   in1=fbc(RSTD[:, :], DB, TC), op=ALU.mult))
        c.inc("v", v.tensor_copy(
            out=XPC[:, :].rearrange("p db -> p db ()"),
            in_=XN[:, :].rearrange("p (db t) -> p db t", t=TC + 1)[:, :, TC:TC + 1]))
        for j, XBt in ((0, XKb), (1, XVb), (2, XRb)):
            for db in range(DB):
                base = db * (TC + 1)
                c.inc("v", v.tensor_scalar_mul(out=T1[:, :], in0=XN[:, base:base + TC],
                                               scalar1=omv(j, db)))
                c.inc("v", v.scalar_tensor_tensor(
                    out=XBt[:, db * TC:(db + 1) * TC], in0=XN[:, base + 1:base + TC + 1],
                    scalar=av(j, db), in1=T1[:, :], op0=ALU.mult, op1=ALU.add))
        mix_done = c.tok("v")

        pslast = [None, None]

        c.wait("t", mix_done)

        c.wait("t", wk_tok, wv_tok, wr_tok)
        k_evs, v_evs, r_evs = [], [], []
        cnt3 = [0]
        for eb in range(DB):
            for j, (wtile, srcb, dst, act, evl) in enumerate((
                    (WT, XKb, K_, None, k_evs),
                    (WT2, XVb, V_, None, v_evs),
                    (WT3, XRb, R_, AF.Sigmoid, r_evs))):
                pp = cnt3[0] % 2
                cnt3[0] += 1
                psb = ps_m[pp]
                if pslast[pp] is not None:
                    c.wait("t", pslast[pp])
                for db in range(DB):
                    mm = t_.matmul(psb[:, :],
                                   lhsT=wtile[:, db * 1024 + eb * 128: db * 1024 + (eb + 1) * 128],
                                   rhs=srcb[:, db * TC:(db + 1) * TC],
                                   start=(db == 0), stop=(db == DB - 1))
                mmt = c.inc("t", mm)
                c.wait("s", mmt)
                if act is None:
                    ev = c.inc("s", s.copy(out=dst[:, eb * TC:(eb + 1) * TC], in_=psb[:, :]))
                else:
                    ev = c.inc("s", s.activation(out=dst[:, eb * TC:(eb + 1) * TC],
                                                 in_=psb[:, :], func=act))
                pslast[pp] = ev
                evl.append(ev)
        c.wait("q", c.tok("t"))
        wo_tok = c.dma("q", "awt", WT[:, :].rearrange("p (db e) -> p db e", db=DB),
                       wbf["wo"].rearrange("(db p) e -> p db e", p=128))

        for eb in range(DB):
            c.wait("v", k_evs[eb], v_evs[eb], r_evs[eb])
            kk = K_[:, eb * TC:(eb + 1) * TC]
            vv = V_[:, eb * TC:(eb + 1) * TC]
            rr = R_[:, eb * TC:(eb + 1) * TC]
            c.inc("v", v.tensor_copy(out=P_[:, 0:1], in_=PQS[:, eb:eb + 1]))
            c.inc("v", v.tensor_copy(out=Q_[:, 0:1], in_=PQS[:, DB + eb:DB + eb + 1]))
            c.inc("v", v.tensor_scalar_add(out=T1[:, :], in0=kk, scalar1=UFIR(eb)))
            c.inc("v", v.tensor_tensor_scan(
                out=P_[:, 1:TC + 1], data0=WDEC(eb).broadcast_to([128, TC]), data1=kk,
                initial=P_[:, 0:1], op0=ALU.add, op1=ALU.max))
            c.inc("v", v.scalar_tensor_tensor(out=SA[:, :], in0=P_[:, 0:TC],
                                              scalar=WDEC(eb), in1=P_[:, 1:TC + 1],
                                              op0=ALU.add, op1=ALU.subtract))
            c.inc("v", v.tensor_tensor(out=SB[:, :], in0=kk, in1=P_[:, 1:TC + 1],
                                       op=ALU.subtract))
            sb_done = c.tok("v")
            c.wait("s", sb_done)
            c.inc("s", s.activation(out=EA[:, :], in_=SA[:, :], func=AF.Exp))
            e_done = c.inc("s", s.activation(out=EB[:, :], in_=SB[:, :], func=AF.Exp))
            c.wait("v", e_done)
            c.inc("v", v.tensor_scalar_max(out=AQ[:, :], in0=EA[:, :], scalar1=C30))
            c.inc("v", v.scalar_tensor_tensor(out=BQ[:, :], in0=EB[:, :], scalar=C30,
                                              in1=vv, op0=ALU.max, op1=ALU.mult))
            c.inc("v", v.tensor_tensor_scan(
                out=Q_[:, 1:TC + 1], data0=AQ[:, :], data1=BQ[:, :],
                initial=Q_[:, 0:1], op0=ALU.mult, op1=ALU.add))
            c.inc("v", v.tensor_copy(out=PQS[:, eb:eb + 1], in_=P_[:, TC:TC + 1]))
            c.inc("v", v.tensor_copy(out=PQS[:, DB + eb:DB + eb + 1], in_=Q_[:, TC:TC + 1]))
            c.inc("v", v.tensor_tensor(out=MM_[:, :], in0=P_[:, 0:TC], in1=T1[:, :],
                                       op=ALU.max))
            c.inc("v", v.tensor_tensor(out=SA[:, :], in0=P_[:, 0:TC], in1=MM_[:, :],
                                       op=ALU.subtract))
            c.inc("v", v.tensor_tensor(out=SB[:, :], in0=T1[:, :], in1=MM_[:, :],
                                       op=ALU.subtract))
            so_in = c.tok("v")
            c.wait("s", so_in)
            c.inc("s", s.activation(out=SO[:, :], in_=SA[:, :], func=AF.Exp))
            sc_done = c.inc("s", s.activation(out=SC[:, :], in_=SB[:, :], func=AF.Exp))
            c.wait("v", sc_done)
            c.inc("v", v.tensor_scalar_max(out=SO[:, :], in0=SO[:, :], scalar1=C30))
            c.inc("v", v.tensor_scalar_max(out=SC[:, :], in0=SC[:, :], scalar1=C30))
            c.inc("v", v.tensor_tensor(out=N1[:, :], in0=SO[:, :], in1=Q_[:, 0:TC],
                                       op=ALU.mult))
            c.inc("v", v.tensor_tensor(out=N2[:, :], in0=SC[:, :], in1=vv, op=ALU.mult))
            c.inc("v", v.tensor_tensor(out=N1[:, :], in0=N1[:, :], in1=N2[:, :], op=ALU.add))
            dn = c.inc("v", v.scalar_tensor_tensor(out=DEN[:, :], in0=SO[:, :], scalar=WKVEPS,
                                                   in1=SC[:, :], op0=ALU.add, op1=ALU.add))
            c.wait("s", dn)
            c.inc("s", s.activation(out=DEN[:, :], in_=DEN[:, :], func=AF.Ln))
            rd = c.inc("s", s.activation(out=DEN[:, :], in_=DEN[:, :], func=AF.Exp, scale=-1.0))
            c.wait("v", rd)
            c.inc("v", v.tensor_tensor(out=N1[:, :], in0=N1[:, :], in1=DEN[:, :], op=ALU.mult))
            c.inc("v", v.tensor_tensor(out=Y_[:, eb * TC:(eb + 1) * TC], in0=rr,
                                       in1=N1[:, :], op=ALU.mult))
        y_done = c.tok("v")

        c.wait("t", wo_tok, y_done)
        for dbo in range(DB):
            psb = ps_m[dbo % 2]
            if pslast[dbo % 2] is not None:
                c.wait("t", pslast[dbo % 2])
            for ebi in range(DB):
                mm = t_.matmul(psb[:, :],
                               lhsT=WT[:, ebi * 1024 + dbo * 128: ebi * 1024 + (dbo + 1) * 128],
                               rhs=Y_[:, ebi * TC:(ebi + 1) * TC],
                               start=(ebi == 0), stop=(ebi == DB - 1))
            mmt = c.inc("t", mm)
            c.wait("v", mmt)
            pslast[dbo % 2] = c.inc("v", v.tensor_tensor(out=xdb(dbo), in0=xdb(dbo),
                                                         in1=psb[:, :], op=ALU.add))


def emit_ffn(c, env):
    nc = c.nc
    v, s, t_, g, q = (c.e[k] for k in Ctx.ENGS)
    X, CT, ONE, IDF, ONF, FV, WS = (env[k] for k in
                                    ("X", "CT", "ONE", "IDF", "ONF", "FV", "WS"))
    w1q, w2q = env["w1q"], env["w2q"]

    def xdb(db):
        return X[:, db * TC:(db + 1) * TC]

    with ExitStack() as st:
        sbt = lambda nm, shp, dt=F32: st.enter_context(nc.sbuf_tensor(_un(nm), shp, dt))
        XB2 = sbt("fXB2", [128, DB * TC], BF16)
        SQ2 = sbt("fSQ2", [128, DB * TC], BF16)
        MU = sbt("fMU", [128, TC])
        RSTD = sbt("fRSTD", [128, TC])
        T1 = sbt("fT1", [128, TC])
        T2 = sbt("fT2", [128, TC])
        T0 = sbt("fT0", [128, DB * TC])
        QREP = sbt("fQREP", [128, TC])
        AREP = sbt("fAREP", [128, TC])
        MXC = sbt("fMXC", [128, 4])
        ROW = sbt("fROW", [1, TC])
        XQ = sbt("fXQ", [128, DB * FFT], BF16)
        HQ = sbt("fHQ", [128, HB * FFT], BF16)
        HG = sbt("fHG", [128, HB * FFT])
        UQ = sbt("fUQ", [128, 8 * FFT])
        IQ = sbt("fIQ", [128, 8 * FFT], I32)
        CQ = sbt("fCQ", [128, 8 * FFT])
        VQ = sbt("fVQ", [128, 8 * FFT])
        WP = [sbt("fWPa", [128, DB * 512], BF16), sbt("fWPb", [128, DB * 512], BF16)]
        ps_s = st.enter_context(nc.psum_tensor(_un("fps_s"), [128, TC], F32))
        ps_q = st.enter_context(nc.psum_tensor(_un("fps_q"), [128, TC], F32))
        ps_m = [st.enter_context(nc.psum_tensor(_un(f"fps_m{i}"), [128, FFT], F32))
                for i in range(2)]
        ps_t = st.enter_context(nc.psum_tensor(_un("fps_t"), [128, 512], F32))
        ps_row = st.enter_context(nc.psum_tensor(_un("fps_row"), [1, 128], F32))
        ps_b = st.enter_context(nc.psum_tensor(_un("fps_b"), [128, TC], F32))

        emit_ln_stats(c, env, xdb, DB, MU, RSTD, XB2, SQ2, ps_s, ps_q, T1, T2, 1.0 / D,
                      src_big=X[:, :])
        c.inc("v", v.tensor_tensor(out=seg(T0[:, :], DB, TC), in0=seg(X[:, :], DB, TC),
                                   in1=fbc(MU[:, :], DB, TC), op=ALU.subtract))
        t0_done = c.tok("v")
        c.wait("t", t0_done)
        red_tok = None
        for sl in range(4):
            for grp in range(2):
                trs = None
                for k in range(4):
                    db = grp * 4 + k
                    if red_tok is not None:
                        c.wait("t", red_tok)
                    trs = c.inc("t", t_.transpose(
                        ps_t[:, k * 128:(k + 1) * 128],
                        T0[:, db * TC + sl * 128: db * TC + (sl + 1) * 128], IDF[:, :]))
                c.wait("v", trs, t0_done)
                if grp == 0:
                    c.fence("v")
                    red_tok = c.inc_fence("v", v.tensor_reduce(out=MXC[:, sl:sl + 1],
                                                                in_=ps_t[:, :], axis=AXX,
                                                                op=ALU.max,
                                                                apply_absolute_value=True))
                else:
                    c.fence("v")
                    c.inc_fence("v", v.tensor_reduce(out=T1[:, 0:1], in_=ps_t[:, :], axis=AXX,
                                                     op=ALU.max, apply_absolute_value=True))
                    red_tok = c.inc("v", v.tensor_tensor(out=MXC[:, sl:sl + 1],
                                                         in0=MXC[:, sl:sl + 1],
                                                         in1=T1[:, 0:1], op=ALU.max))
        bq = emit_bridge(c, env, MXC, 4, ps_row, ROW, ps_b, TC, prev=c.tok("v"))
        c.wait("v", bq)
        c.inc("v", v.tensor_tensor(out=T1[:, :], in0=RSTD[:, :], in1=ps_b[:, :], op=ALU.mult))
        c.inc("v", v.tensor_scalar_max(out=T1[:, :], in0=T1[:, :], scalar1=LNEPS))
        c.inc("v", v.reciprocal(out=T2[:, :], in_=T1[:, :]))
        c.inc("v", v.tensor_scalar_mul(out=T2[:, :], in0=T2[:, :], scalar1=127.0))
        c.inc("v", v.tensor_tensor(out=QREP[:, :], in0=T2[:, :], in1=RSTD[:, :], op=ALU.mult))
        c.inc("v", v.tensor_scalar(out=AREP[:, :], in0=T1[:, :], scalar1=WS[:, 0:1],
                                   scalar2=1.0 / 127.0, op0=ALU.mult, op1=ALU.mult))
        qa_done = c.tok("v")
        pslast = [None, None]
        wp_guard = [None, None]

        for hf in range(TC // FFT):
            toff = hf * FFT
            W8 = 8 * FFT
            t0v = T0[:, :].rearrange("p (b t) -> p b t", b=DB)[:, :, toff:toff + FFT]
            c.inc("v", v.tensor_tensor(out=seg(VQ[:, :], DB, FFT), in0=t0v,
                                       in1=fbc(QREP[:, toff:toff + FFT], DB, FFT),
                                       op=ALU.mult))
            c.inc("v", v.tensor_scalar_add(out=VQ[:, 0:W8], in0=VQ[:, 0:W8], scalar1=0.5))
            emit_floor_round(c, VQ, IQ, UQ, CQ, XQ[:, 0:W8], W8)
            xq_done = c.tok("v")
            def wp_dma1(chunk, p):
                c.wait("q", wp_guard[p])
                wsrc = w1q.rearrange("(db p) e -> p db e", p=128)[
                    :, :, chunk * 512:(chunk + 1) * 512]
                wdst = WP[p][:, :].rearrange("p (db e) -> p db e", db=DB)
                return (c.dma("q", f"wp{p}a", wdst[:, 0:4], wsrc[:, 0:4]),
                        c.dma("q", f"wp{p}b", wdst[:, 4:8], wsrc[:, 4:8]))

            tokw = {0: wp_dma1(0, 0), 1: wp_dma1(1, 1)}
            for obg in range(HB // 4):
                p = obg % 2
                c.wait("t", tokw[p][0], tokw[p][1], xq_done)
                for oi in range(4):
                    ob = obg * 4 + oi
                    psb = ps_m[ob % 2]
                    if pslast[ob % 2] is not None:
                        c.wait("t", pslast[ob % 2])
                    for db in range(DB):
                        mm = t_.matmul(psb[:, :],
                                       lhsT=WP[p][:, db * 512 + oi * 128: db * 512 + (oi + 1) * 128],
                                       rhs=XQ[:, db * FFT:(db + 1) * FFT],
                                       start=(db == 0), stop=(db == DB - 1))
                    mmt = c.inc("t", mm)
                    c.wait("v", mmt, qa_done)
                    mt = c.inc("v", v.tensor_tensor(out=HG[:, ob * FFT:(ob + 1) * FFT],
                                                    in0=psb[:, :],
                                                    in1=AREP[:, toff:toff + FFT],
                                                    op=ALU.mult))
                    c.wait("s", mt)
                    gg = c.inc("s", s.activation(out=HG[:, ob * FFT:(ob + 1) * FFT],
                                                 in_=HG[:, ob * FFT:(ob + 1) * FFT],
                                                 func=AF.Gelu, bias=FV[:, ob:ob + 1],
                                                 scale=1.0))
                    pslast[ob % 2] = gg
                wp_guard[p] = mmt
                if obg + 2 < HB // 4:
                    tokw[p] = wp_dma1(obg + 2, p)
            h_done = pslast[(HB - 1) % 2]
            def wp_dma2(chunk, p):
                c.wait("q", wp_guard[p])
                wsrc = w2q.rearrange("(hb p) e -> p hb e", p=128)[
                    :, :, chunk * 128:(chunk + 1) * 128]
                wdst = WP[p][:, :].rearrange("p (hb e) -> p hb e", hb=HB)
                return (c.dma("q", f"wp{p}a", wdst[:, 0:16], wsrc[:, 0:16]),
                        c.dma("q", f"wp{p}b", wdst[:, 16:32], wsrc[:, 16:32]))

            tokw = {0: wp_dma2(0, 0), 1: wp_dma2(1, 1)}
            c.wait("t", h_done, c.tok("v"))
            red_tok = None
            for sl in range(FFT // 128):
                first = True
                for grp in range(8):
                    trs = None
                    for k in range(4):
                        hb_ = grp * 4 + k
                        if red_tok is not None:
                            c.wait("t", red_tok)
                        trs = c.inc("t", t_.transpose(
                            ps_t[:, k * 128:(k + 1) * 128],
                            HG[:, hb_ * FFT + sl * 128: hb_ * FFT + (sl + 1) * 128],
                            IDF[:, :]))
                    c.wait("v", trs, h_done)
                    if first:
                        c.fence("v")
                        red_tok = c.inc_fence("v", v.tensor_reduce(out=MXC[:, sl:sl + 1],
                                                                    in_=ps_t[:, :], axis=AXX,
                                                                    op=ALU.max,
                                                                    apply_absolute_value=True))
                        first = False
                    else:
                        c.fence("v")
                        c.inc_fence("v", v.tensor_reduce(out=T1[:, 0:1], in_=ps_t[:, :],
                                                         axis=AXX, op=ALU.max,
                                                         apply_absolute_value=True))
                        red_tok = c.inc("v", v.tensor_tensor(out=MXC[:, sl:sl + 1],
                                                             in0=MXC[:, sl:sl + 1],
                                                             in1=T1[:, 0:1], op=ALU.max))
            bq = emit_bridge(c, env, MXC, FFT // 128, ps_row, ROW, ps_b, FFT,
                             prev=c.tok("v"))
            c.wait("v", bq)
            c.inc("v", v.tensor_scalar_max(out=T1[:, 0:FFT], in0=ps_b[:, 0:FFT],
                                           scalar1=LNEPS))
            c.fence("v")
            c.inc_fence("v", v.reciprocal(out=T2[:, 0:FFT], in_=T1[:, 0:FFT]))
            c.inc("v", v.tensor_scalar_mul(out=QREP[:, toff:toff + FFT], in0=T2[:, 0:FFT],
                                           scalar1=127.0))
            c.inc("v", v.tensor_scalar(out=AREP[:, toff:toff + FFT], in0=T1[:, 0:FFT],
                                       scalar1=WS[:, 1:2], scalar2=1.0 / 127.0,
                                       op0=ALU.mult, op1=ALU.mult))
            for hg in range(HB // 8):
                hoff = hg * 8 * FFT
                c.inc("v", v.tensor_tensor(
                    out=seg(VQ[:, :], 8, FFT),
                    in0=HG[:, :].rearrange("p (b t) -> p b t", b=HB)[:, hg * 8:(hg + 1) * 8, :],
                    in1=fbc(QREP[:, toff:toff + FFT], 8, FFT), op=ALU.mult))
                c.inc("v", v.tensor_scalar_add(out=VQ[:, 0:W8], in0=VQ[:, 0:W8],
                                               scalar1=0.5))
                emit_floor_round(c, VQ, IQ, UQ, CQ, HQ[:, hoff:hoff + W8], W8)
            hq_done = c.tok("v")
            for dbo in range(DB):
                p = dbo % 2
                c.wait("t", tokw[p][0], tokw[p][1], hq_done)
                psb = ps_m[dbo % 2]
                if pslast[dbo % 2] is not None:
                    c.wait("t", pslast[dbo % 2])
                for hb_ in range(HB):
                    mm = t_.matmul(psb[:, :], lhsT=WP[p][:, hb_ * 128:(hb_ + 1) * 128],
                                   rhs=HQ[:, hb_ * FFT:(hb_ + 1) * FFT],
                                   start=(hb_ == 0), stop=(hb_ == HB - 1))
                mmt = c.inc("t", mm)
                wp_guard[p] = mmt
                if dbo + 2 < DB:
                    tokw[p] = wp_dma2(dbo + 2, p)
                c.wait("v", mmt)
                c.inc("v", v.tensor_tensor(out=VQ[:, 0:FFT], in0=psb[:, :],
                                           in1=AREP[:, toff:toff + FFT], op=ALU.mult))
                pslast[dbo % 2] = c.inc("v", v.scalar_tensor_tensor(
                    out=X[:, dbo * TC + toff: dbo * TC + toff + FFT],
                    in0=VQ[:, 0:FFT], scalar=FV[:, HB + dbo:HB + dbo + 1],
                    in1=X[:, dbo * TC + toff: dbo * TC + toff + FFT],
                    op0=ALU.add, op1=ALU.add))


def emit_mamba(c, env):
    nc = c.nc
    v, s, t_, g, q = (c.e[k] for k in Ctx.ENGS)
    X, CT, ONE, MV, ATL, HSTT, CCAR = (env[k] for k in
                                       ("X", "CT", "ONE", "MV", "ATL", "HSTT", "CCAR"))
    SEL = env["SEL"]
    IDE = env["IDE"]
    inwb, outwb, xpwb, dtwb = (env[k] for k in ("inwb", "outwb", "xpwb", "dtwb"))

    def xdb(db):
        return X[:, db * TC:(db + 1) * TC]

    def cw(l, dib, tap):
        return MV[l][:, dib * 4 + tap: dib * 4 + tap + 1]

    def cb(l, dib):
        return MV[l][:, 64 + dib:64 + dib + 1]

    def dtb(l, dob):
        return MV[l][:, 80 + dob:80 + dob + 1]

    def dsk(l, dib):
        return MV[l][:, 96 + dib:96 + dib + 1]

    def lnw(l, db):
        return MV[l][:, 112 + db:112 + db + 1]

    def lnb(l, db):
        return MV[l][:, 120 + db:120 + db + 1]

    def acol(l, dib, n):
        return ATL[l][:, dib * NST + n: dib * NST + n + 1]

    with ExitStack() as st:
        sbt = lambda nm, shp, dt=F32: st.enter_context(nc.sbuf_tensor(_un(nm), shp, dt))
        XX = sbt("mXX", [128, DB * TC])
        MU = sbt("mMU", [128, TC])
        RSTD = sbt("mRSTD", [128, TC])
        T1 = sbt("mT1", [128, TC])
        T2 = sbt("mT2", [128, TC])
        SZ = sbt("mSZ", [128, DIB * TC], BF16)
        YA = sbt("mYA", [128, DIB * TC])
        DTR = sbt("mDTR", [64, TC], BF16)
        BF = sbt("mBF", [16, TC], BF16)
        CF = sbt("mCF", [16, TC], BF16)
        ps_m = [st.enter_context(nc.psum_tensor(_un(f"mps_m{i}"), [128, TC], F32))
                for i in range(2)]

        with ExitStack() as st2:
            XB2 = st2.enter_context(nc.sbuf_tensor(_un("mXB2"), [128, DB * TC], BF16))
            SQ2 = st2.enter_context(nc.sbuf_tensor(_un("mSQ2"), [128, DB * TC], BF16))
            ps_s = st2.enter_context(nc.psum_tensor(_un("mps_s"), [128, TC], F32))
            ps_q = st2.enter_context(nc.psum_tensor(_un("mps_q"), [128, TC], F32))
            emit_ln_stats(c, env, xdb, DB, MU, RSTD, XB2, SQ2, ps_s, ps_q, T1, T2,
                          1.0 / D, src_big=X[:, :])
            XXs = seg(XX[:, :], DB, TC)
            c.inc("v", v.tensor_tensor(out=XXs, in0=seg(X[:, :], DB, TC),
                                       in1=fbc(MU[:, :], DB, TC), op=ALU.subtract))
            c.inc("v", v.tensor_tensor(out=XXs, in0=XXs,
                                       in1=fbc(RSTD[:, :], DB, TC), op=ALU.mult))
            c.barrier()

        pslast = [None, None]
        for l in range(2):
          with ExitStack() as stL:
            XXb = stL.enter_context(nc.sbuf_tensor(_un("mXXb"), [128, DB * TC], BF16))
            WINZ = stL.enter_context(nc.sbuf_tensor(_un("mWINZ"), [128, DB * 512], BF16))
            UB = stL.enter_context(nc.sbuf_tensor(_un("mUB"), [128, DIB * TC], BF16))
            HTA = stL.enter_context(nc.sbuf_tensor(_un("mHTA"), [128, 8 * TC], BF16))
            WXP = stL.enter_context(nc.sbuf_tensor(_un("mWXP"), [128, DIB * 128], BF16))
            WDT = stL.enter_context(nc.sbuf_tensor(_un("mWDT"), [64, DI], BF16))
            wxp_tok = c.dma("q", "wxp", WXP[:, :].rearrange("p (dib e) -> p dib e", dib=DIB),
                            xpwb[l].rearrange("(dib p) e -> p dib e", p=128))
            wdt_tok = c.dma("q", "wdt", WDT[:, :], dtwb[l][:, :])
            with ExitStack() as st2:
                WIN = st2.enter_context(nc.sbuf_tensor(_un("mWIN"), [128, DB * 512], BF16))
                GT1 = st2.enter_context(nc.sbuf_tensor(_un("mGT1"), [128, TC], BF16))
                XC = st2.enter_context(nc.sbuf_tensor(_un("mXCc"), [128, DIB * (TC + 3)], BF16))
                cv_tok = {}
                WI2 = [WIN, WINZ]

                def wi_dma(chunk, p):
                    c.wait("q", wi_guard[p])
                    wsrc = inwb[l].rearrange("(db p) e -> p db e", p=128)[
                        :, :, chunk * 512:(chunk + 1) * 512]
                    wdst = WI2[p][:, :].rearrange("p (db e) -> p db e", db=DB)
                    return (c.dma("q", f"wi{p}a", wdst[:, 0:4], wsrc[:, 0:4]),
                            c.dma("q", f"wi{p}b", wdst[:, 4:8], wsrc[:, 4:8]))

                wi_guard = [None, None]
                tokw = {0: wi_dma(0, 0), 1: wi_dma(1, 1)}
                xxc = None
                for db in range(DB):
                    xxc = c.inc("s", s.copy(out=XXb[:, db * TC:(db + 1) * TC],
                                            in_=XX[:, db * TC:(db + 1) * TC]))
                for half in (0,):
                    for og in range(DI // 512):
                        p = og % 2
                        c.wait("t", tokw[p][0], tokw[p][1], xxc)
                        for oi in range(4):
                            dib = og * 4 + oi
                            psb = ps_m[dib % 2]
                            if pslast[dib % 2] is not None:
                                c.wait("t", pslast[dib % 2])
                            for db in range(DB):
                                mm = t_.matmul(
                                    psb[:, :],
                                    lhsT=WI2[p][:, db * 512 + oi * 128: db * 512 + (oi + 1) * 128],
                                    rhs=XXb[:, db * TC:(db + 1) * TC],
                                    start=(db == 0), stop=(db == DB - 1))
                            mmt = c.inc("t", mm)
                            wi_guard[p] = mmt
                            if oi == 3 and og + 2 < DI // 512:
                                tokw[p] = wi_dma(og + 2, p)
                            c.wait("s", mmt)
                            if half == 0:
                                base = dib * (TC + 3)
                                c.inc("s", s.copy(out=XC[:, base:base + 3],
                                                  in_=CCAR[l][:, dib * 3:(dib + 1) * 3]))
                                c.inc("s", s.copy(out=XC[:, base + 3:base + 3 + TC],
                                                  in_=psb[:, :]))
                                ev = c.inc("s", s.copy(out=CCAR[l][:, dib * 3:(dib + 1) * 3],
                                                       in_=psb[:, TC - 3:TC]))
                                ub_s = UB[:, dib * TC:(dib + 1) * TC]
                                c.wait("v", ev)
                                c.inc("v", v.tensor_scalar_mul(
                                    out=ub_s, in0=XC[:, base:base + TC],
                                    scalar1=cw(l, dib, 0)))
                                for tap in range(1, 4):
                                    c.inc("v", v.scalar_tensor_tensor(
                                        out=ub_s,
                                        in0=XC[:, base + tap:base + tap + TC],
                                        scalar=cw(l, dib, tap), in1=ub_s,
                                        op0=ALU.mult, op1=ALU.add))
                                cv_tok[dib] = c.tok("v")
                            else:
                                ev = c.inc("s", s.activation(
                                    out=SZ[:, dib * TC:(dib + 1) * TC],
                                    in_=psb[:, :], func=AF.Silu))
                            pslast[dib % 2] = ev
                        del ev
                for dib in range(DIB):
                    c.wait("s", cv_tok[dib])
                    c.inc("s", s.activation(out=UB[:, dib * TC:(dib + 1) * TC],
                                            in_=UB[:, dib * TC:(dib + 1) * TC],
                                            func=AF.Silu, bias=cb(l, dib), scale=1.0))
                u_done = c.tok("s")
                z_winz_guard = wi_guard[1]
            with ExitStack() as st2:
                ps_xd = st2.enter_context(nc.psum_tensor(_un("mps_xd"), [64, TC], F32))
                ps_xb = st2.enter_context(nc.psum_tensor(_un("mps_xb"), [16, TC], F32))
                ps_xc = st2.enter_context(nc.psum_tensor(_un("mps_xc"), [16, TC], F32))
                c.wait("t", wxp_tok, u_done)
                for dib in range(DIB):
                    st_f = (dib == 0)
                    sp_f = (dib == DIB - 1)
                    t_.matmul(ps_xd[:, :], lhsT=WXP[:, dib * 128: dib * 128 + 64],
                              rhs=UB[:, dib * TC:(dib + 1) * TC], start=st_f, stop=sp_f)
                    t_.matmul(ps_xb[:, :], lhsT=WXP[:, dib * 128 + 64: dib * 128 + 80],
                              rhs=UB[:, dib * TC:(dib + 1) * TC], start=st_f, stop=sp_f)
                    mm = t_.matmul(ps_xc[:, :], lhsT=WXP[:, dib * 128 + 80: dib * 128 + 96],
                                   rhs=UB[:, dib * TC:(dib + 1) * TC], start=st_f, stop=sp_f)
                xp_done = c.inc("t", mm)
                c.wait("s", xp_done)
                c.inc("s", s.copy(out=DTR[:, :], in_=ps_xd[:, :]))
                c.inc("s", s.copy(out=BF[:, :], in_=ps_xb[:, :]))
                ev = c.inc("s", s.copy(out=CF[:, :], in_=ps_xc[:, :]))
                c.wait("t", ev)
            stDT = ExitStack()
            DT = stDT.enter_context(nc.sbuf_tensor(_un("mDT"), [128, DIB * TC], BF16))
            WTl = stDT.enter_context(nc.sbuf_tensor(_un("mWT"), [128, DIB * TC], BF16))
            AEX = stDT.enter_context(nc.sbuf_tensor(_un("mAEX"), [128, 4 * TC], F32))
            c.wait("t", wdt_tok)
            for dob in range(DIB):
                psb = ps_m[dob % 2]
                if pslast[dob % 2] is not None:
                    c.wait("t", pslast[dob % 2])
                mm = t_.matmul(psb[:, :], lhsT=WDT[:, dob * 128:(dob + 1) * 128],
                               rhs=DTR[:, :], start=True, stop=True)
                mmt = c.inc("t", mm)
                c.wait("s", mmt)
                c.inc("s", s.activation(out=T1[:, :], in_=psb[:, :], func=AF.Identity,
                                        bias=dtb(l, dob), scale=1.0))
                c.inc("s", s.activation(out=T2[:, :], in_=T1[:, :], func=AF.Abs))
                c.inc("s", s.activation(out=T2[:, :], in_=T2[:, :], func=AF.Exp, scale=-1.0))
                c.inc("s", s.activation(out=T2[:, :], in_=T2[:, :], func=AF.Ln, bias=1.0))
                rl = c.inc("s", s.activation(out=T1[:, :], in_=T1[:, :], func=AF.Relu))
                c.wait("v", rl)
                dtv = c.inc("v", v.tensor_tensor(out=DT[:, dob * TC:(dob + 1) * TC],
                                                 in0=T1[:, :], in1=T2[:, :], op=ALU.add))
                pslast[dob % 2] = dtv
                c.wait("s", dtv)
            c.inc("v", v.tensor_tensor(out=seg(WTl[:, :], DIB, TC),
                                       in0=seg(DT[:, :], DIB, TC),
                                       in1=seg(UB[:, :], DIB, TC), op=ALU.mult))
            c.inc("v", v.tensor_tensor(out=seg(YA[:, :], DIB, TC),
                                       in0=seg(UB[:, :], DIB, TC),
                                       in1=sbc(MV[l][:, 96:112], DIB, TC), op=ALU.mult))
            prep_done = c.tok("v")
            with ExitStack() as st2:
                ps_y = st2.enter_context(nc.psum_tensor(_un("mps_y"), [128, 4 * TC], F32))
                ps_bc2 = st2.enter_context(nc.psum_tensor(_un("mps_bc2"), [128, 2 * TC], F32))
                BCREP = st2.enter_context(nc.sbuf_tensor(_un("mBCREP"), [128, 6 * TC], BF16))
                WB = UB
                aex_used = [None, None, None, None]
                z_done = [None]

                def z_group(og):
                    c.wait("q", z_winz_guard)
                    wtok = c.dma("q", "winz",
                                 WINZ[:, :].rearrange("p (db e) -> p db e", db=DB),
                                 inwb[l].rearrange("(db p) e -> p db e", p=128)
                                 [:, :, DI + og * 512: DI + (og + 1) * 512])
                    c.wait("t", wtok)
                    for oi in range(4):
                        dib = og * 4 + oi
                        psb = ps_m[dib % 2]
                        if pslast[dib % 2] is not None:
                            c.wait("t", pslast[dib % 2])
                        for db in range(DB):
                            mm = t_.matmul(
                                psb[:, :],
                                lhsT=WINZ[:, db * 512 + oi * 128: db * 512 + (oi + 1) * 128],
                                rhs=XXb[:, db * TC:(db + 1) * TC],
                                start=(db == 0), stop=(db == DB - 1))
                        mmt = c.inc("t", mm)
                        c.wait("s", mmt)
                        pslast[dib % 2] = c.inc("s", s.activation(
                            out=SZ[:, dib * TC:(dib + 1) * TC],
                            in_=psb[:, :], func=AF.Silu))
                        z_done[0] = pslast[dib % 2]
                    c.wait("q", c.tok("t"))

                Q4 = 4 * TC
                reps_tok = [None, None, None]
                bc_guard = [None, None, None]
                prev_mq = [None, None]
                prev_yacc = [None, None]
                evac_last = [None]
                wbg = {}
                tts_last = [None, None]
                sv_g = [None]

                def issue_wb(kk2):
                    if kk2 >= 4 * NST:
                        return
                    pq2 = kk2 % 2
                    sl2 = kk2 % 3
                    goff2 = (kk2 // NST) * 4
                    wbq2 = WB[:, (2 * pq2) * Q4:(2 * pq2) * Q4 + Q4]
                    Bq2 = BCREP[:, sl2 * 2 * TC: sl2 * 2 * TC + TC]
                    c.wait("g", reps_tok[sl2], tts_last[pq2])
                    wbg[kk2] = c.inc("g", g.tensor_tensor(
                        out=seg(wbq2, 4, TC),
                        in0=WTl[:, :].rearrange("p (b w) -> p b w", b=DIB)
                        [:, goff2:goff2 + 4, :],
                        in1=fbc(Bq2, 4, TC), op=ALU.mult))

                def issue_bc(k):
                    col, sl = k % NST, k % 3
                    c.wait("t", evac_last[0])
                    t_.matmul(ps_bc2[:, 0:TC], lhsT=SEL[:, col * 128:(col + 1) * 128],
                              rhs=BF[:, :], start=True, stop=True)
                    mmt = c.inc("t", t_.matmul(
                        ps_bc2[:, TC:2 * TC], lhsT=SEL[:, col * 128:(col + 1) * 128],
                        rhs=CF[:, :], start=True, stop=True))
                    c.wait("s", mmt, bc_guard[sl])
                    ev = c.inc("s", s.copy(out=BCREP[:, sl * 2 * TC:(sl + 1) * 2 * TC],
                                           in_=ps_bc2[:, :]))
                    reps_tok[sl] = ev
                    evac_last[0] = ev

                for g4 in range(4):
                    goff = g4 * 4
                    for n in range(NST):
                        pq = n % 2
                        kk = g4 * NST + n
                        sl = kk % 3
                        if kk == 0:
                            issue_bc(0)
                            issue_bc(1)
                            c.wait("g", prep_done)
                            issue_wb(0)
                        issue_wb(kk + 1)
                        Cq = BCREP[:, sl * 2 * TC + TC: sl * 2 * TC + 2 * TC]
                        wbq = WB[:, (2 * pq) * Q4:(2 * pq) * Q4 + Q4]
                        mq = WB[:, (2 * pq + 1) * Q4:(2 * pq + 1) * Q4 + Q4]
                        for di in range(4):
                            dib = goff + di
                            c.wait("s", aex_used[di])
                            atok = c.inc("s", s.activation(
                                out=AEX[:, di * TC:(di + 1) * TC],
                                in_=DT[:, dib * TC:(dib + 1) * TC],
                                func=AF.Exp, scale=acol(l, dib, n)))
                            c.wait("v", atok, wbg.get(kk), sv_g[0])
                            c.inc("v", v.tensor_tensor_scan(
                                out=HTA[:, di * TC:(di + 1) * TC],
                                data0=AEX[:, di * TC:(di + 1) * TC],
                                data1=wbq[:, di * TC:(di + 1) * TC],
                                initial=HSTT[l][:, dib * NST + n: dib * NST + n + 1],
                                op0=ALU.mult, op1=ALU.add))
                            aex_used[di] = c.tok("v")
                        tts_last[pq] = c.tok("v")
                        if kk + 2 < 4 * NST:
                            issue_bc(kk + 2)
                        c.wait("g", tts_last[pq])
                        sv_g[0] = c.inc("g", g.tensor_copy(
                            out=HSTT[l][:, :].rearrange("p (d s) -> p d s", s=NST)
                            [:, goff:goff + 4, n:n + 1],
                            in_=seg(HTA[:, 0:Q4], 4, TC)[:, :, TC - 1:TC]))
                        c.wait("v", prev_mq[pq], prev_yacc[pq])
                        mvq = c.inc("v", v.tensor_tensor(
                            out=seg(mq, 4, TC), in0=seg(HTA[:, 0:Q4], 4, TC),
                            in1=fbc(Cq, 4, TC), op=ALU.mult))
                        prev_mq[pq] = mvq
                        bc_guard[sl] = mvq
                        c.wait("t", mvq)
                        for di in range(4):
                            mm = t_.matmul(ps_y[:, di * TC:(di + 1) * TC],
                                           lhsT=IDE[:, :],
                                           rhs=mq[:, di * TC:(di + 1) * TC],
                                           start=(n == 0), stop=(n == NST - 1))
                        yacc = c.inc("t", mm)
                        prev_yacc[pq] = yacc
                        if n == NST - 1:
                            c.wait("v", yacc)
                            c.inc("v", v.tensor_tensor(
                                out=YA[:, goff * TC:goff * TC + Q4],
                                in0=YA[:, goff * TC:goff * TC + Q4],
                                in1=ps_y[:, :], op=ALU.add))
                            prev_mq = [c.tok("v"), c.tok("v")]
                        if n == 8:
                            z_group(g4)
                    c.wait("t", c.tok("v"))
            scan_done = c.tok("v")
            stDT.close()
            with ExitStack() as st2:
                YB = st2.enter_context(nc.sbuf_tensor(_un("mYB"), [128, DIB * TC], BF16))
                HH = st2.enter_context(nc.sbuf_tensor(_un("mHH"), [128, DB * TC], F32))
                WO2 = [st2.enter_context(nc.sbuf_tensor(_un(f"mWOU{i}"), [128, DIB * 128], BF16))
                       for i in range(2)]

                def wo_dma(chunk, p):
                    c.wait("q", wo_guard[p])
                    wsrc = outwb[l].rearrange("(dib p) e -> p dib e", p=128)[
                        :, :, chunk * 128:(chunk + 1) * 128]
                    wdst = WO2[p][:, :].rearrange("p (dib e) -> p dib e", dib=DIB)
                    return (c.dma("q", f"wo{p}a", wdst[:, 0:8], wsrc[:, 0:8]),
                            c.dma("q", f"wo{p}b", wdst[:, 8:16], wsrc[:, 8:16]))

                wo_guard = [scan_done, scan_done]
                tokw = {0: wo_dma(0, 0), 1: wo_dma(1, 1)}
                c.wait("v", z_done[0])
                c.inc("v", v.tensor_tensor(out=seg(YB[:, :], DIB, TC),
                                           in0=seg(YA[:, :], DIB, TC),
                                           in1=seg(SZ[:, :], DIB, TC), op=ALU.mult))
                yb_done = c.tok("v")
                evs = []
                for dbo in range(DB):
                    p = dbo % 2
                    c.wait("t", tokw[p][0], tokw[p][1], yb_done)
                    psb = ps_m[dbo % 2]
                    if pslast[dbo % 2] is not None:
                        c.wait("t", pslast[dbo % 2])
                    for dib in range(DIB):
                        mm = t_.matmul(psb[:, :], lhsT=WO2[p][:, dib * 128:(dib + 1) * 128],
                                       rhs=YB[:, dib * TC:(dib + 1) * TC],
                                       start=(dib == 0), stop=(dib == DIB - 1))
                    mmt = c.inc("t", mm)
                    wo_guard[p] = mmt
                    if dbo + 2 < DB:
                        tokw[p] = wo_dma(dbo + 2, p)
                    c.wait("s", mmt)
                    ev = c.inc("s", s.copy(out=HH[:, dbo * TC:(dbo + 1) * TC],
                                           in_=psb[:, :]))
                    evs.append(ev)
                    pslast[dbo % 2] = ev
                with ExitStack() as st3:
                    XB2 = st3.enter_context(nc.sbuf_tensor(_un("mXB2b"), [128, DB * TC], BF16))
                    SQ2 = st3.enter_context(nc.sbuf_tensor(_un("mSQ2b"), [128, DB * TC], BF16))
                    ps_s = st3.enter_context(nc.psum_tensor(_un("mps_sb"), [128, TC], F32))
                    ps_q = st3.enter_context(nc.psum_tensor(_un("mps_qb"), [128, TC], F32))
                    c.wait("s", evs[-1])
                    emit_ln_stats(c, env, lambda i: HH[:, i * TC:(i + 1) * TC], DB,
                                  MU, RSTD, XB2, SQ2, ps_s, ps_q, T1, T2, 1.0 / D,
                                  src_big=HH[:, :])
                    HHs = seg(HH[:, :], DB, TC)
                    c.inc("v", v.tensor_tensor(out=HHs, in0=HHs,
                                               in1=fbc(MU[:, :], DB, TC),
                                               op=ALU.subtract))
                    c.inc("v", v.tensor_tensor(out=HHs, in0=HHs,
                                               in1=fbc(RSTD[:, :], DB, TC), op=ALU.mult))
                    c.inc("v", v.tensor_tensor(out=HHs, in0=HHs,
                                               in1=sbc(MV[l][:, 112:120], DB, TC),
                                               op=ALU.mult))
                    c.inc("v", v.tensor_tensor(out=HHs, in0=HHs,
                                               in1=sbc(MV[l][:, 120:128], DB, TC),
                                               op=ALU.add))
                    c.inc("v", v.tensor_tensor(out=seg(XX[:, :], DB, TC),
                                               in0=seg(XX[:, :], DB, TC),
                                               in1=HHs, op=ALU.add))
                c.barrier()
        c.inc("v", v.tensor_tensor(out=seg(X[:, :], DB, TC), in0=seg(X[:, :], DB, TC),
                                   in1=seg(XX[:, :], DB, TC), op=ALU.add))
def host_prep(inputs):
    f32 = np.float32
    asc = np.ascontiguousarray
    consts = np.tile(np.array([0.0, 1.0, LNEPS, 0.5, -0.5, -1.0, C30, 127.0], f32),
                     (128, 1))

    def vec8(x):
        return asc(np.asarray(x, f32).reshape(8, 128).T)

    def vec16(x):
        return asc(np.asarray(x, f32).reshape(16, 128).T)

    avec = np.concatenate([vec8(inputs["L0_attn_a_time_mix_k"]),
                           vec8(inputs["L0_attn_a_time_mix_v"]),
                           vec8(inputs["L0_attn_a_time_mix_r"]),
                           vec8(inputs["L0_attn_a_time_decay"]),
                           vec8(inputs["L0_attn_a_time_first"])], axis=1)
    fvec = np.concatenate([asc(np.asarray(inputs["L0_ffn_b1"], f32).reshape(32, 128).T),
                           vec8(inputs["L0_ffn_b2"])], axis=1)
    shared = {
        "consts": consts,
        "identb": np.eye(128, dtype=ml_dtypes.bfloat16),
        "identf": np.eye(128, dtype=f32),
        "onesb": np.ones((128, 128), dtype=ml_dtypes.bfloat16),
        "onesf": np.ones((1, 512), dtype=f32),
        "selb": np.kron(np.eye(16), np.ones((1, 128))).astype(ml_dtypes.bfloat16),
        "wk": asc(np.asarray(inputs["L0_attn_a_W_k"], f32).T),
        "wv": asc(np.asarray(inputs["L0_attn_a_W_v"], f32).T),
        "wr": asc(np.asarray(inputs["L0_attn_a_W_r"], f32).T),
        "wo": asc(np.asarray(inputs["L0_attn_a_W_o"], f32).T),
        "avec": avec,
        "w1t": asc(np.asarray(inputs["L0_ffn_W1"], f32).T),
        "w2t": asc(np.asarray(inputs["L0_ffn_W2"], f32).T),
        "fvec": fvec,
    }
    for l in range(2):
        shared[f"inwt{l}"] = asc(np.asarray(inputs["L0_attn_b_in_proj_w"][l], f32).T)
        shared[f"outwt{l}"] = asc(np.asarray(inputs["L0_attn_b_out_proj_w"][l], f32).T)
        xp = np.zeros((DI, 128), f32)
        xp[:, 0:96] = np.asarray(inputs["L0_attn_b_x_proj_w"][l], f32).T
        shared[f"xpwt{l}"] = asc(xp)
        shared[f"dtwt{l}"] = asc(np.asarray(inputs["L0_attn_b_dt_proj_w"][l], f32).T)
        cwl = asc(np.asarray(inputs["L0_attn_b_conv_w"][l], f32)
                  .reshape(16, 128, 4).transpose(1, 0, 2).reshape(128, 64))
        mv = np.concatenate([
            cwl,
            vec16(inputs["L0_attn_b_conv_b"][l]),
            vec16(inputs["L0_attn_b_dt_proj_b"][l]),
            vec16(inputs["L0_attn_b_Dskip"][l]),
            vec8(inputs["L0_attn_b_ln_w"][l]),
            vec8(inputs["L0_attn_b_ln_b"][l]),
        ], axis=1)
        shared[f"mvec{l}"] = asc(mv)
        al = np.asarray(inputs["L0_attn_b_A_log"][l], f32)
        shared[f"alog{l}"] = asc(al.reshape(16, 128, 16).transpose(1, 0, 2).reshape(128, 256))
    x = np.asarray(inputs["L0_attn_a_x"], f32)
    in_maps = []
    for core in range(8):
        m = dict(shared)
        m["xin"] = asc(x[core * BC:(core + 1) * BC].transpose(0, 2, 1))
        in_maps.append(m)
    return in_maps


_NC_CACHE = {}


def kernel(**inputs):
    in_maps = host_prep(inputs)
    if "prog" not in _NC_CACHE:
        _NC_CACHE["prog"] = build_program()
    nc = _NC_CACHE["prog"]
    res = run_bass_kernel_spmd(nc, in_maps, core_ids=list(range(8)))
    outs = [np.asarray(res.results[core]["xout"]).transpose(0, 2, 1) for core in range(8)]
    return np.ascontiguousarray(np.concatenate(outs, axis=0).astype(np.float32))


# version 14
# speedup vs baseline: 1.0982x; 1.0982x over previous
import math
import os
import sys
from contextlib import ExitStack

if "/opt/trn_rl_repo" not in sys.path:
    sys.path.insert(0, "/opt/trn_rl_repo")

import numpy as np
import ml_dtypes

import concourse.bass as bass
import concourse.mybir as mybir
from concourse.bass_utils import run_bass_kernel_spmd

F32 = mybir.dt.float32
BF16 = mybir.dt.bfloat16
I32 = mybir.dt.int32
AF = mybir.ActivationFunctionType
ALU = mybir.AluOpType
AXX = mybir.AxisListType.X

D = 1024
T = 2048
BC = 2
DB = D // 128
TC = 512
NCH = T // TC
DI = 2048
DIB = DI // 128
NST = 16
RK = 64
HF = 4096
HB = HF // 128
FFT = 256
C30 = math.exp(-30.0)
LNEPS = 1e-5
WKVEPS = 1e-9

STAGES = int(os.environ.get("KERN_STAGES", "15"))
_UN = [0]


def _un(nm):
    _UN[0] += 1
    return f"{nm}_{_UN[0]}"



def seg(ap, nblk, width, lead=0):
    r = ap.rearrange("p (b w) -> p b w", b=nblk)
    return r[:, :, lead:lead + width] if lead else r


def fbc(ap, nblk, width):
    return ap.rearrange("p w -> p () w").broadcast_to([128, nblk, width])


def sbc(ap, nblk, width):
    return ap.rearrange("p b -> p b ()").broadcast_to([128, nblk, width])


class _FakeIns:
    __slots__ = ("incs", "post_fence")

    def __init__(self):
        self.incs = []
        self.post_fence = False

    def then_inc(self, sem, val):
        self.incs.append((sem, val))
        return self


class _Rec:

    def __init__(self, name, ctx):
        self._name = name
        self._ctx = ctx

    OPTOUT = ("X", "mXX", "mXXb", "mUB", "mDT", "mWT", "mYA", "mSZ", "mXCc",
              "mHH", "mYB", "fT0", "fHG", "fHQ", "fXQ", "aK", "aV", "aR",
              "aXKb", "aXVb", "aXRb", "aY", "aXN", "HST0", "HST1", "CCAR0",
              "CCAR1", "XPC", "PQS")

    @staticmethod
    def _base(nm):
        parts = nm.rsplit("_", 1)
        return parts[0] if len(parts) == 2 and parts[1].isdigit() else nm

    def __getattr__(self, meth):
        nm, ctx = self._name, self._ctx

        def call(*a, **kw):
            if meth in ("wait_ge", "nop", "dma_start", "engine_nop"):
                if meth == "dma_start":
                    pass
                fi = _FakeIns()
                ctx.log.append((nm, meth, a, kw, fi))
                return fi
            outs, ins_ = set(), set()
            for k, v_ in kw.items():
                t = getattr(v_, "tensor", None)
                if t is None or _Rec._base(t.name) in _Rec.OPTOUT:
                    continue
                (outs if k in ("out", "accum_out") else ins_).add(t.name)
            for v_ in a:
                t = getattr(v_, "tensor", None)
                if t is not None and _Rec._base(t.name) not in _Rec.OPTOUT:
                    ins_.add(t.name)
            special = meth in ("tensor_reduce", "reduce_max", "reduce_sum",
                               "reciprocal")
            if not special:
                cmp_ops = (ALU.is_gt, ALU.is_ge, ALU.is_lt, ALU.is_le,
                           ALU.is_equal, ALU.not_equal)
                for opk in ("op", "op0", "op1"):
                    if kw.get(opk) in cmp_ops:
                        special = True
                        break
            if not special:
                odt = getattr(kw.get("out"), "dtype", None)
                if odt is not None:
                    for k, v_ in kw.items():
                        if k in ("out", "accum_out"):
                            continue
                        dt_ = getattr(v_, "dtype", None)
                        if dt_ is not None and dt_ != odt:
                            special = True
                            break
            if not special:
                for v_ in list(kw.values()) + list(a):
                    apl = getattr(v_, "ap", None)
                    if apl is not None and len(apl) >= 3:
                        special = True
                        break
            pw, pr = ctx.pend_w[nm], ctx.pend_r[nm]
            if nm in ("v", "s", "g") and (
                    special or (ins_ & pw) or (outs & pw) or (outs & pr)):
                ctx.fence(nm)
            fi = _FakeIns()
            fi.post_fence = special and nm in ("v", "s", "g")
            ctx.log.append((nm, meth, a, kw, fi))
            ctx.pend_w[nm] |= outs
            ctx.pend_r[nm] |= ins_
            return fi
        return call


class Ctx:
    ENGS = ("v", "s", "t", "g", "q")

    def __init__(self, nc, block, stack):
        self.nc = nc
        self.block = block
        self.stack = stack
        self.log = []
        self.e = {k: _Rec(k, self) for k in self.ENGS}
        self.sem = {k: stack.enter_context(nc.semaphore(f"sem_{k}")) for k in self.ENGS}
        self.dsem = {}
        self.cnt = {k: 0 for k in self.ENGS}
        self.dcnt = {}
        self.lastw = {k: dict() for k in self.ENGS}
        self.pend_w = {k: set() for k in self.ENGS}
        self.pend_r = {k: set() for k in self.ENGS}
        self.fcnt = {k: -1 for k in self.ENGS}

    def finalize(self):
        def replay(eng, name):
            for (nm, meth, a, kw, fi) in self.log:
                if nm != name:
                    continue
                ins = getattr(eng, meth)(*a, **kw)
                for sem, val in fi.incs:
                    ins.then_inc(sem, val)

        block = self.block
        @block.vector
        def _(e):
            replay(e, "v")
        @block.scalar
        def _(e):
            replay(e, "s")
        @block.tensor
        def _(e):
            replay(e, "t")
        @block.gpsimd
        def _(e):
            replay(e, "g")
        @block.sync
        def _(e):
            replay(e, "q")

    def inc(self, e, ins):
        ins.then_inc(self.sem[e], 1)
        self.cnt[e] += 1
        tok = (e, self.cnt[e])
        if getattr(ins, "post_fence", False):
            self.fence(e)
        return tok

    def tok(self, e):
        return (e, self.cnt[e])

    def fence(self, e):
        self.pend_w[e].clear()
        self.pend_r[e].clear()
        if self.fcnt[e] == self.cnt[e]:
            return
        self.e[e].wait_ge(self.sem[e], self.cnt[e])
        self.fcnt[e] = self.cnt[e]

    def inc_fence(self, e, ins):
        tk = self.inc(e, ins)
        self.e[e].wait_ge(self.sem[e], self.cnt[e])
        return tk

    def dma(self, qe, stream, out, in_, **kw):
        if stream not in self.dsem:
            self.dsem[stream] = self.stack.enter_context(
                self.nc.semaphore(f"ds_{stream}"))
            self.dcnt[stream] = 0
        ins = self.e[qe].dma_start(out=out, in_=in_, **kw)
        ins.then_inc(self.dsem[stream], 16)
        self.dcnt[stream] += 16
        return ("D:" + stream, self.dcnt[stream])

    def wait(self, e, *tokens):
        for tk in tokens:
            if tk is None:
                continue
            pe, val = tk
            if pe == e:
                continue
            if pe.startswith("D:"):
                val = self.dcnt[pe[2:]]
            if self.lastw[e].get(pe, 0) >= val:
                continue
            sem = self.dsem[pe[2:]] if pe.startswith("D:") else self.sem[pe]
            self.e[e].wait_ge(sem, val)
            self.lastw[e][pe] = val

    def barrier(self, engines="vstgq"):
        toks = [(e, self.cnt[e]) for e in engines]
        dt_ = [("D:" + k, self.dcnt[k]) for k in self.dcnt]
        for e in engines:
            self.fence(e)
            self.wait(e, *toks, *dt_)


def build_program():
    nc = bass.Bass("TRN2", target_bir_lowering=False)

    xin = nc.dram_tensor("xin", [BC, D, T], F32, kind="ExternalInput").ap()
    consts = nc.dram_tensor("consts", [128, 8], F32, kind="ExternalInput").ap()
    identb = nc.dram_tensor("identb", [128, 128], BF16, kind="ExternalInput").ap()
    identf = nc.dram_tensor("identf", [128, 128], F32, kind="ExternalInput").ap()
    onesb = nc.dram_tensor("onesb", [128, 128], BF16, kind="ExternalInput").ap()
    onesf = nc.dram_tensor("onesf", [1, 512], F32, kind="ExternalInput").ap()
    selb = nc.dram_tensor("selb", [16, 16 * 128], BF16, kind="ExternalInput").ap()
    wsrc = {}
    for nm in ("wk", "wv", "wr", "wo"):
        wsrc[nm] = nc.dram_tensor(nm, [D, D], F32, kind="ExternalInput").ap()
    avec = nc.dram_tensor("avec", [128, 8 * 5], F32, kind="ExternalInput").ap()
    w1t = nc.dram_tensor("w1t", [D, HF], F32, kind="ExternalInput").ap()
    w2t = nc.dram_tensor("w2t", [HF, D], F32, kind="ExternalInput").ap()
    fvec = nc.dram_tensor("fvec", [128, HB + DB], F32, kind="ExternalInput").ap()
    inwt, outwt, xpwt, dtwt, mvec, alog = [], [], [], [], [], []
    for l in range(2):
        inwt.append(nc.dram_tensor(f"inwt{l}", [D, 2 * DI], F32, kind="ExternalInput").ap())
        outwt.append(nc.dram_tensor(f"outwt{l}", [DI, D], F32, kind="ExternalInput").ap())
        xpwt.append(nc.dram_tensor(f"xpwt{l}", [DI, 128], F32, kind="ExternalInput").ap())
        dtwt.append(nc.dram_tensor(f"dtwt{l}", [RK, DI], F32, kind="ExternalInput").ap())
        mvec.append(nc.dram_tensor(f"mvec{l}", [128, 128], F32, kind="ExternalInput").ap())
        alog.append(nc.dram_tensor(f"alog{l}", [128, DIB * NST], F32, kind="ExternalInput").ap())
    wbf = {nm: nc.dram_tensor(nm + "b", [D, D], BF16, kind="Internal").ap()
           for nm in ("wk", "wv", "wr", "wo")}
    w1q = nc.dram_tensor("w1q", [D, HF], BF16, kind="Internal").ap()
    w2q = nc.dram_tensor("w2q", [HF, D], BF16, kind="Internal").ap()
    inwb = [nc.dram_tensor(f"inwb{l}", [D, 2 * DI], BF16, kind="Internal").ap() for l in range(2)]
    outwb = [nc.dram_tensor(f"outwb{l}", [DI, D], BF16, kind="Internal").ap() for l in range(2)]
    xpwb = [nc.dram_tensor(f"xpwb{l}", [DI, 128], BF16, kind="Internal").ap() for l in range(2)]
    dtwb = [nc.dram_tensor(f"dtwb{l}", [RK, DI], BF16, kind="Internal").ap() for l in range(2)]
    xout = nc.dram_tensor("xout", [BC, D, T], F32, kind="ExternalOutput").ap()

    stack = ExitStack()
    with stack:
        block = stack.enter_context(nc.Block())
        c = Ctx(nc, block, stack)
        v, s, t_, g, q = (c.e[k] for k in Ctx.ENGS)
        sb = lambda nm, shp, dt=F32: stack.enter_context(nc.sbuf_tensor(nm, shp, dt))

        CT = sb("CT", [128, 8])
        for _j, _cv in enumerate([0.0, 1.0, LNEPS, 0.5, -0.5, -1.0, C30, 127.0]):
            nc.const_aps.register(F32, _cv, CT[:, _j:_j + 1])
        IDE = sb("IDE", [128, 128], BF16)
        IDF = sb("IDF", [128, 128])
        ONE = sb("ONE", [128, 128], BF16)
        ONF = sb("ONF", [1, 512])
        SEL = sb("SEL", [16, 16 * 128], BF16)
        AV = sb("AV", [128, 8 * 5])
        OMV = sb("OMV", [128, 8 * 3])
        FV = sb("FV", [128, HB + DB])
        MV = [sb(f"MV{l}", [128, 128]) for l in range(2)]
        ATL = [sb(f"ATL{l}", [128, DIB * NST]) for l in range(2)]
        WS = sb("WS", [128, 2])
        X = sb("X", [128, DB * TC])
        XPC = sb("XPC", [128, DB])
        PQS = sb("PQS", [128, 2 * DB])
        HSTT = [sb(f"HST{l}", [128, NST * DIB], BF16) for l in range(2)]
        CCAR = [sb(f"CCAR{l}", [128, DIB * 3]) for l in range(2)]

        env = dict(nc=nc, c=c, CT=CT, IDE=IDE, IDF=IDF, ONE=ONE, ONF=ONF, AV=AV,
                   OMV=OMV, FV=FV, MV=MV, ATL=ATL, WS=WS, X=X, XPC=XPC, PQS=PQS,
                   HSTT=HSTT, CCAR=CCAR, wbf=wbf, w1q=w1q, w2q=w2q, inwb=inwb,
                   outwb=outwb, xpwb=xpwb, dtwb=dtwb, SEL=SEL)

        ld0 = [c.dma("q", "init", CT[:, :], consts[:, :]),
               c.dma("q", "init", IDE[:, :], identb[:, :]),
               c.dma("q", "init", IDF[:, :], identf[:, :]),
               c.dma("q", "init", ONE[:, :], onesb[:, :]),
               c.dma("q", "init", ONF[:, :], onesf[:, :]),
               c.dma("q", "init", SEL[:, :], selb[:, :]),
               c.dma("q", "init", AV[:, :], avec[:, :]),
               c.dma("q", "init", FV[:, :], fvec[:, :])]
        for l in range(2):
            ld0.append(c.dma("q", "init", MV[l][:, :], mvec[l][:, :]))
            ld0.append(c.dma("q", "init", ATL[l][:, :], alog[l][:, :]))
        c.wait("v", *ld0)
        c.wait("s", *ld0)
        for j in range(3):
            c.inc("v", v.tensor_scalar(out=OMV[:, j * 8:(j + 1) * 8], in0=AV[:, j * 8:(j + 1) * 8],
                                       scalar1=-1.0, scalar2=1.0, op0=ALU.mult, op1=ALU.add))
        for l in range(2):
            c.inc("s", s.activation(out=ATL[l][:, :], in_=ATL[l][:, :], func=AF.Exp))
            c.wait("v", c.tok("s"))
            c.inc("v", v.tensor_scalar_mul(out=ATL[l][:, :], in0=ATL[l][:, :], scalar1=-1.0))
        c.barrier()

        for args in ((wsrc["wk"], wbf["wk"], D, D), (wsrc["wv"], wbf["wv"], D, D),
                     (wsrc["wr"], wbf["wr"], D, D), (wsrc["wo"], wbf["wo"], D, D),
                     (inwt[0], inwb[0], D, 2 * DI), (outwt[0], outwb[0], DI, D),
                     (xpwt[0], xpwb[0], DI, 128), (dtwt[0], dtwb[0], RK, DI),
                     (inwt[1], inwb[1], D, 2 * DI), (outwt[1], outwb[1], DI, D),
                     (xpwt[1], xpwb[1], DI, 128), (dtwt[1], dtwb[1], RK, DI)):
            emit_wcast(c, *args)
            c.barrier("qg")
        emit_wquant(c, env, w1t, w1q, 0, D, HF)
        c.barrier()
        emit_wquant(c, env, w2t, w2q, 1, HF, D)
        c.barrier()

        for b in range(BC):
            zc = CT[:, 0:1]
            c.inc("v", v.tensor_copy(out=XPC[:, :], in_=zc.broadcast_to([128, DB])))
            c.inc("v", v.tensor_scalar(out=PQS[:, 0:DB], in0=zc.broadcast_to([128, DB]),
                                       scalar1=0.0, scalar2=-1.0e30, op0=ALU.mult, op1=ALU.add))
            c.inc("v", v.tensor_copy(out=PQS[:, DB:2 * DB], in_=zc.broadcast_to([128, DB])))
            for l in range(2):
                c.inc("v", v.tensor_copy(out=HSTT[l][:, :], in_=zc.broadcast_to([128, NST * DIB])))
                c.inc("v", v.tensor_copy(out=CCAR[l][:, :], in_=zc.broadcast_to([128, DIB * 3])))
            c.barrier()

            for ch in range(NCH):
                t0 = ch * TC
                ldx = c.dma("q", "xload", X[:, :].rearrange("p (db t) -> p db t", db=DB),
                            xin[b].rearrange("(db p) t -> p db t", p=128)[:, :, t0:t0 + TC])
                for e in ("v", "s", "t"):
                    c.wait(e, ldx)
                if STAGES & 1:
                    emit_attn_a(c, env)
                    c.barrier()
                if STAGES & 2:
                    emit_ffn(c, env)
                    c.barrier()
                if STAGES & 4:
                    emit_mamba(c, env)
                    c.barrier()
                if STAGES & 8:
                    emit_ffn(c, env)
                    c.barrier()
                c.dma("q", "xstore", xout[b].rearrange("(db p) t -> p db t", p=128)[:, :, t0:t0 + TC],
                      X[:, :].rearrange("p (db t) -> p db t", db=DB))
                c.barrier()
        c.barrier()
        c.finalize()
    return nc


def emit_wcast(c, src, dst, K, M):
    nc = c.nc
    colchunk = min(M, 2048)
    with ExitStack() as st:
        wt = [st.enter_context(nc.sbuf_tensor(_un(f"wc_t{i}"), [128, colchunk], BF16))
              for i in range(2)]
        prev_store = [None, None]
        idx = 0
        for i in range((K + 127) // 128):
            pb = min(128, K - i * 128)
            for j in range(M // colchunk):
                p = idx % 2
                idx += 1
                c.wait("g", prev_store[p])
                tk = c.dma("g", f"wci{p}", wt[p][0:pb, :],
                           src[i * 128:i * 128 + pb, j * colchunk:(j + 1) * colchunk])
                c.wait("q", tk)
                prev_store[p] = c.dma(
                    "q", f"wco{p}",
                    dst[i * 128:i * 128 + pb, j * colchunk:(j + 1) * colchunk],
                    wt[p][0:pb, :])


def emit_wquant(c, env, src, dst, widx, K, M):
    nc = c.nc
    v, s, t_, g, q = (c.e[k] for k in Ctx.ENGS)
    ONE, WS = env["ONE"], env["WS"]
    nch, mch = K // 128, M // 512
    with ExitStack() as st:
        sbt = lambda nm, shp, dt=F32: st.enter_context(nc.sbuf_tensor(_un(nm), shp, dt))
        wt = [sbt(f"wq_in{i}", [128, 512]) for i in range(2)]
        acc = sbt("wq_acc", [128, 1])
        red = sbt("wq_red", [128, 1])
        accb = sbt("wq_accb", [128, 1], BF16)
        rws = sbt("wq_rws", [128, 1])
        wqt = [sbt(f"wq_out{i}", [128, 512], BF16) for i in range(2)]
        uq = sbt("wq_u", [128, 512])
        iq = sbt("wq_i", [128, 512], I32)
        cq = sbt("wq_c", [128, 512])
        vq = sbt("wq_v", [128, 512])
        ps = st.enter_context(nc.psum_tensor(_un("wq_ps"), [128, 1], F32))
        first = True
        prev_red = [None, None]
        for idx in range(nch * mch):
            i, j = divmod(idx, mch)
            p = idx % 2
            c.wait("q", prev_red[p])
            tk = c.dma("q", f"wq{p}", wt[p][:, :], src[i * 128:(i + 1) * 128, j * 512:(j + 1) * 512])
            c.wait("v", tk)
            c.fence("v")
            c.inc_fence("v", v.tensor_reduce(out=red[:, :], in_=wt[p][:, :], axis=AXX,
                                             op=ALU.add, apply_absolute_value=True))
            if first:
                rt = c.inc("v", v.tensor_copy(out=acc[:, :], in_=red[:, :]))
                first = False
            else:
                rt = c.inc("v", v.tensor_add(out=acc[:, :], in0=acc[:, :], in1=red[:, :]))
            prev_red[p] = rt
        c.fence("v")
        cb = c.inc("v", v.tensor_copy(out=accb[:, :], in_=acc[:, :]))
        c.wait("t", cb)
        mm = c.inc("t", t_.matmul(ps[:, :], lhsT=ONE[:, :], rhs=accb[:, :],
                                  start=True, stop=True))
        c.wait("v", mm)
        c.inc("v", v.tensor_scalar(out=WS[:, widx:widx + 1], in0=ps[:, :],
                                   scalar1=1.0 / (K * M), scalar2=LNEPS,
                                   op0=ALU.mult, op1=ALU.max))
        c.fence("v")
        rv = c.inc_fence("v", v.reciprocal(out=rws[:, :], in_=WS[:, widx:widx + 1]))
        prev_st = [rv, rv]
        prev_wq = [None, None]
        for idx in range(nch * mch):
            i, j = divmod(idx, mch)
            p = idx % 2
            c.wait("q", prev_st[p])
            tk = c.dma("q", f"wq{p}", wt[p][:, :], src[i * 128:(i + 1) * 128, j * 512:(j + 1) * 512])
            c.wait("v", tk, prev_wq[p])
            c.inc("v", v.tensor_scalar(out=vq[:, :], in0=wt[p][:, :], scalar1=rws[:, 0:1],
                                       scalar2=0.5, op0=ALU.mult, op1=ALU.add))
            c.inc("v", v.tensor_copy(out=iq[:, :], in_=vq[:, :]))
            c.inc("v", v.tensor_copy(out=uq[:, :], in_=iq[:, :]))
            c.inc("v", v.tensor_tensor(out=cq[:, :], in0=uq[:, :], in1=vq[:, :], op=ALU.is_gt))
            c.inc("v", v.tensor_tensor(out=uq[:, :], in0=uq[:, :], in1=cq[:, :], op=ALU.subtract))
            wq = c.inc("v", v.tensor_scalar(out=wqt[p][:, :], in0=uq[:, :], scalar1=-1.0,
                                            scalar2=1.0, op0=ALU.max, op1=ALU.min))
            c.wait("q", wq)
            stt = c.dma("q", f"ws{p}", dst[i * 128:(i + 1) * 128, j * 512:(j + 1) * 512], wqt[p][:, :])
            prev_st[p] = wq
            prev_wq[p] = stt
        c.barrier("vqt")


def emit_ln_stats(c, env, src_fn, nblk, MU, RSTD, XB2, SQ2, ps_s, ps_q, T1, T2, invn,
                  src_big=None):
    nc = c.nc
    v, s, t_, g, q = (c.e[k] for k in Ctx.ENGS)
    ONE = env["ONE"]
    if src_big is not None:
        c.inc("s", s.copy(out=XB2[:, 0:nblk * TC], in_=src_big))
        sq = c.inc("s", s.activation(out=SQ2[:, 0:nblk * TC], in_=src_big,
                                     func=AF.Square))
        c.wait("t", sq)
        for i in range(nblk):
            c.inc("t", t_.matmul(ps_s[:, :], lhsT=ONE[:, :],
                                 rhs=XB2[:, i * TC:(i + 1) * TC],
                                 start=(i == 0), stop=(i == nblk - 1)))
            mm = c.inc("t", t_.matmul(ps_q[:, :], lhsT=ONE[:, :],
                                      rhs=SQ2[:, i * TC:(i + 1) * TC],
                                      start=(i == 0), stop=(i == nblk - 1)))
        c.wait("v", mm)
    else:
        prev_mm = [None, None]
        for i in range(nblk):
            p = i % 2
            c.wait("s", prev_mm[p])
            cb = c.inc("s", s.copy(out=XB2[:, p * TC:(p + 1) * TC], in_=src_fn(i)))
            sq = c.inc("s", s.activation(out=SQ2[:, p * TC:(p + 1) * TC], in_=src_fn(i),
                                         func=AF.Square))
            c.wait("t", sq)
            c.inc("t", t_.matmul(ps_s[:, :], lhsT=ONE[:, :],
                                 rhs=XB2[:, p * TC:(p + 1) * TC],
                                 start=(i == 0), stop=(i == nblk - 1)))
            mm = c.inc("t", t_.matmul(ps_q[:, :],
                                      lhsT=ONE[:, :], rhs=SQ2[:, p * TC:(p + 1) * TC],
                                      start=(i == 0), stop=(i == nblk - 1)))
            prev_mm[p] = mm
        c.wait("v", prev_mm[(nblk - 1) % 2])
    c.inc("v", v.tensor_scalar_mul(out=MU[:, :], in0=ps_s[:, :], scalar1=invn))
    m2 = c.inc("v", v.tensor_scalar_mul(out=T1[:, :], in0=ps_q[:, :], scalar1=invn))
    c.wait("s", m2)
    sq2 = c.inc("s", s.activation(out=T2[:, :], in_=MU[:, :], func=AF.Square))
    c.wait("v", sq2)
    vv = c.inc("v", v.tensor_tensor(out=T1[:, :], in0=T1[:, :], in1=T2[:, :], op=ALU.subtract))
    c.wait("s", vv)
    c.inc("s", s.activation(out=T1[:, :], in_=T1[:, :], func=AF.Ln, bias=LNEPS))
    rs = c.inc("s", s.activation(out=RSTD[:, :], in_=T1[:, :], func=AF.Exp, scale=-0.5))
    c.wait("v", rs)
    return rs


def emit_floor_round(c, VQ, IQ, UQ, CQ, out_ap, n):
    v = c.e["v"]
    c.inc("v", v.tensor_copy(out=IQ[:, 0:n], in_=VQ[:, 0:n]))
    c.inc("v", v.tensor_copy(out=UQ[:, 0:n], in_=IQ[:, 0:n]))
    c.inc("v", v.tensor_tensor(out=CQ[:, 0:n], in0=UQ[:, 0:n], in1=VQ[:, 0:n], op=ALU.is_gt))
    return c.inc("v", v.tensor_tensor(out=out_ap, in0=UQ[:, 0:n], in1=CQ[:, 0:n],
                                      op=ALU.subtract))


def emit_bridge(c, env, col_src, ncols, ps_row, ROW, ps_rep, width, prev=None):
    nc = c.nc
    v, s, t_, g, q = (c.e[k] for k in Ctx.ENGS)
    IDF, ONF = env["IDF"], env["ONF"]
    ev = prev
    for i in range(ncols):
        c.wait("t", ev)
        tr = c.inc("t", t_.transpose(ps_row[0:1, :], col_src[:, i:i + 1], IDF[:, :]))
        c.wait("s", tr)
        ev = c.inc("s", s.copy(out=ROW[0:1, i * 128:(i + 1) * 128], in_=ps_row[0:1, :]))
    c.wait("t", ev, prev)
    mm = c.inc("t", t_.matmul(ps_rep[:, 0:width], lhsT=ONF[0:1, 0:128],
                              rhs=ROW[0:1, 0:width], start=True, stop=True))
    return mm


def emit_attn_a(c, env):
    nc = c.nc
    v, s, t_, g, q = (c.e[k] for k in Ctx.ENGS)
    X, CT, ONE, AV, OMV, XPC, PQS, wbf = (env[k] for k in
                                          ("X", "CT", "ONE", "AV", "OMV", "XPC", "PQS", "wbf"))
    tmax = {"k": 0, "v": 1, "r": 2}

    def av(j, db):
        return AV[:, j * 8 + db: j * 8 + db + 1]

    def omv(j, db):
        return OMV[:, j * 8 + db: j * 8 + db + 1]

    WDEC = lambda db: AV[:, 24 + db: 24 + db + 1]
    UFIR = lambda db: AV[:, 32 + db: 32 + db + 1]

    def xdb(db):
        return X[:, db * TC:(db + 1) * TC]

    with ExitStack() as st:
        sbt = lambda nm, shp, dt=F32: st.enter_context(nc.sbuf_tensor(_un(nm), shp, dt))
        XN = sbt("aXN", [128, DB * (TC + 1)])
        XB2 = sbt("aXB2", [128, DB * TC], BF16)
        SQ2 = sbt("aSQ2", [128, DB * TC], BF16)
        MU = sbt("aMU", [128, TC])
        RSTD = sbt("aRSTD", [128, TC])
        T1 = sbt("aT1", [128, TC])
        T2 = sbt("aT2", [128, TC])
        XKb = sbt("aXKb", [128, DB * TC], BF16)
        XVb = sbt("aXVb", [128, DB * TC], BF16)
        XRb = sbt("aXRb", [128, DB * TC], BF16)
        K_ = sbt("aK", [128, DB * TC], BF16)
        V_ = sbt("aV", [128, DB * TC], BF16)
        R_ = sbt("aR", [128, DB * TC], BF16)
        Y_ = sbt("aY", [128, DB * TC], BF16)
        WT = sbt("aWT", [128, DB * 1024], BF16)
        WT2 = sbt("aWT2", [128, DB * 1024], BF16)
        WT3 = sbt("aWT3", [128, DB * 1024], BF16)
        P_ = sbt("aP", [128, TC + 1])
        Q_ = sbt("aQ", [128, TC + 1])
        SA = sbt("aSA", [128, TC])
        SB = sbt("aSB", [128, TC])
        EA = sbt("aEA", [128, TC])
        EB = sbt("aEB", [128, TC])
        AQ = sbt("aAQ", [128, TC])
        BQ = sbt("aBQ", [128, TC])
        MM_ = sbt("aM", [128, TC])
        SO = sbt("aSO", [128, TC])
        SC = sbt("aSC", [128, TC])
        N1 = sbt("aN1", [128, TC])
        N2 = sbt("aN2", [128, TC])
        DEN = sbt("aDEN", [128, TC])
        ps_s = st.enter_context(nc.psum_tensor(_un("aps_s"), [128, TC], F32))
        ps_q = st.enter_context(nc.psum_tensor(_un("aps_q"), [128, TC], F32))
        ps_m = [st.enter_context(nc.psum_tensor(_un(f"aps_m{i}"), [128, TC], F32))
                for i in range(2)]

        wk_tok = c.dma("q", "awt", WT[:, :].rearrange("p (db e) -> p db e", db=DB),
                       wbf["wk"].rearrange("(db p) e -> p db e", p=128))
        wv_tok = c.dma("q", "awt2", WT2[:, :].rearrange("p (db e) -> p db e", db=DB),
                       wbf["wv"].rearrange("(db p) e -> p db e", p=128))
        wr_tok = c.dma("q", "awt3", WT3[:, :].rearrange("p (db e) -> p db e", db=DB),
                       wbf["wr"].rearrange("(db p) e -> p db e", p=128))

        emit_ln_stats(c, env, xdb, DB, MU, RSTD, XB2, SQ2, ps_s, ps_q, T1, T2, 1.0 / D,
                      src_big=X[:, :])
        c.inc("v", v.tensor_copy(
            out=XN[:, :].rearrange("p (db t) -> p db t", t=TC + 1)[:, :, 0:1],
            in_=XPC[:, :].rearrange("p db -> p db ()")))
        XNs = seg(XN[:, :], DB, TC, lead=1)
        c.inc("v", v.tensor_tensor(out=XNs, in0=seg(X[:, :], DB, TC),
                                   in1=fbc(MU[:, :], DB, TC), op=ALU.subtract))
        c.inc("v", v.tensor_tensor(out=XNs, in0=XNs,
                                   in1=fbc(RSTD[:, :], DB, TC), op=ALU.mult))
        c.inc("v", v.tensor_copy(
            out=XPC[:, :].rearrange("p db -> p db ()"),
            in_=XN[:, :].rearrange("p (db t) -> p db t", t=TC + 1)[:, :, TC:TC + 1]))
        for j, XBt in ((0, XKb), (1, XVb), (2, XRb)):
            for db in range(DB):
                base = db * (TC + 1)
                c.inc("v", v.tensor_scalar_mul(out=T1[:, :], in0=XN[:, base:base + TC],
                                               scalar1=omv(j, db)))
                c.inc("v", v.scalar_tensor_tensor(
                    out=XBt[:, db * TC:(db + 1) * TC], in0=XN[:, base + 1:base + TC + 1],
                    scalar=av(j, db), in1=T1[:, :], op0=ALU.mult, op1=ALU.add))
        mix_done = c.tok("v")

        pslast = [None, None]

        c.wait("t", mix_done)

        c.wait("t", wk_tok, wv_tok, wr_tok)
        k_evs, v_evs, r_evs = [], [], []
        cnt3 = [0]
        for eb in range(DB):
            for j, (wtile, srcb, dst, act, evl) in enumerate((
                    (WT, XKb, K_, None, k_evs),
                    (WT2, XVb, V_, None, v_evs),
                    (WT3, XRb, R_, AF.Sigmoid, r_evs))):
                pp = cnt3[0] % 2
                cnt3[0] += 1
                psb = ps_m[pp]
                if pslast[pp] is not None:
                    c.wait("t", pslast[pp])
                for db in range(DB):
                    mm = t_.matmul(psb[:, :],
                                   lhsT=wtile[:, db * 1024 + eb * 128: db * 1024 + (eb + 1) * 128],
                                   rhs=srcb[:, db * TC:(db + 1) * TC],
                                   start=(db == 0), stop=(db == DB - 1))
                mmt = c.inc("t", mm)
                c.wait("s", mmt)
                if act is None:
                    ev = c.inc("s", s.copy(out=dst[:, eb * TC:(eb + 1) * TC], in_=psb[:, :]))
                else:
                    ev = c.inc("s", s.activation(out=dst[:, eb * TC:(eb + 1) * TC],
                                                 in_=psb[:, :], func=act))
                pslast[pp] = ev
                evl.append(ev)
        c.wait("q", c.tok("t"))
        wo_tok = c.dma("q", "awt", WT[:, :].rearrange("p (db e) -> p db e", db=DB),
                       wbf["wo"].rearrange("(db p) e -> p db e", p=128))

        for eb in range(DB):
            c.wait("v", k_evs[eb], v_evs[eb], r_evs[eb])
            kk = K_[:, eb * TC:(eb + 1) * TC]
            vv = V_[:, eb * TC:(eb + 1) * TC]
            rr = R_[:, eb * TC:(eb + 1) * TC]
            c.inc("v", v.tensor_copy(out=P_[:, 0:1], in_=PQS[:, eb:eb + 1]))
            c.inc("v", v.tensor_copy(out=Q_[:, 0:1], in_=PQS[:, DB + eb:DB + eb + 1]))
            c.inc("v", v.tensor_scalar_add(out=T1[:, :], in0=kk, scalar1=UFIR(eb)))
            c.inc("v", v.tensor_tensor_scan(
                out=P_[:, 1:TC + 1], data0=WDEC(eb).broadcast_to([128, TC]), data1=kk,
                initial=P_[:, 0:1], op0=ALU.add, op1=ALU.max))
            c.inc("v", v.scalar_tensor_tensor(out=SA[:, :], in0=P_[:, 0:TC],
                                              scalar=WDEC(eb), in1=P_[:, 1:TC + 1],
                                              op0=ALU.add, op1=ALU.subtract))
            c.inc("v", v.tensor_tensor(out=SB[:, :], in0=kk, in1=P_[:, 1:TC + 1],
                                       op=ALU.subtract))
            sb_done = c.tok("v")
            c.wait("s", sb_done)
            c.inc("s", s.activation(out=EA[:, :], in_=SA[:, :], func=AF.Exp))
            e_done = c.inc("s", s.activation(out=EB[:, :], in_=SB[:, :], func=AF.Exp))
            c.wait("v", e_done)
            c.inc("v", v.tensor_scalar_max(out=AQ[:, :], in0=EA[:, :], scalar1=C30))
            c.inc("v", v.scalar_tensor_tensor(out=BQ[:, :], in0=EB[:, :], scalar=C30,
                                              in1=vv, op0=ALU.max, op1=ALU.mult))
            c.inc("v", v.tensor_tensor_scan(
                out=Q_[:, 1:TC + 1], data0=AQ[:, :], data1=BQ[:, :],
                initial=Q_[:, 0:1], op0=ALU.mult, op1=ALU.add))
            c.inc("v", v.tensor_copy(out=PQS[:, eb:eb + 1], in_=P_[:, TC:TC + 1]))
            c.inc("v", v.tensor_copy(out=PQS[:, DB + eb:DB + eb + 1], in_=Q_[:, TC:TC + 1]))
            c.inc("v", v.tensor_tensor(out=MM_[:, :], in0=P_[:, 0:TC], in1=T1[:, :],
                                       op=ALU.max))
            c.inc("v", v.tensor_tensor(out=SA[:, :], in0=P_[:, 0:TC], in1=MM_[:, :],
                                       op=ALU.subtract))
            c.inc("v", v.tensor_tensor(out=SB[:, :], in0=T1[:, :], in1=MM_[:, :],
                                       op=ALU.subtract))
            so_in = c.tok("v")
            c.wait("s", so_in)
            c.inc("s", s.activation(out=SO[:, :], in_=SA[:, :], func=AF.Exp))
            sc_done = c.inc("s", s.activation(out=SC[:, :], in_=SB[:, :], func=AF.Exp))
            c.wait("v", sc_done)
            c.inc("v", v.tensor_scalar_max(out=SO[:, :], in0=SO[:, :], scalar1=C30))
            c.inc("v", v.tensor_scalar_max(out=SC[:, :], in0=SC[:, :], scalar1=C30))
            c.inc("v", v.tensor_tensor(out=N1[:, :], in0=SO[:, :], in1=Q_[:, 0:TC],
                                       op=ALU.mult))
            c.inc("v", v.tensor_tensor(out=N2[:, :], in0=SC[:, :], in1=vv, op=ALU.mult))
            c.inc("v", v.tensor_tensor(out=N1[:, :], in0=N1[:, :], in1=N2[:, :], op=ALU.add))
            dn = c.inc("v", v.scalar_tensor_tensor(out=DEN[:, :], in0=SO[:, :], scalar=WKVEPS,
                                                   in1=SC[:, :], op0=ALU.add, op1=ALU.add))
            c.wait("s", dn)
            c.inc("s", s.activation(out=DEN[:, :], in_=DEN[:, :], func=AF.Ln))
            rd = c.inc("s", s.activation(out=DEN[:, :], in_=DEN[:, :], func=AF.Exp, scale=-1.0))
            c.wait("v", rd)
            c.inc("v", v.tensor_tensor(out=N1[:, :], in0=N1[:, :], in1=DEN[:, :], op=ALU.mult))
            c.inc("v", v.tensor_tensor(out=Y_[:, eb * TC:(eb + 1) * TC], in0=rr,
                                       in1=N1[:, :], op=ALU.mult))
        y_done = c.tok("v")

        c.wait("t", wo_tok, y_done)
        for dbo in range(DB):
            psb = ps_m[dbo % 2]
            if pslast[dbo % 2] is not None:
                c.wait("t", pslast[dbo % 2])
            for ebi in range(DB):
                mm = t_.matmul(psb[:, :],
                               lhsT=WT[:, ebi * 1024 + dbo * 128: ebi * 1024 + (dbo + 1) * 128],
                               rhs=Y_[:, ebi * TC:(ebi + 1) * TC],
                               start=(ebi == 0), stop=(ebi == DB - 1))
            mmt = c.inc("t", mm)
            c.wait("v", mmt)
            pslast[dbo % 2] = c.inc("v", v.tensor_tensor(out=xdb(dbo), in0=xdb(dbo),
                                                         in1=psb[:, :], op=ALU.add))


def emit_ffn(c, env):
    nc = c.nc
    v, s, t_, g, q = (c.e[k] for k in Ctx.ENGS)
    X, CT, ONE, IDF, ONF, FV, WS = (env[k] for k in
                                    ("X", "CT", "ONE", "IDF", "ONF", "FV", "WS"))
    w1q, w2q = env["w1q"], env["w2q"]

    def xdb(db):
        return X[:, db * TC:(db + 1) * TC]

    with ExitStack() as st:
        sbt = lambda nm, shp, dt=F32: st.enter_context(nc.sbuf_tensor(_un(nm), shp, dt))
        XB2 = sbt("fXB2", [128, DB * TC], BF16)
        SQ2 = sbt("fSQ2", [128, DB * TC], BF16)
        MU = sbt("fMU", [128, TC])
        RSTD = sbt("fRSTD", [128, TC])
        T1 = sbt("fT1", [128, TC])
        T2 = sbt("fT2", [128, TC])
        T0 = sbt("fT0", [128, DB * TC])
        QREP = sbt("fQREP", [128, TC])
        AREP = sbt("fAREP", [128, TC])
        MXC = sbt("fMXC", [128, 4])
        ROW = sbt("fROW", [1, TC])
        XQ = sbt("fXQ", [128, DB * FFT], BF16)
        HQ = sbt("fHQ", [128, HB * FFT], BF16)
        HG = sbt("fHG", [128, HB * FFT])
        UQ = sbt("fUQ", [128, 8 * FFT])
        IQ = sbt("fIQ", [128, 8 * FFT], I32)
        CQ = sbt("fCQ", [128, 8 * FFT])
        VQ = sbt("fVQ", [128, 8 * FFT])
        WP = [sbt("fWPa", [128, DB * 512], BF16), sbt("fWPb", [128, DB * 512], BF16)]
        ps_s = st.enter_context(nc.psum_tensor(_un("fps_s"), [128, TC], F32))
        ps_q = st.enter_context(nc.psum_tensor(_un("fps_q"), [128, TC], F32))
        ps_m = [st.enter_context(nc.psum_tensor(_un(f"fps_m{i}"), [128, FFT], F32))
                for i in range(2)]
        ps_t = st.enter_context(nc.psum_tensor(_un("fps_t"), [128, 512], F32))
        ps_t2 = st.enter_context(nc.psum_tensor(_un("fps_t2"), [128, 512], F32))
        MXG = sbt("fMXG", [128, 16])
        ps_row = st.enter_context(nc.psum_tensor(_un("fps_row"), [1, 128], F32))
        ps_b = st.enter_context(nc.psum_tensor(_un("fps_b"), [128, TC], F32))

        emit_ln_stats(c, env, xdb, DB, MU, RSTD, XB2, SQ2, ps_s, ps_q, T1, T2, 1.0 / D,
                      src_big=X[:, :])
        c.inc("v", v.tensor_tensor(out=seg(T0[:, :], DB, TC), in0=seg(X[:, :], DB, TC),
                                   in1=fbc(MU[:, :], DB, TC), op=ALU.subtract))
        t0_done = c.tok("v")
        c.wait("t", t0_done)
        red_bank = [None, None]
        rr = 0
        for sl in range(4):
            for grp in range(2):
                pt = (ps_t, ps_t2)[rr % 2]
                trs = None
                for k in range(4):
                    db = grp * 4 + k
                    if k == 0:
                        c.wait("t", red_bank[rr % 2])
                    trs = c.inc("t", t_.transpose(
                        pt[:, k * 128:(k + 1) * 128],
                        T0[:, db * TC + sl * 128: db * TC + (sl + 1) * 128], IDF[:, :]))
                c.wait("v", trs, t0_done)
                c.fence("v")
                red_bank[rr % 2] = c.inc_fence("v", v.tensor_reduce(
                    out=MXG[:, sl * 2 + grp: sl * 2 + grp + 1],
                    in_=pt[:, :], axis=AXX, op=ALU.max,
                    apply_absolute_value=True))
                rr += 1
        c.fence("v")
        c.inc_fence("v", v.tensor_reduce(
            out=MXC[:, 0:4], in_=MXG[:, 0:8].rearrange("p (a b) -> p a b", a=4),
            axis=AXX, op=ALU.max, apply_absolute_value=True))
        bq = emit_bridge(c, env, MXC, 4, ps_row, ROW, ps_b, TC, prev=c.tok("v"))
        c.wait("v", bq)
        c.inc("v", v.tensor_tensor(out=T1[:, :], in0=RSTD[:, :], in1=ps_b[:, :], op=ALU.mult))
        c.inc("v", v.tensor_scalar_max(out=T1[:, :], in0=T1[:, :], scalar1=LNEPS))
        c.inc("v", v.reciprocal(out=T2[:, :], in_=T1[:, :]))
        c.inc("v", v.tensor_scalar_mul(out=T2[:, :], in0=T2[:, :], scalar1=127.0))
        c.inc("v", v.tensor_tensor(out=QREP[:, :], in0=T2[:, :], in1=RSTD[:, :], op=ALU.mult))
        c.inc("v", v.tensor_scalar(out=AREP[:, :], in0=T1[:, :], scalar1=WS[:, 0:1],
                                   scalar2=1.0 / 127.0, op0=ALU.mult, op1=ALU.mult))
        qa_done = c.tok("v")
        pslast = [None, None]
        wp_guard = [None, None]

        for hf in range(TC // FFT):
            toff = hf * FFT
            W8 = 8 * FFT
            t0v = T0[:, :].rearrange("p (b t) -> p b t", b=DB)[:, :, toff:toff + FFT]
            c.inc("v", v.tensor_tensor(out=seg(VQ[:, :], DB, FFT), in0=t0v,
                                       in1=fbc(QREP[:, toff:toff + FFT], DB, FFT),
                                       op=ALU.mult))
            c.inc("v", v.tensor_scalar_add(out=VQ[:, 0:W8], in0=VQ[:, 0:W8], scalar1=0.5))
            emit_floor_round(c, VQ, IQ, UQ, CQ, XQ[:, 0:W8], W8)
            xq_done = c.tok("v")
            def wp_dma1(chunk, p):
                c.wait("q", wp_guard[p])
                wsrc = w1q.rearrange("(db p) e -> p db e", p=128)[
                    :, :, chunk * 512:(chunk + 1) * 512]
                wdst = WP[p][:, :].rearrange("p (db e) -> p db e", db=DB)
                return (c.dma("q", f"wp{p}a", wdst[:, 0:4], wsrc[:, 0:4]),
                        c.dma("q", f"wp{p}b", wdst[:, 4:8], wsrc[:, 4:8]))

            tokw = {0: wp_dma1(0, 0), 1: wp_dma1(1, 1)}
            for obg in range(HB // 4):
                p = obg % 2
                c.wait("t", tokw[p][0], tokw[p][1], xq_done)
                for oi in range(4):
                    ob = obg * 4 + oi
                    psb = ps_m[ob % 2]
                    if pslast[ob % 2] is not None:
                        c.wait("t", pslast[ob % 2])
                    for db in range(DB):
                        mm = t_.matmul(psb[:, :],
                                       lhsT=WP[p][:, db * 512 + oi * 128: db * 512 + (oi + 1) * 128],
                                       rhs=XQ[:, db * FFT:(db + 1) * FFT],
                                       start=(db == 0), stop=(db == DB - 1))
                    mmt = c.inc("t", mm)
                    c.wait("v", mmt, qa_done)
                    mt = c.inc("v", v.tensor_tensor(out=HG[:, ob * FFT:(ob + 1) * FFT],
                                                    in0=psb[:, :],
                                                    in1=AREP[:, toff:toff + FFT],
                                                    op=ALU.mult))
                    c.wait("s", mt)
                    gg = c.inc("s", s.activation(out=HG[:, ob * FFT:(ob + 1) * FFT],
                                                 in_=HG[:, ob * FFT:(ob + 1) * FFT],
                                                 func=AF.Gelu, bias=FV[:, ob:ob + 1],
                                                 scale=1.0))
                    pslast[ob % 2] = gg
                wp_guard[p] = mmt
                if obg + 2 < HB // 4:
                    tokw[p] = wp_dma1(obg + 2, p)
            h_done = pslast[(HB - 1) % 2]
            def wp_dma2(chunk, p):
                c.wait("q", wp_guard[p])
                wsrc = w2q.rearrange("(hb p) e -> p hb e", p=128)[
                    :, :, chunk * 128:(chunk + 1) * 128]
                wdst = WP[p][:, :].rearrange("p (hb e) -> p hb e", hb=HB)
                return (c.dma("q", f"wp{p}a", wdst[:, 0:16], wsrc[:, 0:16]),
                        c.dma("q", f"wp{p}b", wdst[:, 16:32], wsrc[:, 16:32]))

            tokw = {0: wp_dma2(0, 0), 1: wp_dma2(1, 1)}
            c.wait("t", h_done, c.tok("v"))
            red_bank = [None, None]
            rr = 0
            for sl in range(FFT // 128):
                for grp in range(8):
                    pt = (ps_t, ps_t2)[rr % 2]
                    trs = None
                    for k in range(4):
                        hb_ = grp * 4 + k
                        if k == 0:
                            c.wait("t", red_bank[rr % 2])
                        trs = c.inc("t", t_.transpose(
                            pt[:, k * 128:(k + 1) * 128],
                            HG[:, hb_ * FFT + sl * 128: hb_ * FFT + (sl + 1) * 128],
                            IDF[:, :]))
                    c.wait("v", trs, h_done)
                    c.fence("v")
                    red_bank[rr % 2] = c.inc_fence("v", v.tensor_reduce(
                        out=MXG[:, sl * 8 + grp: sl * 8 + grp + 1],
                        in_=pt[:, :], axis=AXX, op=ALU.max,
                        apply_absolute_value=True))
                    rr += 1
            c.fence("v")
            c.inc_fence("v", v.tensor_reduce(
                out=MXC[:, 0:2], in_=MXG[:, 0:16].rearrange("p (a b) -> p a b", a=2),
                axis=AXX, op=ALU.max, apply_absolute_value=True))
            bq = emit_bridge(c, env, MXC, FFT // 128, ps_row, ROW, ps_b, FFT,
                             prev=c.tok("v"))
            c.wait("v", bq)
            c.inc("v", v.tensor_scalar_max(out=T1[:, 0:FFT], in0=ps_b[:, 0:FFT],
                                           scalar1=LNEPS))
            c.fence("v")
            c.inc_fence("v", v.reciprocal(out=T2[:, 0:FFT], in_=T1[:, 0:FFT]))
            c.inc("v", v.tensor_scalar_mul(out=QREP[:, toff:toff + FFT], in0=T2[:, 0:FFT],
                                           scalar1=127.0))
            c.inc("v", v.tensor_scalar(out=AREP[:, toff:toff + FFT], in0=T1[:, 0:FFT],
                                       scalar1=WS[:, 1:2], scalar2=1.0 / 127.0,
                                       op0=ALU.mult, op1=ALU.mult))
            for hg in range(HB // 8):
                hoff = hg * 8 * FFT
                c.inc("v", v.tensor_tensor(
                    out=seg(VQ[:, :], 8, FFT),
                    in0=HG[:, :].rearrange("p (b t) -> p b t", b=HB)[:, hg * 8:(hg + 1) * 8, :],
                    in1=fbc(QREP[:, toff:toff + FFT], 8, FFT), op=ALU.mult))
                c.inc("v", v.tensor_scalar_add(out=VQ[:, 0:W8], in0=VQ[:, 0:W8],
                                               scalar1=0.5))
                emit_floor_round(c, VQ, IQ, UQ, CQ, HQ[:, hoff:hoff + W8], W8)
            hq_done = c.tok("v")
            for dbo in range(DB):
                p = dbo % 2
                c.wait("t", tokw[p][0], tokw[p][1], hq_done)
                psb = ps_m[dbo % 2]
                if pslast[dbo % 2] is not None:
                    c.wait("t", pslast[dbo % 2])
                for hb_ in range(HB):
                    mm = t_.matmul(psb[:, :], lhsT=WP[p][:, hb_ * 128:(hb_ + 1) * 128],
                                   rhs=HQ[:, hb_ * FFT:(hb_ + 1) * FFT],
                                   start=(hb_ == 0), stop=(hb_ == HB - 1))
                mmt = c.inc("t", mm)
                wp_guard[p] = mmt
                if dbo + 2 < DB:
                    tokw[p] = wp_dma2(dbo + 2, p)
                c.wait("v", mmt)
                c.inc("v", v.tensor_tensor(out=VQ[:, 0:FFT], in0=psb[:, :],
                                           in1=AREP[:, toff:toff + FFT], op=ALU.mult))
                pslast[dbo % 2] = c.inc("v", v.scalar_tensor_tensor(
                    out=X[:, dbo * TC + toff: dbo * TC + toff + FFT],
                    in0=VQ[:, 0:FFT], scalar=FV[:, HB + dbo:HB + dbo + 1],
                    in1=X[:, dbo * TC + toff: dbo * TC + toff + FFT],
                    op0=ALU.add, op1=ALU.add))


def emit_mamba(c, env):
    nc = c.nc
    v, s, t_, g, q = (c.e[k] for k in Ctx.ENGS)
    X, CT, ONE, MV, ATL, HSTT, CCAR = (env[k] for k in
                                       ("X", "CT", "ONE", "MV", "ATL", "HSTT", "CCAR"))
    SEL = env["SEL"]
    IDE = env["IDE"]
    inwb, outwb, xpwb, dtwb = (env[k] for k in ("inwb", "outwb", "xpwb", "dtwb"))

    def xdb(db):
        return X[:, db * TC:(db + 1) * TC]

    def cw(l, dib, tap):
        return MV[l][:, dib * 4 + tap: dib * 4 + tap + 1]

    def cb(l, dib):
        return MV[l][:, 64 + dib:64 + dib + 1]

    def dtb(l, dob):
        return MV[l][:, 80 + dob:80 + dob + 1]

    def dsk(l, dib):
        return MV[l][:, 96 + dib:96 + dib + 1]

    def lnw(l, db):
        return MV[l][:, 112 + db:112 + db + 1]

    def lnb(l, db):
        return MV[l][:, 120 + db:120 + db + 1]

    def acol(l, dib, n):
        return ATL[l][:, dib * NST + n: dib * NST + n + 1]

    with ExitStack() as st:
        sbt = lambda nm, shp, dt=F32: st.enter_context(nc.sbuf_tensor(_un(nm), shp, dt))
        XX = sbt("mXX", [128, DB * TC])
        MU = sbt("mMU", [128, TC])
        RSTD = sbt("mRSTD", [128, TC])
        T1 = sbt("mT1", [128, TC])
        T2 = sbt("mT2", [128, TC])
        SZ = sbt("mSZ", [128, DIB * TC], BF16)
        YA = sbt("mYA", [128, DIB * TC])
        DTR = sbt("mDTR", [64, TC], BF16)
        BF = sbt("mBF", [16, TC], BF16)
        CF = sbt("mCF", [16, TC], BF16)
        ps_m = [st.enter_context(nc.psum_tensor(_un(f"mps_m{i}"), [128, TC], F32))
                for i in range(2)]

        with ExitStack() as st2:
            XB2 = st2.enter_context(nc.sbuf_tensor(_un("mXB2"), [128, DB * TC], BF16))
            SQ2 = st2.enter_context(nc.sbuf_tensor(_un("mSQ2"), [128, DB * TC], BF16))
            ps_s = st2.enter_context(nc.psum_tensor(_un("mps_s"), [128, TC], F32))
            ps_q = st2.enter_context(nc.psum_tensor(_un("mps_q"), [128, TC], F32))
            emit_ln_stats(c, env, xdb, DB, MU, RSTD, XB2, SQ2, ps_s, ps_q, T1, T2,
                          1.0 / D, src_big=X[:, :])
            XXs = seg(XX[:, :], DB, TC)
            c.inc("v", v.tensor_tensor(out=XXs, in0=seg(X[:, :], DB, TC),
                                       in1=fbc(MU[:, :], DB, TC), op=ALU.subtract))
            c.inc("v", v.tensor_tensor(out=XXs, in0=XXs,
                                       in1=fbc(RSTD[:, :], DB, TC), op=ALU.mult))
            c.barrier()

        pslast = [None, None]
        for l in range(2):
          with ExitStack() as stL:
            XXb = stL.enter_context(nc.sbuf_tensor(_un("mXXb"), [128, DB * TC], BF16))
            WINZ = stL.enter_context(nc.sbuf_tensor(_un("mWINZ"), [128, DB * 512], BF16))
            UB = stL.enter_context(nc.sbuf_tensor(_un("mUB"), [128, DIB * TC], BF16))
            HTA = stL.enter_context(nc.sbuf_tensor(_un("mHTA"), [128, 8 * TC], BF16))
            WXP = stL.enter_context(nc.sbuf_tensor(_un("mWXP"), [128, DIB * 128], BF16))
            WDT = stL.enter_context(nc.sbuf_tensor(_un("mWDT"), [64, DI], BF16))
            wxp_tok = c.dma("q", "wxp", WXP[:, :].rearrange("p (dib e) -> p dib e", dib=DIB),
                            xpwb[l].rearrange("(dib p) e -> p dib e", p=128))
            wdt_tok = c.dma("q", "wdt", WDT[:, :], dtwb[l][:, :])
            with ExitStack() as st2:
                WIN = st2.enter_context(nc.sbuf_tensor(_un("mWIN"), [128, DB * 512], BF16))
                GT1 = st2.enter_context(nc.sbuf_tensor(_un("mGT1"), [128, TC], BF16))
                XC = st2.enter_context(nc.sbuf_tensor(_un("mXCc"), [128, DIB * (TC + 3)], BF16))
                cv_tok = {}
                WI2 = [WIN, WINZ]

                def wi_dma(chunk, p):
                    c.wait("q", wi_guard[p])
                    wsrc = inwb[l].rearrange("(db p) e -> p db e", p=128)[
                        :, :, chunk * 512:(chunk + 1) * 512]
                    wdst = WI2[p][:, :].rearrange("p (db e) -> p db e", db=DB)
                    return (c.dma("q", f"wi{p}a", wdst[:, 0:4], wsrc[:, 0:4]),
                            c.dma("q", f"wi{p}b", wdst[:, 4:8], wsrc[:, 4:8]))

                wi_guard = [None, None]
                tokw = {0: wi_dma(0, 0), 1: wi_dma(1, 1)}
                xxc = None
                for db in range(DB):
                    xxc = c.inc("s", s.copy(out=XXb[:, db * TC:(db + 1) * TC],
                                            in_=XX[:, db * TC:(db + 1) * TC]))
                for half in (0,):
                    for og in range(DI // 512):
                        p = og % 2
                        c.wait("t", tokw[p][0], tokw[p][1], xxc)
                        for oi in range(4):
                            dib = og * 4 + oi
                            psb = ps_m[dib % 2]
                            if pslast[dib % 2] is not None:
                                c.wait("t", pslast[dib % 2])
                            for db in range(DB):
                                mm = t_.matmul(
                                    psb[:, :],
                                    lhsT=WI2[p][:, db * 512 + oi * 128: db * 512 + (oi + 1) * 128],
                                    rhs=XXb[:, db * TC:(db + 1) * TC],
                                    start=(db == 0), stop=(db == DB - 1))
                            mmt = c.inc("t", mm)
                            wi_guard[p] = mmt
                            if oi == 3 and og + 2 < DI // 512:
                                tokw[p] = wi_dma(og + 2, p)
                            c.wait("s", mmt)
                            if half == 0:
                                base = dib * (TC + 3)
                                c.inc("s", s.copy(out=XC[:, base:base + 3],
                                                  in_=CCAR[l][:, dib * 3:(dib + 1) * 3]))
                                c.inc("s", s.copy(out=XC[:, base + 3:base + 3 + TC],
                                                  in_=psb[:, :]))
                                ev = c.inc("s", s.copy(out=CCAR[l][:, dib * 3:(dib + 1) * 3],
                                                       in_=psb[:, TC - 3:TC]))
                                ub_s = UB[:, dib * TC:(dib + 1) * TC]
                                c.wait("v", ev)
                                c.inc("v", v.tensor_scalar_mul(
                                    out=ub_s, in0=XC[:, base:base + TC],
                                    scalar1=cw(l, dib, 0)))
                                for tap in range(1, 4):
                                    c.inc("v", v.scalar_tensor_tensor(
                                        out=ub_s,
                                        in0=XC[:, base + tap:base + tap + TC],
                                        scalar=cw(l, dib, tap), in1=ub_s,
                                        op0=ALU.mult, op1=ALU.add))
                                cv_tok[dib] = c.tok("v")
                            else:
                                ev = c.inc("s", s.activation(
                                    out=SZ[:, dib * TC:(dib + 1) * TC],
                                    in_=psb[:, :], func=AF.Silu))
                            pslast[dib % 2] = ev
                        del ev
                for dib in range(DIB):
                    c.wait("s", cv_tok[dib])
                    c.inc("s", s.activation(out=UB[:, dib * TC:(dib + 1) * TC],
                                            in_=UB[:, dib * TC:(dib + 1) * TC],
                                            func=AF.Silu, bias=cb(l, dib), scale=1.0))
                u_done = c.tok("s")
                z_winz_guard = wi_guard[1]
            with ExitStack() as st2:
                ps_xd = st2.enter_context(nc.psum_tensor(_un("mps_xd"), [64, TC], F32))
                ps_xb = st2.enter_context(nc.psum_tensor(_un("mps_xb"), [16, TC], F32))
                ps_xc = st2.enter_context(nc.psum_tensor(_un("mps_xc"), [16, TC], F32))
                c.wait("t", wxp_tok, u_done)
                for dib in range(DIB):
                    st_f = (dib == 0)
                    sp_f = (dib == DIB - 1)
                    t_.matmul(ps_xd[:, :], lhsT=WXP[:, dib * 128: dib * 128 + 64],
                              rhs=UB[:, dib * TC:(dib + 1) * TC], start=st_f, stop=sp_f)
                    t_.matmul(ps_xb[:, :], lhsT=WXP[:, dib * 128 + 64: dib * 128 + 80],
                              rhs=UB[:, dib * TC:(dib + 1) * TC], start=st_f, stop=sp_f)
                    mm = t_.matmul(ps_xc[:, :], lhsT=WXP[:, dib * 128 + 80: dib * 128 + 96],
                                   rhs=UB[:, dib * TC:(dib + 1) * TC], start=st_f, stop=sp_f)
                xp_done = c.inc("t", mm)
                c.wait("s", xp_done)
                c.inc("s", s.copy(out=DTR[:, :], in_=ps_xd[:, :]))
                c.inc("s", s.copy(out=BF[:, :], in_=ps_xb[:, :]))
                ev = c.inc("s", s.copy(out=CF[:, :], in_=ps_xc[:, :]))
                c.wait("t", ev)
            stDT = ExitStack()
            DT = stDT.enter_context(nc.sbuf_tensor(_un("mDT"), [128, DIB * TC], BF16))
            WTl = stDT.enter_context(nc.sbuf_tensor(_un("mWT"), [128, DIB * TC], BF16))
            AEX = stDT.enter_context(nc.sbuf_tensor(_un("mAEX"), [128, 4 * TC], F32))
            c.wait("t", wdt_tok)
            for dob in range(DIB):
                psb = ps_m[dob % 2]
                if pslast[dob % 2] is not None:
                    c.wait("t", pslast[dob % 2])
                mm = t_.matmul(psb[:, :], lhsT=WDT[:, dob * 128:(dob + 1) * 128],
                               rhs=DTR[:, :], start=True, stop=True)
                mmt = c.inc("t", mm)
                c.wait("s", mmt)
                c.inc("s", s.activation(out=T1[:, :], in_=psb[:, :], func=AF.Identity,
                                        bias=dtb(l, dob), scale=1.0))
                c.inc("s", s.activation(out=T2[:, :], in_=T1[:, :], func=AF.Abs))
                c.inc("s", s.activation(out=T2[:, :], in_=T2[:, :], func=AF.Exp, scale=-1.0))
                c.inc("s", s.activation(out=T2[:, :], in_=T2[:, :], func=AF.Ln, bias=1.0))
                rl = c.inc("s", s.activation(out=T1[:, :], in_=T1[:, :], func=AF.Relu))
                c.wait("v", rl)
                dtv = c.inc("v", v.tensor_tensor(out=DT[:, dob * TC:(dob + 1) * TC],
                                                 in0=T1[:, :], in1=T2[:, :], op=ALU.add))
                pslast[dob % 2] = dtv
                c.wait("s", dtv)
            c.inc("v", v.tensor_tensor(out=seg(WTl[:, :], DIB, TC),
                                       in0=seg(DT[:, :], DIB, TC),
                                       in1=seg(UB[:, :], DIB, TC), op=ALU.mult))
            c.inc("v", v.tensor_tensor(out=seg(YA[:, :], DIB, TC),
                                       in0=seg(UB[:, :], DIB, TC),
                                       in1=sbc(MV[l][:, 96:112], DIB, TC), op=ALU.mult))
            prep_done = c.tok("v")
            with ExitStack() as st2:
                ps_y = st2.enter_context(nc.psum_tensor(_un("mps_y"), [128, 4 * TC], F32))
                ps_bc2 = st2.enter_context(nc.psum_tensor(_un("mps_bc2"), [128, 2 * TC], F32))
                BCREP = st2.enter_context(nc.sbuf_tensor(_un("mBCREP"), [128, 6 * TC], BF16))
                WB = UB
                aex_used = [None, None, None, None]
                z_done = [None]

                def z_group(og):
                    c.wait("q", z_winz_guard)
                    wtok = c.dma("q", "winz",
                                 WINZ[:, :].rearrange("p (db e) -> p db e", db=DB),
                                 inwb[l].rearrange("(db p) e -> p db e", p=128)
                                 [:, :, DI + og * 512: DI + (og + 1) * 512])
                    c.wait("t", wtok)
                    for oi in range(4):
                        dib = og * 4 + oi
                        psb = ps_m[dib % 2]
                        if pslast[dib % 2] is not None:
                            c.wait("t", pslast[dib % 2])
                        for db in range(DB):
                            mm = t_.matmul(
                                psb[:, :],
                                lhsT=WINZ[:, db * 512 + oi * 128: db * 512 + (oi + 1) * 128],
                                rhs=XXb[:, db * TC:(db + 1) * TC],
                                start=(db == 0), stop=(db == DB - 1))
                        mmt = c.inc("t", mm)
                        c.wait("s", mmt)
                        pslast[dib % 2] = c.inc("s", s.activation(
                            out=SZ[:, dib * TC:(dib + 1) * TC],
                            in_=psb[:, :], func=AF.Silu))
                        z_done[0] = pslast[dib % 2]
                    c.wait("q", c.tok("t"))

                Q4 = 4 * TC
                reps_tok = [None, None, None]
                bc_guard = [None, None, None]
                prev_mq = [None, None]
                prev_yacc = [None, None]
                evac_last = [None]

                def issue_bc(k):
                    col, sl = k % NST, k % 3
                    c.wait("t", evac_last[0])
                    t_.matmul(ps_bc2[:, 0:TC], lhsT=SEL[:, col * 128:(col + 1) * 128],
                              rhs=BF[:, :], start=True, stop=True)
                    mmt = c.inc("t", t_.matmul(
                        ps_bc2[:, TC:2 * TC], lhsT=SEL[:, col * 128:(col + 1) * 128],
                        rhs=CF[:, :], start=True, stop=True))
                    c.wait("s", mmt, bc_guard[sl])
                    ev = c.inc("s", s.copy(out=BCREP[:, sl * 2 * TC:(sl + 1) * 2 * TC],
                                           in_=ps_bc2[:, :]))
                    reps_tok[sl] = ev
                    evac_last[0] = ev

                for g4 in range(4):
                    goff = g4 * 4
                    for n in range(NST):
                        pq = n % 2
                        kk = g4 * NST + n
                        sl = kk % 3
                        if kk == 0:
                            issue_bc(0)
                            issue_bc(1)
                        Bq = BCREP[:, sl * 2 * TC: sl * 2 * TC + TC]
                        Cq = BCREP[:, sl * 2 * TC + TC: sl * 2 * TC + 2 * TC]
                        wbq = WB[:, (2 * pq) * Q4:(2 * pq) * Q4 + Q4]
                        mq = WB[:, (2 * pq + 1) * Q4:(2 * pq + 1) * Q4 + Q4]
                        c.wait("v", reps_tok[sl], prev_mq[pq], prev_yacc[pq])
                        c.inc("v", v.tensor_tensor(
                            out=seg(wbq, 4, TC),
                            in0=WTl[:, :].rearrange("p (b w) -> p b w", b=DIB)
                            [:, goff:goff + 4, :],
                            in1=fbc(Bq, 4, TC), op=ALU.mult))
                        for di in range(4):
                            dib = goff + di
                            c.wait("s", aex_used[di])
                            atok = c.inc("s", s.activation(
                                out=AEX[:, di * TC:(di + 1) * TC],
                                in_=DT[:, dib * TC:(dib + 1) * TC],
                                func=AF.Exp, scale=acol(l, dib, n)))
                            c.wait("v", atok)
                            c.inc("v", v.tensor_tensor_scan(
                                out=HTA[:, di * TC:(di + 1) * TC],
                                data0=AEX[:, di * TC:(di + 1) * TC],
                                data1=wbq[:, di * TC:(di + 1) * TC],
                                initial=HSTT[l][:, dib * NST + n: dib * NST + n + 1],
                                op0=ALU.mult, op1=ALU.add))
                            aex_used[di] = c.tok("v")
                        if kk + 2 < 4 * NST:
                            issue_bc(kk + 2)
                        c.inc("v", v.tensor_copy(
                            out=HSTT[l][:, :].rearrange("p (d s) -> p d s", s=NST)
                            [:, goff:goff + 4, n:n + 1],
                            in_=seg(HTA[:, 0:Q4], 4, TC)[:, :, TC - 1:TC]))
                        mvq = c.inc("v", v.tensor_tensor(
                            out=seg(mq, 4, TC), in0=seg(HTA[:, 0:Q4], 4, TC),
                            in1=fbc(Cq, 4, TC), op=ALU.mult))
                        prev_mq[pq] = mvq
                        bc_guard[sl] = mvq
                        c.wait("t", mvq)
                        for di in range(4):
                            mm = t_.matmul(ps_y[:, di * TC:(di + 1) * TC],
                                           lhsT=IDE[:, :],
                                           rhs=mq[:, di * TC:(di + 1) * TC],
                                           start=(n == 0), stop=(n == NST - 1))
                        yacc = c.inc("t", mm)
                        prev_yacc[pq] = yacc
                        if n == NST - 1:
                            c.wait("v", yacc)
                            c.inc("v", v.tensor_tensor(
                                out=YA[:, goff * TC:goff * TC + Q4],
                                in0=YA[:, goff * TC:goff * TC + Q4],
                                in1=ps_y[:, :], op=ALU.add))
                            prev_mq = [c.tok("v"), c.tok("v")]
                        if n == 8:
                            z_group(g4)
                    c.wait("t", c.tok("v"))
            scan_done = c.tok("v")
            stDT.close()
            with ExitStack() as st2:
                YB = st2.enter_context(nc.sbuf_tensor(_un("mYB"), [128, DIB * TC], BF16))
                HH = st2.enter_context(nc.sbuf_tensor(_un("mHH"), [128, DB * TC], F32))
                WO2 = [st2.enter_context(nc.sbuf_tensor(_un(f"mWOU{i}"), [128, DIB * 128], BF16))
                       for i in range(2)]

                def wo_dma(chunk, p):
                    c.wait("q", wo_guard[p])
                    wsrc = outwb[l].rearrange("(dib p) e -> p dib e", p=128)[
                        :, :, chunk * 128:(chunk + 1) * 128]
                    wdst = WO2[p][:, :].rearrange("p (dib e) -> p dib e", dib=DIB)
                    return (c.dma("q", f"wo{p}a", wdst[:, 0:8], wsrc[:, 0:8]),
                            c.dma("q", f"wo{p}b", wdst[:, 8:16], wsrc[:, 8:16]))

                wo_guard = [scan_done, scan_done]
                tokw = {0: wo_dma(0, 0), 1: wo_dma(1, 1)}
                c.wait("v", z_done[0])
                c.inc("v", v.tensor_tensor(out=seg(YB[:, :], DIB, TC),
                                           in0=seg(YA[:, :], DIB, TC),
                                           in1=seg(SZ[:, :], DIB, TC), op=ALU.mult))
                yb_done = c.tok("v")
                evs = []
                for dbo in range(DB):
                    p = dbo % 2
                    c.wait("t", tokw[p][0], tokw[p][1], yb_done)
                    psb = ps_m[dbo % 2]
                    if pslast[dbo % 2] is not None:
                        c.wait("t", pslast[dbo % 2])
                    for dib in range(DIB):
                        mm = t_.matmul(psb[:, :], lhsT=WO2[p][:, dib * 128:(dib + 1) * 128],
                                       rhs=YB[:, dib * TC:(dib + 1) * TC],
                                       start=(dib == 0), stop=(dib == DIB - 1))
                    mmt = c.inc("t", mm)
                    wo_guard[p] = mmt
                    if dbo + 2 < DB:
                        tokw[p] = wo_dma(dbo + 2, p)
                    c.wait("s", mmt)
                    ev = c.inc("s", s.copy(out=HH[:, dbo * TC:(dbo + 1) * TC],
                                           in_=psb[:, :]))
                    evs.append(ev)
                    pslast[dbo % 2] = ev
                with ExitStack() as st3:
                    XB2 = st3.enter_context(nc.sbuf_tensor(_un("mXB2b"), [128, DB * TC], BF16))
                    SQ2 = st3.enter_context(nc.sbuf_tensor(_un("mSQ2b"), [128, DB * TC], BF16))
                    ps_s = st3.enter_context(nc.psum_tensor(_un("mps_sb"), [128, TC], F32))
                    ps_q = st3.enter_context(nc.psum_tensor(_un("mps_qb"), [128, TC], F32))
                    c.wait("s", evs[-1])
                    emit_ln_stats(c, env, lambda i: HH[:, i * TC:(i + 1) * TC], DB,
                                  MU, RSTD, XB2, SQ2, ps_s, ps_q, T1, T2, 1.0 / D,
                                  src_big=HH[:, :])
                    HHs = seg(HH[:, :], DB, TC)
                    c.inc("v", v.tensor_tensor(out=HHs, in0=HHs,
                                               in1=fbc(MU[:, :], DB, TC),
                                               op=ALU.subtract))
                    c.inc("v", v.tensor_tensor(out=HHs, in0=HHs,
                                               in1=fbc(RSTD[:, :], DB, TC), op=ALU.mult))
                    c.inc("v", v.tensor_tensor(out=HHs, in0=HHs,
                                               in1=sbc(MV[l][:, 112:120], DB, TC),
                                               op=ALU.mult))
                    c.inc("v", v.tensor_tensor(out=HHs, in0=HHs,
                                               in1=sbc(MV[l][:, 120:128], DB, TC),
                                               op=ALU.add))
                    c.inc("v", v.tensor_tensor(out=seg(XX[:, :], DB, TC),
                                               in0=seg(XX[:, :], DB, TC),
                                               in1=HHs, op=ALU.add))
                c.barrier()
        c.inc("v", v.tensor_tensor(out=seg(X[:, :], DB, TC), in0=seg(X[:, :], DB, TC),
                                   in1=seg(XX[:, :], DB, TC), op=ALU.add))
def host_prep(inputs):
    f32 = np.float32
    asc = np.ascontiguousarray
    consts = np.tile(np.array([0.0, 1.0, LNEPS, 0.5, -0.5, -1.0, C30, 127.0], f32),
                     (128, 1))

    def vec8(x):
        return asc(np.asarray(x, f32).reshape(8, 128).T)

    def vec16(x):
        return asc(np.asarray(x, f32).reshape(16, 128).T)

    avec = np.concatenate([vec8(inputs["L0_attn_a_time_mix_k"]),
                           vec8(inputs["L0_attn_a_time_mix_v"]),
                           vec8(inputs["L0_attn_a_time_mix_r"]),
                           vec8(inputs["L0_attn_a_time_decay"]),
                           vec8(inputs["L0_attn_a_time_first"])], axis=1)
    fvec = np.concatenate([asc(np.asarray(inputs["L0_ffn_b1"], f32).reshape(32, 128).T),
                           vec8(inputs["L0_ffn_b2"])], axis=1)
    shared = {
        "consts": consts,
        "identb": np.eye(128, dtype=ml_dtypes.bfloat16),
        "identf": np.eye(128, dtype=f32),
        "onesb": np.ones((128, 128), dtype=ml_dtypes.bfloat16),
        "onesf": np.ones((1, 512), dtype=f32),
        "selb": np.kron(np.eye(16), np.ones((1, 128))).astype(ml_dtypes.bfloat16),
        "wk": asc(np.asarray(inputs["L0_attn_a_W_k"], f32).T),
        "wv": asc(np.asarray(inputs["L0_attn_a_W_v"], f32).T),
        "wr": asc(np.asarray(inputs["L0_attn_a_W_r"], f32).T),
        "wo": asc(np.asarray(inputs["L0_attn_a_W_o"], f32).T),
        "avec": avec,
        "w1t": asc(np.asarray(inputs["L0_ffn_W1"], f32).T),
        "w2t": asc(np.asarray(inputs["L0_ffn_W2"], f32).T),
        "fvec": fvec,
    }
    for l in range(2):
        shared[f"inwt{l}"] = asc(np.asarray(inputs["L0_attn_b_in_proj_w"][l], f32).T)
        shared[f"outwt{l}"] = asc(np.asarray(inputs["L0_attn_b_out_proj_w"][l], f32).T)
        xp = np.zeros((DI, 128), f32)
        xp[:, 0:96] = np.asarray(inputs["L0_attn_b_x_proj_w"][l], f32).T
        shared[f"xpwt{l}"] = asc(xp)
        shared[f"dtwt{l}"] = asc(np.asarray(inputs["L0_attn_b_dt_proj_w"][l], f32).T)
        cwl = asc(np.asarray(inputs["L0_attn_b_conv_w"][l], f32)
                  .reshape(16, 128, 4).transpose(1, 0, 2).reshape(128, 64))
        mv = np.concatenate([
            cwl,
            vec16(inputs["L0_attn_b_conv_b"][l]),
            vec16(inputs["L0_attn_b_dt_proj_b"][l]),
            vec16(inputs["L0_attn_b_Dskip"][l]),
            vec8(inputs["L0_attn_b_ln_w"][l]),
            vec8(inputs["L0_attn_b_ln_b"][l]),
        ], axis=1)
        shared[f"mvec{l}"] = asc(mv)
        al = np.asarray(inputs["L0_attn_b_A_log"][l], f32)
        shared[f"alog{l}"] = asc(al.reshape(16, 128, 16).transpose(1, 0, 2).reshape(128, 256))
    x = np.asarray(inputs["L0_attn_a_x"], f32)
    in_maps = []
    for core in range(8):
        m = dict(shared)
        m["xin"] = asc(x[core * BC:(core + 1) * BC].transpose(0, 2, 1))
        in_maps.append(m)
    return in_maps


_NC_CACHE = {}


def kernel(**inputs):
    in_maps = host_prep(inputs)
    if "prog" not in _NC_CACHE:
        _NC_CACHE["prog"] = build_program()
    nc = _NC_CACHE["prog"]
    res = run_bass_kernel_spmd(nc, in_maps, core_ids=list(range(8)))
    outs = [np.asarray(res.results[core]["xout"]).transpose(0, 2, 1) for core in range(8)]
    return np.ascontiguousarray(np.concatenate(outs, axis=0).astype(np.float32))


# version 15
# speedup vs baseline: 1.1794x; 1.0740x over previous
import math
import os
import sys
from contextlib import ExitStack

if "/opt/trn_rl_repo" not in sys.path:
    sys.path.insert(0, "/opt/trn_rl_repo")

import numpy as np
import ml_dtypes

import concourse.bass as bass
import concourse.mybir as mybir
from concourse.bass_utils import run_bass_kernel_spmd

F32 = mybir.dt.float32
BF16 = mybir.dt.bfloat16
I32 = mybir.dt.int32
AF = mybir.ActivationFunctionType
ALU = mybir.AluOpType
AXX = mybir.AxisListType.X

D = 1024
T = 2048
BC = 2
DB = D // 128
TC = 512
NCH = T // TC
DI = 2048
DIB = DI // 128
NST = 16
RK = 64
HF = 4096
HB = HF // 128
FFT = 256
C30 = math.exp(-30.0)
LNEPS = 1e-5
WKVEPS = 1e-9

STAGES = int(os.environ.get("KERN_STAGES", "15"))
_UN = [0]


def _un(nm):
    _UN[0] += 1
    return f"{nm}_{_UN[0]}"



def seg(ap, nblk, width, lead=0):
    r = ap.rearrange("p (b w) -> p b w", b=nblk)
    return r[:, :, lead:lead + width] if lead else r


def fbc(ap, nblk, width):
    return ap.rearrange("p w -> p () w").broadcast_to([128, nblk, width])


def sbc(ap, nblk, width):
    return ap.rearrange("p b -> p b ()").broadcast_to([128, nblk, width])


class _FakeIns:
    __slots__ = ("incs", "post_fence")

    def __init__(self):
        self.incs = []
        self.post_fence = False

    def then_inc(self, sem, val):
        self.incs.append((sem, val))
        return self


class _Rec:

    def __init__(self, name, ctx):
        self._name = name
        self._ctx = ctx

    OPTOUT = ("X", "mXX", "mXXb", "mUB", "mDT", "mWT", "mYA", "mSZ", "mXCc",
              "mHH", "mYB", "fT0", "fHG", "fHQ", "fXQ", "aK", "aV", "aR",
              "aXKb", "aXVb", "aXRb", "aY", "aXN", "HST0", "HST1", "CCAR0",
              "CCAR1", "XPC", "PQS")

    @staticmethod
    def _base(nm):
        parts = nm.rsplit("_", 1)
        return parts[0] if len(parts) == 2 and parts[1].isdigit() else nm

    def __getattr__(self, meth):
        nm, ctx = self._name, self._ctx

        def call(*a, **kw):
            if meth in ("wait_ge", "nop", "dma_start", "engine_nop"):
                if meth == "dma_start":
                    pass
                fi = _FakeIns()
                ctx.log.append((nm, meth, a, kw, fi))
                return fi
            outs, ins_ = set(), set()
            for k, v_ in kw.items():
                t = getattr(v_, "tensor", None)
                if t is None or _Rec._base(t.name) in _Rec.OPTOUT:
                    continue
                (outs if k in ("out", "accum_out") else ins_).add(t.name)
            for v_ in a:
                t = getattr(v_, "tensor", None)
                if t is not None and _Rec._base(t.name) not in _Rec.OPTOUT:
                    ins_.add(t.name)
            if ctx.relax and meth in ("tensor_tensor_scan", "tensor_tensor",
                                      "tensor_copy"):
                fi = _FakeIns()
                ctx.log.append((nm, meth, a, kw, fi))
                return fi
            special = meth in ("tensor_reduce", "reduce_max", "reduce_sum",
                               "reciprocal")
            if not special:
                cmp_ops = (ALU.is_gt, ALU.is_ge, ALU.is_lt, ALU.is_le,
                           ALU.is_equal, ALU.not_equal)
                for opk in ("op", "op0", "op1"):
                    if kw.get(opk) in cmp_ops:
                        special = True
                        break
            if not special:
                odt = getattr(kw.get("out"), "dtype", None)
                if odt is not None:
                    for k, v_ in kw.items():
                        if k in ("out", "accum_out"):
                            continue
                        dt_ = getattr(v_, "dtype", None)
                        if dt_ is not None and dt_ != odt:
                            special = True
                            break
            if not special:
                for v_ in list(kw.values()) + list(a):
                    apl = getattr(v_, "ap", None)
                    if apl is not None and len(apl) >= 3:
                        special = True
                        break
            pw, pr = ctx.pend_w[nm], ctx.pend_r[nm]
            if nm in ("v", "s", "g") and (
                    special or (ins_ & pw) or (outs & pw) or (outs & pr)):
                ctx.fence(nm)
            fi = _FakeIns()
            fi.post_fence = special and nm in ("v", "s", "g")
            ctx.log.append((nm, meth, a, kw, fi))
            ctx.pend_w[nm] |= outs
            ctx.pend_r[nm] |= ins_
            return fi
        return call


class Ctx:
    ENGS = ("v", "s", "t", "g", "q")

    def __init__(self, nc, block, stack):
        self.nc = nc
        self.block = block
        self.stack = stack
        self.log = []
        self.e = {k: _Rec(k, self) for k in self.ENGS}
        self.sem = {k: stack.enter_context(nc.semaphore(f"sem_{k}")) for k in self.ENGS}
        self.dsem = {}
        self.cnt = {k: 0 for k in self.ENGS}
        self.dcnt = {}
        self.lastw = {k: dict() for k in self.ENGS}
        self.pend_w = {k: set() for k in self.ENGS}
        self.pend_r = {k: set() for k in self.ENGS}
        self.fcnt = {k: -1 for k in self.ENGS}

    def finalize(self):
        def replay(eng, name):
            for (nm, meth, a, kw, fi) in self.log:
                if nm != name:
                    continue
                ins = getattr(eng, meth)(*a, **kw)
                for sem, val in fi.incs:
                    ins.then_inc(sem, val)

        block = self.block
        @block.vector
        def _(e):
            replay(e, "v")
        @block.scalar
        def _(e):
            replay(e, "s")
        @block.tensor
        def _(e):
            replay(e, "t")
        @block.gpsimd
        def _(e):
            replay(e, "g")
        @block.sync
        def _(e):
            replay(e, "q")

    def inc(self, e, ins):
        ins.then_inc(self.sem[e], 1)
        self.cnt[e] += 1
        tok = (e, self.cnt[e])
        if getattr(ins, "post_fence", False):
            self.fence(e)
        return tok

    def tok(self, e):
        return (e, self.cnt[e])

    def fence(self, e):
        self.pend_w[e].clear()
        self.pend_r[e].clear()
        if self.fcnt[e] == self.cnt[e]:
            return
        self.e[e].wait_ge(self.sem[e], self.cnt[e])
        self.fcnt[e] = self.cnt[e]

    def inc_fence(self, e, ins):
        tk = self.inc(e, ins)
        self.e[e].wait_ge(self.sem[e], self.cnt[e])
        return tk

    relax = False

    def selfwait(self, e, tok):
        if tok is None:
            return
        pe, val = tok
        assert pe == e
        if self.fcnt[e] >= val:
            return
        self.e[e].wait_ge(self.sem[e], val)
        self.fcnt[e] = val

    def dma(self, qe, stream, out, in_, **kw):
        if stream not in self.dsem:
            self.dsem[stream] = self.stack.enter_context(
                self.nc.semaphore(f"ds_{stream}"))
            self.dcnt[stream] = 0
        ins = self.e[qe].dma_start(out=out, in_=in_, **kw)
        ins.then_inc(self.dsem[stream], 16)
        self.dcnt[stream] += 16
        return ("D:" + stream, self.dcnt[stream])

    def wait(self, e, *tokens):
        for tk in tokens:
            if tk is None:
                continue
            pe, val = tk
            if pe == e:
                continue
            if pe.startswith("D:"):
                val = self.dcnt[pe[2:]]
            if self.lastw[e].get(pe, 0) >= val:
                continue
            sem = self.dsem[pe[2:]] if pe.startswith("D:") else self.sem[pe]
            self.e[e].wait_ge(sem, val)
            self.lastw[e][pe] = val

    def barrier(self, engines="vstgq"):
        toks = [(e, self.cnt[e]) for e in engines]
        dt_ = [("D:" + k, self.dcnt[k]) for k in self.dcnt]
        for e in engines:
            self.fence(e)
            self.wait(e, *toks, *dt_)


def build_program():
    nc = bass.Bass("TRN2", target_bir_lowering=False)

    xin = nc.dram_tensor("xin", [BC, D, T], F32, kind="ExternalInput").ap()
    consts = nc.dram_tensor("consts", [128, 8], F32, kind="ExternalInput").ap()
    identb = nc.dram_tensor("identb", [128, 128], BF16, kind="ExternalInput").ap()
    identf = nc.dram_tensor("identf", [128, 128], F32, kind="ExternalInput").ap()
    onesb = nc.dram_tensor("onesb", [128, 128], BF16, kind="ExternalInput").ap()
    onesf = nc.dram_tensor("onesf", [1, 512], F32, kind="ExternalInput").ap()
    selb = nc.dram_tensor("selb", [16, 16 * 128], BF16, kind="ExternalInput").ap()
    wsrc = {}
    for nm in ("wk", "wv", "wr", "wo"):
        wsrc[nm] = nc.dram_tensor(nm, [D, D], F32, kind="ExternalInput").ap()
    avec = nc.dram_tensor("avec", [128, 8 * 5], F32, kind="ExternalInput").ap()
    w1t = nc.dram_tensor("w1t", [D, HF], F32, kind="ExternalInput").ap()
    w2t = nc.dram_tensor("w2t", [HF, D], F32, kind="ExternalInput").ap()
    fvec = nc.dram_tensor("fvec", [128, HB + DB], F32, kind="ExternalInput").ap()
    inwt, outwt, xpwt, dtwt, mvec, alog = [], [], [], [], [], []
    for l in range(2):
        inwt.append(nc.dram_tensor(f"inwt{l}", [D, 2 * DI], F32, kind="ExternalInput").ap())
        outwt.append(nc.dram_tensor(f"outwt{l}", [DI, D], F32, kind="ExternalInput").ap())
        xpwt.append(nc.dram_tensor(f"xpwt{l}", [DI, 128], F32, kind="ExternalInput").ap())
        dtwt.append(nc.dram_tensor(f"dtwt{l}", [RK, DI], F32, kind="ExternalInput").ap())
        mvec.append(nc.dram_tensor(f"mvec{l}", [128, 128], F32, kind="ExternalInput").ap())
        alog.append(nc.dram_tensor(f"alog{l}", [128, DIB * NST], F32, kind="ExternalInput").ap())
    wbf = {nm: nc.dram_tensor(nm + "b", [D, D], BF16, kind="Internal").ap()
           for nm in ("wk", "wv", "wr", "wo")}
    w1q = nc.dram_tensor("w1q", [D, HF], BF16, kind="Internal").ap()
    w2q = nc.dram_tensor("w2q", [HF, D], BF16, kind="Internal").ap()
    inwb = [nc.dram_tensor(f"inwb{l}", [D, 2 * DI], BF16, kind="Internal").ap() for l in range(2)]
    outwb = [nc.dram_tensor(f"outwb{l}", [DI, D], BF16, kind="Internal").ap() for l in range(2)]
    xpwb = [nc.dram_tensor(f"xpwb{l}", [DI, 128], BF16, kind="Internal").ap() for l in range(2)]
    dtwb = [nc.dram_tensor(f"dtwb{l}", [RK, DI], BF16, kind="Internal").ap() for l in range(2)]
    xout = nc.dram_tensor("xout", [BC, D, T], F32, kind="ExternalOutput").ap()

    stack = ExitStack()
    with stack:
        block = stack.enter_context(nc.Block())
        c = Ctx(nc, block, stack)
        v, s, t_, g, q = (c.e[k] for k in Ctx.ENGS)
        sb = lambda nm, shp, dt=F32: stack.enter_context(nc.sbuf_tensor(nm, shp, dt))

        CT = sb("CT", [128, 8])
        for _j, _cv in enumerate([0.0, 1.0, LNEPS, 0.5, -0.5, -1.0, C30, 127.0]):
            nc.const_aps.register(F32, _cv, CT[:, _j:_j + 1])
        IDE = sb("IDE", [128, 128], BF16)
        IDF = sb("IDF", [128, 128])
        ONE = sb("ONE", [128, 128], BF16)
        ONF = sb("ONF", [1, 512])
        SEL = sb("SEL", [16, 16 * 128], BF16)
        AV = sb("AV", [128, 8 * 5])
        OMV = sb("OMV", [128, 8 * 3])
        FV = sb("FV", [128, HB + DB])
        MV = [sb(f"MV{l}", [128, 128]) for l in range(2)]
        ATL = [sb(f"ATL{l}", [128, DIB * NST]) for l in range(2)]
        WS = sb("WS", [128, 2])
        X = sb("X", [128, DB * TC])
        XPC = sb("XPC", [128, DB])
        PQS = sb("PQS", [128, 2 * DB])
        HSTT = [sb(f"HST{l}", [128, NST * DIB], BF16) for l in range(2)]
        CCAR = [sb(f"CCAR{l}", [128, DIB * 3]) for l in range(2)]

        env = dict(nc=nc, c=c, CT=CT, IDE=IDE, IDF=IDF, ONE=ONE, ONF=ONF, AV=AV,
                   OMV=OMV, FV=FV, MV=MV, ATL=ATL, WS=WS, X=X, XPC=XPC, PQS=PQS,
                   HSTT=HSTT, CCAR=CCAR, wbf=wbf, w1q=w1q, w2q=w2q, inwb=inwb,
                   outwb=outwb, xpwb=xpwb, dtwb=dtwb, SEL=SEL)

        ld0 = [c.dma("q", "init", CT[:, :], consts[:, :]),
               c.dma("q", "init", IDE[:, :], identb[:, :]),
               c.dma("q", "init", IDF[:, :], identf[:, :]),
               c.dma("q", "init", ONE[:, :], onesb[:, :]),
               c.dma("q", "init", ONF[:, :], onesf[:, :]),
               c.dma("q", "init", SEL[:, :], selb[:, :]),
               c.dma("q", "init", AV[:, :], avec[:, :]),
               c.dma("q", "init", FV[:, :], fvec[:, :])]
        for l in range(2):
            ld0.append(c.dma("q", "init", MV[l][:, :], mvec[l][:, :]))
            ld0.append(c.dma("q", "init", ATL[l][:, :], alog[l][:, :]))
        c.wait("v", *ld0)
        c.wait("s", *ld0)
        for j in range(3):
            c.inc("v", v.tensor_scalar(out=OMV[:, j * 8:(j + 1) * 8], in0=AV[:, j * 8:(j + 1) * 8],
                                       scalar1=-1.0, scalar2=1.0, op0=ALU.mult, op1=ALU.add))
        for l in range(2):
            c.inc("s", s.activation(out=ATL[l][:, :], in_=ATL[l][:, :], func=AF.Exp))
            c.wait("v", c.tok("s"))
            c.inc("v", v.tensor_scalar_mul(out=ATL[l][:, :], in0=ATL[l][:, :], scalar1=-1.0))
        c.barrier()

        for args in ((wsrc["wk"], wbf["wk"], D, D), (wsrc["wv"], wbf["wv"], D, D),
                     (wsrc["wr"], wbf["wr"], D, D), (wsrc["wo"], wbf["wo"], D, D),
                     (inwt[0], inwb[0], D, 2 * DI), (outwt[0], outwb[0], DI, D),
                     (xpwt[0], xpwb[0], DI, 128), (dtwt[0], dtwb[0], RK, DI),
                     (inwt[1], inwb[1], D, 2 * DI), (outwt[1], outwb[1], DI, D),
                     (xpwt[1], xpwb[1], DI, 128), (dtwt[1], dtwb[1], RK, DI)):
            emit_wcast(c, *args)
            c.barrier("qg")
        emit_wquant(c, env, w1t, w1q, 0, D, HF)
        c.barrier()
        emit_wquant(c, env, w2t, w2q, 1, HF, D)
        c.barrier()

        for b in range(BC):
            zc = CT[:, 0:1]
            c.inc("v", v.tensor_copy(out=XPC[:, :], in_=zc.broadcast_to([128, DB])))
            c.inc("v", v.tensor_scalar(out=PQS[:, 0:DB], in0=zc.broadcast_to([128, DB]),
                                       scalar1=0.0, scalar2=-1.0e30, op0=ALU.mult, op1=ALU.add))
            c.inc("v", v.tensor_copy(out=PQS[:, DB:2 * DB], in_=zc.broadcast_to([128, DB])))
            for l in range(2):
                c.inc("v", v.tensor_copy(out=HSTT[l][:, :], in_=zc.broadcast_to([128, NST * DIB])))
                c.inc("v", v.tensor_copy(out=CCAR[l][:, :], in_=zc.broadcast_to([128, DIB * 3])))
            c.barrier()

            for ch in range(NCH):
                t0 = ch * TC
                ldx = c.dma("q", "xload", X[:, :].rearrange("p (db t) -> p db t", db=DB),
                            xin[b].rearrange("(db p) t -> p db t", p=128)[:, :, t0:t0 + TC])
                for e in ("v", "s", "t"):
                    c.wait(e, ldx)
                if STAGES & 1:
                    emit_attn_a(c, env)
                    c.barrier()
                if STAGES & 2:
                    emit_ffn(c, env)
                    c.barrier()
                if STAGES & 4:
                    emit_mamba(c, env)
                    c.barrier()
                if STAGES & 8:
                    emit_ffn(c, env)
                    c.barrier()
                c.dma("q", "xstore", xout[b].rearrange("(db p) t -> p db t", p=128)[:, :, t0:t0 + TC],
                      X[:, :].rearrange("p (db t) -> p db t", db=DB))
                c.barrier()
        c.barrier()
        c.finalize()
    return nc


def emit_wcast(c, src, dst, K, M):
    nc = c.nc
    colchunk = min(M, 2048)
    with ExitStack() as st:
        wt = [st.enter_context(nc.sbuf_tensor(_un(f"wc_t{i}"), [128, colchunk], BF16))
              for i in range(2)]
        prev_store = [None, None]
        idx = 0
        for i in range((K + 127) // 128):
            pb = min(128, K - i * 128)
            for j in range(M // colchunk):
                p = idx % 2
                idx += 1
                c.wait("g", prev_store[p])
                tk = c.dma("g", f"wci{p}", wt[p][0:pb, :],
                           src[i * 128:i * 128 + pb, j * colchunk:(j + 1) * colchunk])
                c.wait("q", tk)
                prev_store[p] = c.dma(
                    "q", f"wco{p}",
                    dst[i * 128:i * 128 + pb, j * colchunk:(j + 1) * colchunk],
                    wt[p][0:pb, :])


def emit_wquant(c, env, src, dst, widx, K, M):
    nc = c.nc
    v, s, t_, g, q = (c.e[k] for k in Ctx.ENGS)
    ONE, WS = env["ONE"], env["WS"]
    nch, mch = K // 128, M // 512
    with ExitStack() as st:
        sbt = lambda nm, shp, dt=F32: st.enter_context(nc.sbuf_tensor(_un(nm), shp, dt))
        wt = [sbt(f"wq_in{i}", [128, 512]) for i in range(2)]
        acc = sbt("wq_acc", [128, 1])
        red = sbt("wq_red", [128, 1])
        accb = sbt("wq_accb", [128, 1], BF16)
        rws = sbt("wq_rws", [128, 1])
        wqt = [sbt(f"wq_out{i}", [128, 512], BF16) for i in range(2)]
        uq = sbt("wq_u", [128, 512])
        iq = sbt("wq_i", [128, 512], I32)
        cq = sbt("wq_c", [128, 512])
        vq = sbt("wq_v", [128, 512])
        ps = st.enter_context(nc.psum_tensor(_un("wq_ps"), [128, 1], F32))
        first = True
        prev_red = [None, None]
        for idx in range(nch * mch):
            i, j = divmod(idx, mch)
            p = idx % 2
            c.wait("q", prev_red[p])
            tk = c.dma("q", f"wq{p}", wt[p][:, :], src[i * 128:(i + 1) * 128, j * 512:(j + 1) * 512])
            c.wait("v", tk)
            c.fence("v")
            c.inc_fence("v", v.tensor_reduce(out=red[:, :], in_=wt[p][:, :], axis=AXX,
                                             op=ALU.add, apply_absolute_value=True))
            if first:
                rt = c.inc("v", v.tensor_copy(out=acc[:, :], in_=red[:, :]))
                first = False
            else:
                rt = c.inc("v", v.tensor_add(out=acc[:, :], in0=acc[:, :], in1=red[:, :]))
            prev_red[p] = rt
        c.fence("v")
        cb = c.inc("v", v.tensor_copy(out=accb[:, :], in_=acc[:, :]))
        c.wait("t", cb)
        mm = c.inc("t", t_.matmul(ps[:, :], lhsT=ONE[:, :], rhs=accb[:, :],
                                  start=True, stop=True))
        c.wait("v", mm)
        c.inc("v", v.tensor_scalar(out=WS[:, widx:widx + 1], in0=ps[:, :],
                                   scalar1=1.0 / (K * M), scalar2=LNEPS,
                                   op0=ALU.mult, op1=ALU.max))
        c.fence("v")
        rv = c.inc_fence("v", v.reciprocal(out=rws[:, :], in_=WS[:, widx:widx + 1]))
        prev_st = [rv, rv]
        prev_wq = [None, None]
        for idx in range(nch * mch):
            i, j = divmod(idx, mch)
            p = idx % 2
            c.wait("q", prev_st[p])
            tk = c.dma("q", f"wq{p}", wt[p][:, :], src[i * 128:(i + 1) * 128, j * 512:(j + 1) * 512])
            c.wait("v", tk, prev_wq[p])
            c.inc("v", v.tensor_scalar(out=vq[:, :], in0=wt[p][:, :], scalar1=rws[:, 0:1],
                                       scalar2=0.5, op0=ALU.mult, op1=ALU.add))
            c.inc("v", v.tensor_copy(out=iq[:, :], in_=vq[:, :]))
            c.inc("v", v.tensor_copy(out=uq[:, :], in_=iq[:, :]))
            c.inc("v", v.tensor_tensor(out=cq[:, :], in0=uq[:, :], in1=vq[:, :], op=ALU.is_gt))
            c.inc("v", v.tensor_tensor(out=uq[:, :], in0=uq[:, :], in1=cq[:, :], op=ALU.subtract))
            wq = c.inc("v", v.tensor_scalar(out=wqt[p][:, :], in0=uq[:, :], scalar1=-1.0,
                                            scalar2=1.0, op0=ALU.max, op1=ALU.min))
            c.wait("q", wq)
            stt = c.dma("q", f"ws{p}", dst[i * 128:(i + 1) * 128, j * 512:(j + 1) * 512], wqt[p][:, :])
            prev_st[p] = wq
            prev_wq[p] = stt
        c.barrier("vqt")


def emit_ln_stats(c, env, src_fn, nblk, MU, RSTD, XB2, SQ2, ps_s, ps_q, T1, T2, invn,
                  src_big=None):
    nc = c.nc
    v, s, t_, g, q = (c.e[k] for k in Ctx.ENGS)
    ONE = env["ONE"]
    if src_big is not None:
        c.inc("s", s.copy(out=XB2[:, 0:nblk * TC], in_=src_big))
        sq = c.inc("s", s.activation(out=SQ2[:, 0:nblk * TC], in_=src_big,
                                     func=AF.Square))
        c.wait("t", sq)
        for i in range(nblk):
            c.inc("t", t_.matmul(ps_s[:, :], lhsT=ONE[:, :],
                                 rhs=XB2[:, i * TC:(i + 1) * TC],
                                 start=(i == 0), stop=(i == nblk - 1)))
            mm = c.inc("t", t_.matmul(ps_q[:, :], lhsT=ONE[:, :],
                                      rhs=SQ2[:, i * TC:(i + 1) * TC],
                                      start=(i == 0), stop=(i == nblk - 1)))
        c.wait("v", mm)
    else:
        prev_mm = [None, None]
        for i in range(nblk):
            p = i % 2
            c.wait("s", prev_mm[p])
            cb = c.inc("s", s.copy(out=XB2[:, p * TC:(p + 1) * TC], in_=src_fn(i)))
            sq = c.inc("s", s.activation(out=SQ2[:, p * TC:(p + 1) * TC], in_=src_fn(i),
                                         func=AF.Square))
            c.wait("t", sq)
            c.inc("t", t_.matmul(ps_s[:, :], lhsT=ONE[:, :],
                                 rhs=XB2[:, p * TC:(p + 1) * TC],
                                 start=(i == 0), stop=(i == nblk - 1)))
            mm = c.inc("t", t_.matmul(ps_q[:, :],
                                      lhsT=ONE[:, :], rhs=SQ2[:, p * TC:(p + 1) * TC],
                                      start=(i == 0), stop=(i == nblk - 1)))
            prev_mm[p] = mm
        c.wait("v", prev_mm[(nblk - 1) % 2])
    c.inc("v", v.tensor_scalar_mul(out=MU[:, :], in0=ps_s[:, :], scalar1=invn))
    m2 = c.inc("v", v.tensor_scalar_mul(out=T1[:, :], in0=ps_q[:, :], scalar1=invn))
    c.wait("s", m2)
    sq2 = c.inc("s", s.activation(out=T2[:, :], in_=MU[:, :], func=AF.Square))
    c.wait("v", sq2)
    vv = c.inc("v", v.tensor_tensor(out=T1[:, :], in0=T1[:, :], in1=T2[:, :], op=ALU.subtract))
    c.wait("s", vv)
    c.inc("s", s.activation(out=T1[:, :], in_=T1[:, :], func=AF.Ln, bias=LNEPS))
    rs = c.inc("s", s.activation(out=RSTD[:, :], in_=T1[:, :], func=AF.Exp, scale=-0.5))
    c.wait("v", rs)
    return rs


def emit_floor_round(c, VQ, IQ, UQ, CQ, out_ap, n):
    v = c.e["v"]
    c.inc("v", v.tensor_copy(out=IQ[:, 0:n], in_=VQ[:, 0:n]))
    c.inc("v", v.tensor_copy(out=UQ[:, 0:n], in_=IQ[:, 0:n]))
    c.inc("v", v.tensor_tensor(out=CQ[:, 0:n], in0=UQ[:, 0:n], in1=VQ[:, 0:n], op=ALU.is_gt))
    return c.inc("v", v.tensor_tensor(out=out_ap, in0=UQ[:, 0:n], in1=CQ[:, 0:n],
                                      op=ALU.subtract))


def emit_bridge(c, env, col_src, ncols, ps_row, ROW, ps_rep, width, prev=None):
    nc = c.nc
    v, s, t_, g, q = (c.e[k] for k in Ctx.ENGS)
    IDF, ONF = env["IDF"], env["ONF"]
    ev = prev
    for i in range(ncols):
        c.wait("t", ev)
        tr = c.inc("t", t_.transpose(ps_row[0:1, :], col_src[:, i:i + 1], IDF[:, :]))
        c.wait("s", tr)
        ev = c.inc("s", s.copy(out=ROW[0:1, i * 128:(i + 1) * 128], in_=ps_row[0:1, :]))
    c.wait("t", ev, prev)
    mm = c.inc("t", t_.matmul(ps_rep[:, 0:width], lhsT=ONF[0:1, 0:128],
                              rhs=ROW[0:1, 0:width], start=True, stop=True))
    return mm


def emit_attn_a(c, env):
    nc = c.nc
    v, s, t_, g, q = (c.e[k] for k in Ctx.ENGS)
    X, CT, ONE, AV, OMV, XPC, PQS, wbf = (env[k] for k in
                                          ("X", "CT", "ONE", "AV", "OMV", "XPC", "PQS", "wbf"))
    tmax = {"k": 0, "v": 1, "r": 2}

    def av(j, db):
        return AV[:, j * 8 + db: j * 8 + db + 1]

    def omv(j, db):
        return OMV[:, j * 8 + db: j * 8 + db + 1]

    WDEC = lambda db: AV[:, 24 + db: 24 + db + 1]
    UFIR = lambda db: AV[:, 32 + db: 32 + db + 1]

    def xdb(db):
        return X[:, db * TC:(db + 1) * TC]

    with ExitStack() as st:
        sbt = lambda nm, shp, dt=F32: st.enter_context(nc.sbuf_tensor(_un(nm), shp, dt))
        XN = sbt("aXN", [128, DB * (TC + 1)])
        XB2 = sbt("aXB2", [128, DB * TC], BF16)
        SQ2 = sbt("aSQ2", [128, DB * TC], BF16)
        MU = sbt("aMU", [128, TC])
        RSTD = sbt("aRSTD", [128, TC])
        T1 = sbt("aT1", [128, TC])
        T2 = sbt("aT2", [128, TC])
        XKb = sbt("aXKb", [128, DB * TC], BF16)
        XVb = sbt("aXVb", [128, DB * TC], BF16)
        XRb = sbt("aXRb", [128, DB * TC], BF16)
        K_ = sbt("aK", [128, DB * TC], BF16)
        V_ = sbt("aV", [128, DB * TC], BF16)
        R_ = sbt("aR", [128, DB * TC], BF16)
        Y_ = sbt("aY", [128, DB * TC], BF16)
        WT = sbt("aWT", [128, DB * 1024], BF16)
        WT2 = sbt("aWT2", [128, DB * 1024], BF16)
        WT3 = sbt("aWT3", [128, DB * 1024], BF16)
        P_ = sbt("aP", [128, TC + 1])
        Q_ = sbt("aQ", [128, TC + 1])
        SA = sbt("aSA", [128, TC])
        SB = sbt("aSB", [128, TC])
        EA = sbt("aEA", [128, TC])
        EB = sbt("aEB", [128, TC])
        AQ = sbt("aAQ", [128, TC])
        BQ = sbt("aBQ", [128, TC])
        MM_ = sbt("aM", [128, TC])
        SO = sbt("aSO", [128, TC])
        SC = sbt("aSC", [128, TC])
        N1 = sbt("aN1", [128, TC])
        N2 = sbt("aN2", [128, TC])
        DEN = sbt("aDEN", [128, TC])
        ps_s = st.enter_context(nc.psum_tensor(_un("aps_s"), [128, TC], F32))
        ps_q = st.enter_context(nc.psum_tensor(_un("aps_q"), [128, TC], F32))
        ps_m = [st.enter_context(nc.psum_tensor(_un(f"aps_m{i}"), [128, TC], F32))
                for i in range(2)]

        wk_tok = c.dma("q", "awt", WT[:, :].rearrange("p (db e) -> p db e", db=DB),
                       wbf["wk"].rearrange("(db p) e -> p db e", p=128))
        wv_tok = c.dma("q", "awt2", WT2[:, :].rearrange("p (db e) -> p db e", db=DB),
                       wbf["wv"].rearrange("(db p) e -> p db e", p=128))
        wr_tok = c.dma("q", "awt3", WT3[:, :].rearrange("p (db e) -> p db e", db=DB),
                       wbf["wr"].rearrange("(db p) e -> p db e", p=128))

        emit_ln_stats(c, env, xdb, DB, MU, RSTD, XB2, SQ2, ps_s, ps_q, T1, T2, 1.0 / D,
                      src_big=X[:, :])
        c.inc("v", v.tensor_copy(
            out=XN[:, :].rearrange("p (db t) -> p db t", t=TC + 1)[:, :, 0:1],
            in_=XPC[:, :].rearrange("p db -> p db ()")))
        XNs = seg(XN[:, :], DB, TC, lead=1)
        c.inc("v", v.tensor_tensor(out=XNs, in0=seg(X[:, :], DB, TC),
                                   in1=fbc(MU[:, :], DB, TC), op=ALU.subtract))
        c.inc("v", v.tensor_tensor(out=XNs, in0=XNs,
                                   in1=fbc(RSTD[:, :], DB, TC), op=ALU.mult))
        c.inc("v", v.tensor_copy(
            out=XPC[:, :].rearrange("p db -> p db ()"),
            in_=XN[:, :].rearrange("p (db t) -> p db t", t=TC + 1)[:, :, TC:TC + 1]))
        for j, XBt in ((0, XKb), (1, XVb), (2, XRb)):
            for db in range(DB):
                base = db * (TC + 1)
                c.inc("v", v.tensor_scalar_mul(out=T1[:, :], in0=XN[:, base:base + TC],
                                               scalar1=omv(j, db)))
                c.inc("v", v.scalar_tensor_tensor(
                    out=XBt[:, db * TC:(db + 1) * TC], in0=XN[:, base + 1:base + TC + 1],
                    scalar=av(j, db), in1=T1[:, :], op0=ALU.mult, op1=ALU.add))
        mix_done = c.tok("v")

        pslast = [None, None]

        c.wait("t", mix_done)

        c.wait("t", wk_tok, wv_tok, wr_tok)
        k_evs, v_evs, r_evs = [], [], []
        cnt3 = [0]
        for eb in range(DB):
            for j, (wtile, srcb, dst, act, evl) in enumerate((
                    (WT, XKb, K_, None, k_evs),
                    (WT2, XVb, V_, None, v_evs),
                    (WT3, XRb, R_, AF.Sigmoid, r_evs))):
                pp = cnt3[0] % 2
                cnt3[0] += 1
                psb = ps_m[pp]
                if pslast[pp] is not None:
                    c.wait("t", pslast[pp])
                for db in range(DB):
                    mm = t_.matmul(psb[:, :],
                                   lhsT=wtile[:, db * 1024 + eb * 128: db * 1024 + (eb + 1) * 128],
                                   rhs=srcb[:, db * TC:(db + 1) * TC],
                                   start=(db == 0), stop=(db == DB - 1))
                mmt = c.inc("t", mm)
                c.wait("s", mmt)
                if act is None:
                    ev = c.inc("s", s.copy(out=dst[:, eb * TC:(eb + 1) * TC], in_=psb[:, :]))
                else:
                    ev = c.inc("s", s.activation(out=dst[:, eb * TC:(eb + 1) * TC],
                                                 in_=psb[:, :], func=act))
                pslast[pp] = ev
                evl.append(ev)
        c.wait("q", c.tok("t"))
        wo_tok = c.dma("q", "awt", WT[:, :].rearrange("p (db e) -> p db e", db=DB),
                       wbf["wo"].rearrange("(db p) e -> p db e", p=128))

        for eb in range(DB):
            c.wait("v", k_evs[eb], v_evs[eb], r_evs[eb])
            kk = K_[:, eb * TC:(eb + 1) * TC]
            vv = V_[:, eb * TC:(eb + 1) * TC]
            rr = R_[:, eb * TC:(eb + 1) * TC]
            c.inc("v", v.tensor_copy(out=P_[:, 0:1], in_=PQS[:, eb:eb + 1]))
            c.inc("v", v.tensor_copy(out=Q_[:, 0:1], in_=PQS[:, DB + eb:DB + eb + 1]))
            c.inc("v", v.tensor_scalar_add(out=T1[:, :], in0=kk, scalar1=UFIR(eb)))
            c.inc("v", v.tensor_tensor_scan(
                out=P_[:, 1:TC + 1], data0=WDEC(eb).broadcast_to([128, TC]), data1=kk,
                initial=P_[:, 0:1], op0=ALU.add, op1=ALU.max))
            c.inc("v", v.scalar_tensor_tensor(out=SA[:, :], in0=P_[:, 0:TC],
                                              scalar=WDEC(eb), in1=P_[:, 1:TC + 1],
                                              op0=ALU.add, op1=ALU.subtract))
            c.inc("v", v.tensor_tensor(out=SB[:, :], in0=kk, in1=P_[:, 1:TC + 1],
                                       op=ALU.subtract))
            sb_done = c.tok("v")
            c.wait("s", sb_done)
            c.inc("s", s.activation(out=EA[:, :], in_=SA[:, :], func=AF.Exp))
            e_done = c.inc("s", s.activation(out=EB[:, :], in_=SB[:, :], func=AF.Exp))
            c.wait("v", e_done)
            c.inc("v", v.tensor_scalar_max(out=AQ[:, :], in0=EA[:, :], scalar1=C30))
            c.inc("v", v.scalar_tensor_tensor(out=BQ[:, :], in0=EB[:, :], scalar=C30,
                                              in1=vv, op0=ALU.max, op1=ALU.mult))
            c.inc("v", v.tensor_tensor_scan(
                out=Q_[:, 1:TC + 1], data0=AQ[:, :], data1=BQ[:, :],
                initial=Q_[:, 0:1], op0=ALU.mult, op1=ALU.add))
            c.inc("v", v.tensor_copy(out=PQS[:, eb:eb + 1], in_=P_[:, TC:TC + 1]))
            c.inc("v", v.tensor_copy(out=PQS[:, DB + eb:DB + eb + 1], in_=Q_[:, TC:TC + 1]))
            c.inc("v", v.tensor_tensor(out=MM_[:, :], in0=P_[:, 0:TC], in1=T1[:, :],
                                       op=ALU.max))
            c.inc("v", v.tensor_tensor(out=SA[:, :], in0=P_[:, 0:TC], in1=MM_[:, :],
                                       op=ALU.subtract))
            c.inc("v", v.tensor_tensor(out=SB[:, :], in0=T1[:, :], in1=MM_[:, :],
                                       op=ALU.subtract))
            so_in = c.tok("v")
            c.wait("s", so_in)
            c.inc("s", s.activation(out=SO[:, :], in_=SA[:, :], func=AF.Exp))
            sc_done = c.inc("s", s.activation(out=SC[:, :], in_=SB[:, :], func=AF.Exp))
            c.wait("v", sc_done)
            c.inc("v", v.tensor_scalar_max(out=SO[:, :], in0=SO[:, :], scalar1=C30))
            c.inc("v", v.tensor_scalar_max(out=SC[:, :], in0=SC[:, :], scalar1=C30))
            c.inc("v", v.tensor_tensor(out=N1[:, :], in0=SO[:, :], in1=Q_[:, 0:TC],
                                       op=ALU.mult))
            c.inc("v", v.tensor_tensor(out=N2[:, :], in0=SC[:, :], in1=vv, op=ALU.mult))
            c.inc("v", v.tensor_tensor(out=N1[:, :], in0=N1[:, :], in1=N2[:, :], op=ALU.add))
            dn = c.inc("v", v.scalar_tensor_tensor(out=DEN[:, :], in0=SO[:, :], scalar=WKVEPS,
                                                   in1=SC[:, :], op0=ALU.add, op1=ALU.add))
            c.wait("s", dn)
            c.inc("s", s.activation(out=DEN[:, :], in_=DEN[:, :], func=AF.Ln))
            rd = c.inc("s", s.activation(out=DEN[:, :], in_=DEN[:, :], func=AF.Exp, scale=-1.0))
            c.wait("v", rd)
            c.inc("v", v.tensor_tensor(out=N1[:, :], in0=N1[:, :], in1=DEN[:, :], op=ALU.mult))
            c.inc("v", v.tensor_tensor(out=Y_[:, eb * TC:(eb + 1) * TC], in0=rr,
                                       in1=N1[:, :], op=ALU.mult))
        y_done = c.tok("v")

        c.wait("t", wo_tok, y_done)
        for dbo in range(DB):
            psb = ps_m[dbo % 2]
            if pslast[dbo % 2] is not None:
                c.wait("t", pslast[dbo % 2])
            for ebi in range(DB):
                mm = t_.matmul(psb[:, :],
                               lhsT=WT[:, ebi * 1024 + dbo * 128: ebi * 1024 + (dbo + 1) * 128],
                               rhs=Y_[:, ebi * TC:(ebi + 1) * TC],
                               start=(ebi == 0), stop=(ebi == DB - 1))
            mmt = c.inc("t", mm)
            c.wait("v", mmt)
            pslast[dbo % 2] = c.inc("v", v.tensor_tensor(out=xdb(dbo), in0=xdb(dbo),
                                                         in1=psb[:, :], op=ALU.add))


def emit_ffn(c, env):
    nc = c.nc
    v, s, t_, g, q = (c.e[k] for k in Ctx.ENGS)
    X, CT, ONE, IDF, ONF, FV, WS = (env[k] for k in
                                    ("X", "CT", "ONE", "IDF", "ONF", "FV", "WS"))
    w1q, w2q = env["w1q"], env["w2q"]

    def xdb(db):
        return X[:, db * TC:(db + 1) * TC]

    with ExitStack() as st:
        sbt = lambda nm, shp, dt=F32: st.enter_context(nc.sbuf_tensor(_un(nm), shp, dt))
        XB2 = sbt("fXB2", [128, DB * TC], BF16)
        SQ2 = sbt("fSQ2", [128, DB * TC], BF16)
        MU = sbt("fMU", [128, TC])
        RSTD = sbt("fRSTD", [128, TC])
        T1 = sbt("fT1", [128, TC])
        T2 = sbt("fT2", [128, TC])
        T0 = sbt("fT0", [128, DB * TC])
        QREP = sbt("fQREP", [128, TC])
        AREP = sbt("fAREP", [128, TC])
        MXC = sbt("fMXC", [128, 4])
        ROW = sbt("fROW", [1, TC])
        XQ = sbt("fXQ", [128, DB * FFT], BF16)
        HQ = sbt("fHQ", [128, HB * FFT], BF16)
        HG = sbt("fHG", [128, HB * FFT])
        UQ = sbt("fUQ", [128, 8 * FFT])
        IQ = sbt("fIQ", [128, 8 * FFT], I32)
        CQ = sbt("fCQ", [128, 8 * FFT])
        VQ = sbt("fVQ", [128, 8 * FFT])
        WP = [sbt("fWPa", [128, DB * 512], BF16), sbt("fWPb", [128, DB * 512], BF16)]
        ps_s = st.enter_context(nc.psum_tensor(_un("fps_s"), [128, TC], F32))
        ps_q = st.enter_context(nc.psum_tensor(_un("fps_q"), [128, TC], F32))
        ps_m = [st.enter_context(nc.psum_tensor(_un(f"fps_m{i}"), [128, FFT], F32))
                for i in range(2)]
        ps_t = st.enter_context(nc.psum_tensor(_un("fps_t"), [128, 512], F32))
        ps_t2 = st.enter_context(nc.psum_tensor(_un("fps_t2"), [128, 512], F32))
        MXG = sbt("fMXG", [128, 16])
        ps_row = st.enter_context(nc.psum_tensor(_un("fps_row"), [1, 128], F32))
        ps_b = st.enter_context(nc.psum_tensor(_un("fps_b"), [128, TC], F32))

        emit_ln_stats(c, env, xdb, DB, MU, RSTD, XB2, SQ2, ps_s, ps_q, T1, T2, 1.0 / D,
                      src_big=X[:, :])
        c.inc("v", v.tensor_tensor(out=seg(T0[:, :], DB, TC), in0=seg(X[:, :], DB, TC),
                                   in1=fbc(MU[:, :], DB, TC), op=ALU.subtract))
        t0_done = c.tok("v")
        c.wait("t", t0_done)
        red_bank = [None, None]
        rr = 0
        for sl in range(4):
            for grp in range(2):
                pt = (ps_t, ps_t2)[rr % 2]
                trs = None
                for k in range(4):
                    db = grp * 4 + k
                    if k == 0:
                        c.wait("t", red_bank[rr % 2])
                    trs = c.inc("t", t_.transpose(
                        pt[:, k * 128:(k + 1) * 128],
                        T0[:, db * TC + sl * 128: db * TC + (sl + 1) * 128], IDF[:, :]))
                c.wait("v", trs, t0_done)
                c.fence("v")
                red_bank[rr % 2] = c.inc_fence("v", v.tensor_reduce(
                    out=MXG[:, sl * 2 + grp: sl * 2 + grp + 1],
                    in_=pt[:, :], axis=AXX, op=ALU.max,
                    apply_absolute_value=True))
                rr += 1
        c.fence("v")
        c.inc_fence("v", v.tensor_reduce(
            out=MXC[:, 0:4], in_=MXG[:, 0:8].rearrange("p (a b) -> p a b", a=4),
            axis=AXX, op=ALU.max, apply_absolute_value=True))
        bq = emit_bridge(c, env, MXC, 4, ps_row, ROW, ps_b, TC, prev=c.tok("v"))
        c.wait("v", bq)
        c.inc("v", v.tensor_tensor(out=T1[:, :], in0=RSTD[:, :], in1=ps_b[:, :], op=ALU.mult))
        c.inc("v", v.tensor_scalar_max(out=T1[:, :], in0=T1[:, :], scalar1=LNEPS))
        c.inc("v", v.reciprocal(out=T2[:, :], in_=T1[:, :]))
        c.inc("v", v.tensor_scalar_mul(out=T2[:, :], in0=T2[:, :], scalar1=127.0))
        c.inc("v", v.tensor_tensor(out=QREP[:, :], in0=T2[:, :], in1=RSTD[:, :], op=ALU.mult))
        c.inc("v", v.tensor_scalar(out=AREP[:, :], in0=T1[:, :], scalar1=WS[:, 0:1],
                                   scalar2=1.0 / 127.0, op0=ALU.mult, op1=ALU.mult))
        qa_done = c.tok("v")
        pslast = [None, None]
        wp_guard = [None, None]

        for hf in range(TC // FFT):
            toff = hf * FFT
            W8 = 8 * FFT
            t0v = T0[:, :].rearrange("p (b t) -> p b t", b=DB)[:, :, toff:toff + FFT]
            c.inc("v", v.tensor_tensor(out=seg(VQ[:, :], DB, FFT), in0=t0v,
                                       in1=fbc(QREP[:, toff:toff + FFT], DB, FFT),
                                       op=ALU.mult))
            c.inc("v", v.tensor_scalar_add(out=VQ[:, 0:W8], in0=VQ[:, 0:W8], scalar1=0.5))
            emit_floor_round(c, VQ, IQ, UQ, CQ, XQ[:, 0:W8], W8)
            xq_done = c.tok("v")
            def wp_dma1(chunk, p):
                c.wait("q", wp_guard[p])
                wsrc = w1q.rearrange("(db p) e -> p db e", p=128)[
                    :, :, chunk * 512:(chunk + 1) * 512]
                wdst = WP[p][:, :].rearrange("p (db e) -> p db e", db=DB)
                return (c.dma("q", f"wp{p}a", wdst[:, 0:4], wsrc[:, 0:4]),
                        c.dma("q", f"wp{p}b", wdst[:, 4:8], wsrc[:, 4:8]))

            tokw = {0: wp_dma1(0, 0), 1: wp_dma1(1, 1)}
            for obg in range(HB // 4):
                p = obg % 2
                c.wait("t", tokw[p][0], tokw[p][1], xq_done)
                for oi in range(4):
                    ob = obg * 4 + oi
                    psb = ps_m[ob % 2]
                    if pslast[ob % 2] is not None:
                        c.wait("t", pslast[ob % 2])
                    for db in range(DB):
                        mm = t_.matmul(psb[:, :],
                                       lhsT=WP[p][:, db * 512 + oi * 128: db * 512 + (oi + 1) * 128],
                                       rhs=XQ[:, db * FFT:(db + 1) * FFT],
                                       start=(db == 0), stop=(db == DB - 1))
                    mmt = c.inc("t", mm)
                    c.wait("v", mmt, qa_done)
                    mt = c.inc("v", v.tensor_tensor(out=HG[:, ob * FFT:(ob + 1) * FFT],
                                                    in0=psb[:, :],
                                                    in1=AREP[:, toff:toff + FFT],
                                                    op=ALU.mult))
                    c.wait("s", mt)
                    gg = c.inc("s", s.activation(out=HG[:, ob * FFT:(ob + 1) * FFT],
                                                 in_=HG[:, ob * FFT:(ob + 1) * FFT],
                                                 func=AF.Gelu, bias=FV[:, ob:ob + 1],
                                                 scale=1.0))
                    pslast[ob % 2] = gg
                wp_guard[p] = mmt
                if obg + 2 < HB // 4:
                    tokw[p] = wp_dma1(obg + 2, p)
            h_done = pslast[(HB - 1) % 2]
            def wp_dma2(chunk, p):
                c.wait("q", wp_guard[p])
                wsrc = w2q.rearrange("(hb p) e -> p hb e", p=128)[
                    :, :, chunk * 128:(chunk + 1) * 128]
                wdst = WP[p][:, :].rearrange("p (hb e) -> p hb e", hb=HB)
                return (c.dma("q", f"wp{p}a", wdst[:, 0:16], wsrc[:, 0:16]),
                        c.dma("q", f"wp{p}b", wdst[:, 16:32], wsrc[:, 16:32]))

            tokw = {0: wp_dma2(0, 0), 1: wp_dma2(1, 1)}
            c.wait("t", h_done, c.tok("v"))
            red_bank = [None, None]
            rr = 0
            for sl in range(FFT // 128):
                for grp in range(8):
                    pt = (ps_t, ps_t2)[rr % 2]
                    trs = None
                    for k in range(4):
                        hb_ = grp * 4 + k
                        if k == 0:
                            c.wait("t", red_bank[rr % 2])
                        trs = c.inc("t", t_.transpose(
                            pt[:, k * 128:(k + 1) * 128],
                            HG[:, hb_ * FFT + sl * 128: hb_ * FFT + (sl + 1) * 128],
                            IDF[:, :]))
                    c.wait("v", trs, h_done)
                    c.fence("v")
                    red_bank[rr % 2] = c.inc_fence("v", v.tensor_reduce(
                        out=MXG[:, sl * 8 + grp: sl * 8 + grp + 1],
                        in_=pt[:, :], axis=AXX, op=ALU.max,
                        apply_absolute_value=True))
                    rr += 1
            c.fence("v")
            c.inc_fence("v", v.tensor_reduce(
                out=MXC[:, 0:2], in_=MXG[:, 0:16].rearrange("p (a b) -> p a b", a=2),
                axis=AXX, op=ALU.max, apply_absolute_value=True))
            bq = emit_bridge(c, env, MXC, FFT // 128, ps_row, ROW, ps_b, FFT,
                             prev=c.tok("v"))
            c.wait("v", bq)
            c.inc("v", v.tensor_scalar_max(out=T1[:, 0:FFT], in0=ps_b[:, 0:FFT],
                                           scalar1=LNEPS))
            c.fence("v")
            c.inc_fence("v", v.reciprocal(out=T2[:, 0:FFT], in_=T1[:, 0:FFT]))
            c.inc("v", v.tensor_scalar_mul(out=QREP[:, toff:toff + FFT], in0=T2[:, 0:FFT],
                                           scalar1=127.0))
            c.inc("v", v.tensor_scalar(out=AREP[:, toff:toff + FFT], in0=T1[:, 0:FFT],
                                       scalar1=WS[:, 1:2], scalar2=1.0 / 127.0,
                                       op0=ALU.mult, op1=ALU.mult))
            for hg in range(HB // 8):
                hoff = hg * 8 * FFT
                c.inc("v", v.tensor_tensor(
                    out=seg(VQ[:, :], 8, FFT),
                    in0=HG[:, :].rearrange("p (b t) -> p b t", b=HB)[:, hg * 8:(hg + 1) * 8, :],
                    in1=fbc(QREP[:, toff:toff + FFT], 8, FFT), op=ALU.mult))
                c.inc("v", v.tensor_scalar_add(out=VQ[:, 0:W8], in0=VQ[:, 0:W8],
                                               scalar1=0.5))
                emit_floor_round(c, VQ, IQ, UQ, CQ, HQ[:, hoff:hoff + W8], W8)
            hq_done = c.tok("v")
            for dbo in range(DB):
                p = dbo % 2
                c.wait("t", tokw[p][0], tokw[p][1], hq_done)
                psb = ps_m[dbo % 2]
                if pslast[dbo % 2] is not None:
                    c.wait("t", pslast[dbo % 2])
                for hb_ in range(HB):
                    mm = t_.matmul(psb[:, :], lhsT=WP[p][:, hb_ * 128:(hb_ + 1) * 128],
                                   rhs=HQ[:, hb_ * FFT:(hb_ + 1) * FFT],
                                   start=(hb_ == 0), stop=(hb_ == HB - 1))
                mmt = c.inc("t", mm)
                wp_guard[p] = mmt
                if dbo + 2 < DB:
                    tokw[p] = wp_dma2(dbo + 2, p)
                c.wait("v", mmt)
                c.inc("v", v.tensor_tensor(out=VQ[:, 0:FFT], in0=psb[:, :],
                                           in1=AREP[:, toff:toff + FFT], op=ALU.mult))
                pslast[dbo % 2] = c.inc("v", v.scalar_tensor_tensor(
                    out=X[:, dbo * TC + toff: dbo * TC + toff + FFT],
                    in0=VQ[:, 0:FFT], scalar=FV[:, HB + dbo:HB + dbo + 1],
                    in1=X[:, dbo * TC + toff: dbo * TC + toff + FFT],
                    op0=ALU.add, op1=ALU.add))


def emit_mamba(c, env):
    nc = c.nc
    v, s, t_, g, q = (c.e[k] for k in Ctx.ENGS)
    X, CT, ONE, MV, ATL, HSTT, CCAR = (env[k] for k in
                                       ("X", "CT", "ONE", "MV", "ATL", "HSTT", "CCAR"))
    SEL = env["SEL"]
    IDE = env["IDE"]
    inwb, outwb, xpwb, dtwb = (env[k] for k in ("inwb", "outwb", "xpwb", "dtwb"))

    def xdb(db):
        return X[:, db * TC:(db + 1) * TC]

    def cw(l, dib, tap):
        return MV[l][:, dib * 4 + tap: dib * 4 + tap + 1]

    def cb(l, dib):
        return MV[l][:, 64 + dib:64 + dib + 1]

    def dtb(l, dob):
        return MV[l][:, 80 + dob:80 + dob + 1]

    def dsk(l, dib):
        return MV[l][:, 96 + dib:96 + dib + 1]

    def lnw(l, db):
        return MV[l][:, 112 + db:112 + db + 1]

    def lnb(l, db):
        return MV[l][:, 120 + db:120 + db + 1]

    def acol(l, dib, n):
        return ATL[l][:, dib * NST + n: dib * NST + n + 1]

    with ExitStack() as st:
        sbt = lambda nm, shp, dt=F32: st.enter_context(nc.sbuf_tensor(_un(nm), shp, dt))
        XX = sbt("mXX", [128, DB * TC])
        MU = sbt("mMU", [128, TC])
        RSTD = sbt("mRSTD", [128, TC])
        T1 = sbt("mT1", [128, TC])
        T2 = sbt("mT2", [128, TC])
        SZ = sbt("mSZ", [128, DIB * TC], BF16)
        YA = sbt("mYA", [128, DIB * TC])
        DTR = sbt("mDTR", [64, TC], BF16)
        BF = sbt("mBF", [16, TC], BF16)
        CF = sbt("mCF", [16, TC], BF16)
        ps_m = [st.enter_context(nc.psum_tensor(_un(f"mps_m{i}"), [128, TC], F32))
                for i in range(2)]

        with ExitStack() as st2:
            XB2 = st2.enter_context(nc.sbuf_tensor(_un("mXB2"), [128, DB * TC], BF16))
            SQ2 = st2.enter_context(nc.sbuf_tensor(_un("mSQ2"), [128, DB * TC], BF16))
            ps_s = st2.enter_context(nc.psum_tensor(_un("mps_s"), [128, TC], F32))
            ps_q = st2.enter_context(nc.psum_tensor(_un("mps_q"), [128, TC], F32))
            emit_ln_stats(c, env, xdb, DB, MU, RSTD, XB2, SQ2, ps_s, ps_q, T1, T2,
                          1.0 / D, src_big=X[:, :])
            XXs = seg(XX[:, :], DB, TC)
            c.inc("v", v.tensor_tensor(out=XXs, in0=seg(X[:, :], DB, TC),
                                       in1=fbc(MU[:, :], DB, TC), op=ALU.subtract))
            c.inc("v", v.tensor_tensor(out=XXs, in0=XXs,
                                       in1=fbc(RSTD[:, :], DB, TC), op=ALU.mult))
            c.barrier()

        pslast = [None, None]
        for l in range(2):
          with ExitStack() as stL:
            XXb = stL.enter_context(nc.sbuf_tensor(_un("mXXb"), [128, DB * TC], BF16))
            WINZ = stL.enter_context(nc.sbuf_tensor(_un("mWINZ"), [128, DB * 512], BF16))
            UB = stL.enter_context(nc.sbuf_tensor(_un("mUB"), [128, DIB * TC], BF16))
            HTA = stL.enter_context(nc.sbuf_tensor(_un("mHTA"), [128, 8 * TC], BF16))
            WXP = stL.enter_context(nc.sbuf_tensor(_un("mWXP"), [128, DIB * 128], BF16))
            WDT = stL.enter_context(nc.sbuf_tensor(_un("mWDT"), [64, DI], BF16))
            wxp_tok = c.dma("q", "wxp", WXP[:, :].rearrange("p (dib e) -> p dib e", dib=DIB),
                            xpwb[l].rearrange("(dib p) e -> p dib e", p=128))
            wdt_tok = c.dma("q", "wdt", WDT[:, :], dtwb[l][:, :])
            with ExitStack() as st2:
                WIN = st2.enter_context(nc.sbuf_tensor(_un("mWIN"), [128, DB * 512], BF16))
                GT1 = st2.enter_context(nc.sbuf_tensor(_un("mGT1"), [128, TC], BF16))
                XC = st2.enter_context(nc.sbuf_tensor(_un("mXCc"), [128, DIB * (TC + 3)], BF16))
                cv_tok = {}
                WI2 = [WIN, WINZ]

                def wi_dma(chunk, p):
                    c.wait("q", wi_guard[p])
                    wsrc = inwb[l].rearrange("(db p) e -> p db e", p=128)[
                        :, :, chunk * 512:(chunk + 1) * 512]
                    wdst = WI2[p][:, :].rearrange("p (db e) -> p db e", db=DB)
                    return (c.dma("q", f"wi{p}a", wdst[:, 0:4], wsrc[:, 0:4]),
                            c.dma("q", f"wi{p}b", wdst[:, 4:8], wsrc[:, 4:8]))

                wi_guard = [None, None]
                tokw = {0: wi_dma(0, 0), 1: wi_dma(1, 1)}
                xxc = None
                for db in range(DB):
                    xxc = c.inc("s", s.copy(out=XXb[:, db * TC:(db + 1) * TC],
                                            in_=XX[:, db * TC:(db + 1) * TC]))
                for half in (0,):
                    for og in range(DI // 512):
                        p = og % 2
                        c.wait("t", tokw[p][0], tokw[p][1], xxc)
                        for oi in range(4):
                            dib = og * 4 + oi
                            psb = ps_m[dib % 2]
                            if pslast[dib % 2] is not None:
                                c.wait("t", pslast[dib % 2])
                            for db in range(DB):
                                mm = t_.matmul(
                                    psb[:, :],
                                    lhsT=WI2[p][:, db * 512 + oi * 128: db * 512 + (oi + 1) * 128],
                                    rhs=XXb[:, db * TC:(db + 1) * TC],
                                    start=(db == 0), stop=(db == DB - 1))
                            mmt = c.inc("t", mm)
                            wi_guard[p] = mmt
                            if oi == 3 and og + 2 < DI // 512:
                                tokw[p] = wi_dma(og + 2, p)
                            c.wait("s", mmt)
                            if half == 0:
                                base = dib * (TC + 3)
                                c.inc("s", s.copy(out=XC[:, base:base + 3],
                                                  in_=CCAR[l][:, dib * 3:(dib + 1) * 3]))
                                c.inc("s", s.copy(out=XC[:, base + 3:base + 3 + TC],
                                                  in_=psb[:, :]))
                                ev = c.inc("s", s.copy(out=CCAR[l][:, dib * 3:(dib + 1) * 3],
                                                       in_=psb[:, TC - 3:TC]))
                                ub_s = UB[:, dib * TC:(dib + 1) * TC]
                                c.wait("v", ev)
                                c.inc("v", v.tensor_scalar_mul(
                                    out=ub_s, in0=XC[:, base:base + TC],
                                    scalar1=cw(l, dib, 0)))
                                for tap in range(1, 4):
                                    c.inc("v", v.scalar_tensor_tensor(
                                        out=ub_s,
                                        in0=XC[:, base + tap:base + tap + TC],
                                        scalar=cw(l, dib, tap), in1=ub_s,
                                        op0=ALU.mult, op1=ALU.add))
                                cv_tok[dib] = c.tok("v")
                            else:
                                ev = c.inc("s", s.activation(
                                    out=SZ[:, dib * TC:(dib + 1) * TC],
                                    in_=psb[:, :], func=AF.Silu))
                            pslast[dib % 2] = ev
                        del ev
                for dib in range(DIB):
                    c.wait("s", cv_tok[dib])
                    c.inc("s", s.activation(out=UB[:, dib * TC:(dib + 1) * TC],
                                            in_=UB[:, dib * TC:(dib + 1) * TC],
                                            func=AF.Silu, bias=cb(l, dib), scale=1.0))
                u_done = c.tok("s")
                z_winz_guard = wi_guard[1]
            with ExitStack() as st2:
                ps_xd = st2.enter_context(nc.psum_tensor(_un("mps_xd"), [64, TC], F32))
                ps_xb = st2.enter_context(nc.psum_tensor(_un("mps_xb"), [16, TC], F32))
                ps_xc = st2.enter_context(nc.psum_tensor(_un("mps_xc"), [16, TC], F32))
                c.wait("t", wxp_tok, u_done)
                for dib in range(DIB):
                    st_f = (dib == 0)
                    sp_f = (dib == DIB - 1)
                    t_.matmul(ps_xd[:, :], lhsT=WXP[:, dib * 128: dib * 128 + 64],
                              rhs=UB[:, dib * TC:(dib + 1) * TC], start=st_f, stop=sp_f)
                    t_.matmul(ps_xb[:, :], lhsT=WXP[:, dib * 128 + 64: dib * 128 + 80],
                              rhs=UB[:, dib * TC:(dib + 1) * TC], start=st_f, stop=sp_f)
                    mm = t_.matmul(ps_xc[:, :], lhsT=WXP[:, dib * 128 + 80: dib * 128 + 96],
                                   rhs=UB[:, dib * TC:(dib + 1) * TC], start=st_f, stop=sp_f)
                xp_done = c.inc("t", mm)
                c.wait("s", xp_done)
                c.inc("s", s.copy(out=DTR[:, :], in_=ps_xd[:, :]))
                c.inc("s", s.copy(out=BF[:, :], in_=ps_xb[:, :]))
                ev = c.inc("s", s.copy(out=CF[:, :], in_=ps_xc[:, :]))
                c.wait("t", ev)
            stDT = ExitStack()
            DT = stDT.enter_context(nc.sbuf_tensor(_un("mDT"), [128, DIB * TC], BF16))
            WTl = stDT.enter_context(nc.sbuf_tensor(_un("mWT"), [128, DIB * TC], BF16))
            AEX = stDT.enter_context(nc.sbuf_tensor(_un("mAEX"), [128, 4 * TC], F32))
            c.wait("t", wdt_tok)
            for dob in range(DIB):
                psb = ps_m[dob % 2]
                if pslast[dob % 2] is not None:
                    c.wait("t", pslast[dob % 2])
                mm = t_.matmul(psb[:, :], lhsT=WDT[:, dob * 128:(dob + 1) * 128],
                               rhs=DTR[:, :], start=True, stop=True)
                mmt = c.inc("t", mm)
                c.wait("s", mmt)
                c.inc("s", s.activation(out=T1[:, :], in_=psb[:, :], func=AF.Identity,
                                        bias=dtb(l, dob), scale=1.0))
                c.inc("s", s.activation(out=T2[:, :], in_=T1[:, :], func=AF.Abs))
                c.inc("s", s.activation(out=T2[:, :], in_=T2[:, :], func=AF.Exp, scale=-1.0))
                c.inc("s", s.activation(out=T2[:, :], in_=T2[:, :], func=AF.Ln, bias=1.0))
                rl = c.inc("s", s.activation(out=T1[:, :], in_=T1[:, :], func=AF.Relu))
                c.wait("v", rl)
                dtv = c.inc("v", v.tensor_tensor(out=DT[:, dob * TC:(dob + 1) * TC],
                                                 in0=T1[:, :], in1=T2[:, :], op=ALU.add))
                pslast[dob % 2] = dtv
                c.wait("s", dtv)
            c.inc("v", v.tensor_tensor(out=seg(WTl[:, :], DIB, TC),
                                       in0=seg(DT[:, :], DIB, TC),
                                       in1=seg(UB[:, :], DIB, TC), op=ALU.mult))
            c.inc("v", v.tensor_tensor(out=seg(YA[:, :], DIB, TC),
                                       in0=seg(UB[:, :], DIB, TC),
                                       in1=sbc(MV[l][:, 96:112], DIB, TC), op=ALU.mult))
            prep_done = c.tok("v")
            with ExitStack() as st2:
                ps_y = st2.enter_context(nc.psum_tensor(_un("mps_y"), [128, 4 * TC], F32))
                ps_bc2 = st2.enter_context(nc.psum_tensor(_un("mps_bc2"), [128, 2 * TC], F32))
                BCREP = st2.enter_context(nc.sbuf_tensor(_un("mBCREP"), [128, 6 * TC], BF16))
                WB = UB
                aex_used = [None, None, None, None]
                z_done = [None]

                def z_group(og):
                    c.wait("q", z_winz_guard)
                    wtok = c.dma("q", "winz",
                                 WINZ[:, :].rearrange("p (db e) -> p db e", db=DB),
                                 inwb[l].rearrange("(db p) e -> p db e", p=128)
                                 [:, :, DI + og * 512: DI + (og + 1) * 512])
                    c.wait("t", wtok)
                    for oi in range(4):
                        dib = og * 4 + oi
                        psb = ps_m[dib % 2]
                        if pslast[dib % 2] is not None:
                            c.wait("t", pslast[dib % 2])
                        for db in range(DB):
                            mm = t_.matmul(
                                psb[:, :],
                                lhsT=WINZ[:, db * 512 + oi * 128: db * 512 + (oi + 1) * 128],
                                rhs=XXb[:, db * TC:(db + 1) * TC],
                                start=(db == 0), stop=(db == DB - 1))
                        mmt = c.inc("t", mm)
                        c.wait("s", mmt)
                        pslast[dib % 2] = c.inc("s", s.activation(
                            out=SZ[:, dib * TC:(dib + 1) * TC],
                            in_=psb[:, :], func=AF.Silu))
                        z_done[0] = pslast[dib % 2]
                    c.wait("q", c.tok("t"))

                Q4 = 4 * TC
                reps_tok = [None, None, None]
                bc_guard = [None, None, None]
                prev_mq = [None, None]
                prev_yacc = [None, None]
                evac_last = [None]

                def issue_bc(k):
                    col, sl = k % NST, k % 3
                    c.wait("t", evac_last[0])
                    t_.matmul(ps_bc2[:, 0:TC], lhsT=SEL[:, col * 128:(col + 1) * 128],
                              rhs=BF[:, :], start=True, stop=True)
                    mmt = c.inc("t", t_.matmul(
                        ps_bc2[:, TC:2 * TC], lhsT=SEL[:, col * 128:(col + 1) * 128],
                        rhs=CF[:, :], start=True, stop=True))
                    c.wait("s", mmt, bc_guard[sl])
                    ev = c.inc("s", s.copy(out=BCREP[:, sl * 2 * TC:(sl + 1) * 2 * TC],
                                           in_=ps_bc2[:, :]))
                    reps_tok[sl] = ev
                    evac_last[0] = ev

                c.relax = True
                wb_tok = {}
                save_tok = [None]
                KTOT = 4 * NST

                def emit_wb(kk2):
                    pq2 = kk2 % 2
                    sl2 = kk2 % 3
                    goff2 = (kk2 // NST) * 4
                    wbq2 = WB[:, (2 * pq2) * Q4:(2 * pq2) * Q4 + Q4]
                    Bq2 = BCREP[:, sl2 * 2 * TC: sl2 * 2 * TC + TC]
                    c.wait("v", reps_tok[sl2])
                    wb_tok[kk2] = c.inc("v", v.tensor_tensor(
                        out=seg(wbq2, 4, TC),
                        in0=WTl[:, :].rearrange("p (b w) -> p b w", b=DIB)
                        [:, goff2:goff2 + 4, :],
                        in1=fbc(Bq2, 4, TC), op=ALU.mult))

                for g4 in range(4):
                    goff = g4 * 4
                    for n in range(NST):
                        pq = n % 2
                        kk = g4 * NST + n
                        sl = kk % 3
                        if kk == 0:
                            issue_bc(0)
                            issue_bc(1)
                            c.wait("v", prep_done)
                            emit_wb(0)
                        Cq = BCREP[:, sl * 2 * TC + TC: sl * 2 * TC + 2 * TC]
                        mq = WB[:, (2 * pq + 1) * Q4:(2 * pq + 1) * Q4 + Q4]
                        wbq = WB[:, (2 * pq) * Q4:(2 * pq) * Q4 + Q4]
                        c.selfwait("v", save_tok[0])
                        c.selfwait("v", wb_tok[kk])
                        c.wait("v", prev_mq[pq], prev_yacc[pq])
                        tts_tok = [None] * 4

                        def emit_tts(di):
                            dib = goff + di
                            c.wait("s", aex_used[di])
                            atok = c.inc("s", s.activation(
                                out=AEX[:, di * TC:(di + 1) * TC],
                                in_=DT[:, dib * TC:(dib + 1) * TC],
                                func=AF.Exp, scale=acol(l, dib, n)))
                            c.wait("v", atok)
                            c.inc("v", v.tensor_tensor_scan(
                                out=HTA[:, di * TC:(di + 1) * TC],
                                data0=AEX[:, di * TC:(di + 1) * TC],
                                data1=wbq[:, di * TC:(di + 1) * TC],
                                initial=HSTT[l][:, dib * NST + n: dib * NST + n + 1],
                                op0=ALU.mult, op1=ALU.add))
                            tts_tok[di] = c.tok("v")
                            aex_used[di] = tts_tok[di]

                        def emit_m(di):
                            c.selfwait("v", tts_tok[di])
                            return c.inc("v", v.tensor_tensor(
                                out=mq[:, di * TC:(di + 1) * TC],
                                in0=HTA[:, di * TC:(di + 1) * TC],
                                in1=Cq, op=ALU.mult))

                        emit_tts(0)
                        emit_tts(1)
                        emit_m(0)
                        emit_tts(2)
                        emit_m(1)
                        emit_tts(3)
                        if kk + 1 < KTOT:
                            emit_wb(kk + 1)
                        if kk + 2 < KTOT:
                            issue_bc(kk + 2)
                        c.selfwait("v", tts_tok[3])
                        c.inc("v", v.tensor_copy(
                            out=HSTT[l][:, :].rearrange("p (d s) -> p d s", s=NST)
                            [:, goff:goff + 4, n:n + 1],
                            in_=seg(HTA[:, 0:Q4], 4, TC)[:, :, TC - 1:TC]))
                        save_tok[0] = c.tok("v")
                        emit_m(2)
                        mvq = emit_m(3)
                        prev_mq[pq] = mvq
                        bc_guard[sl] = mvq
                        c.wait("t", mvq)
                        for di in range(4):
                            mm = t_.matmul(ps_y[:, di * TC:(di + 1) * TC],
                                           lhsT=IDE[:, :],
                                           rhs=mq[:, di * TC:(di + 1) * TC],
                                           start=(n == 0), stop=(n == NST - 1))
                        yacc = c.inc("t", mm)
                        prev_yacc[pq] = yacc
                        if n == NST - 1:
                            c.wait("v", yacc)
                            c.inc("v", v.tensor_tensor(
                                out=YA[:, goff * TC:goff * TC + Q4],
                                in0=YA[:, goff * TC:goff * TC + Q4],
                                in1=ps_y[:, :], op=ALU.add))
                            prev_mq = [c.tok("v"), c.tok("v")]
                        if n == 8:
                            z_group(g4)
                    c.wait("t", c.tok("v"))
                c.relax = False
            scan_done = c.tok("v")
            stDT.close()
            with ExitStack() as st2:
                YB = st2.enter_context(nc.sbuf_tensor(_un("mYB"), [128, DIB * TC], BF16))
                HH = st2.enter_context(nc.sbuf_tensor(_un("mHH"), [128, DB * TC], F32))
                WO2 = [st2.enter_context(nc.sbuf_tensor(_un(f"mWOU{i}"), [128, DIB * 128], BF16))
                       for i in range(2)]

                def wo_dma(chunk, p):
                    c.wait("q", wo_guard[p])
                    wsrc = outwb[l].rearrange("(dib p) e -> p dib e", p=128)[
                        :, :, chunk * 128:(chunk + 1) * 128]
                    wdst = WO2[p][:, :].rearrange("p (dib e) -> p dib e", dib=DIB)
                    return (c.dma("q", f"wo{p}a", wdst[:, 0:8], wsrc[:, 0:8]),
                            c.dma("q", f"wo{p}b", wdst[:, 8:16], wsrc[:, 8:16]))

                wo_guard = [scan_done, scan_done]
                tokw = {0: wo_dma(0, 0), 1: wo_dma(1, 1)}
                c.wait("v", z_done[0])
                c.inc("v", v.tensor_tensor(out=seg(YB[:, :], DIB, TC),
                                           in0=seg(YA[:, :], DIB, TC),
                                           in1=seg(SZ[:, :], DIB, TC), op=ALU.mult))
                yb_done = c.tok("v")
                evs = []
                for dbo in range(DB):
                    p = dbo % 2
                    c.wait("t", tokw[p][0], tokw[p][1], yb_done)
                    psb = ps_m[dbo % 2]
                    if pslast[dbo % 2] is not None:
                        c.wait("t", pslast[dbo % 2])
                    for dib in range(DIB):
                        mm = t_.matmul(psb[:, :], lhsT=WO2[p][:, dib * 128:(dib + 1) * 128],
                                       rhs=YB[:, dib * TC:(dib + 1) * TC],
                                       start=(dib == 0), stop=(dib == DIB - 1))
                    mmt = c.inc("t", mm)
                    wo_guard[p] = mmt
                    if dbo + 2 < DB:
                        tokw[p] = wo_dma(dbo + 2, p)
                    c.wait("s", mmt)
                    ev = c.inc("s", s.copy(out=HH[:, dbo * TC:(dbo + 1) * TC],
                                           in_=psb[:, :]))
                    evs.append(ev)
                    pslast[dbo % 2] = ev
                with ExitStack() as st3:
                    XB2 = st3.enter_context(nc.sbuf_tensor(_un("mXB2b"), [128, DB * TC], BF16))
                    SQ2 = st3.enter_context(nc.sbuf_tensor(_un("mSQ2b"), [128, DB * TC], BF16))
                    ps_s = st3.enter_context(nc.psum_tensor(_un("mps_sb"), [128, TC], F32))
                    ps_q = st3.enter_context(nc.psum_tensor(_un("mps_qb"), [128, TC], F32))
                    c.wait("s", evs[-1])
                    emit_ln_stats(c, env, lambda i: HH[:, i * TC:(i + 1) * TC], DB,
                                  MU, RSTD, XB2, SQ2, ps_s, ps_q, T1, T2, 1.0 / D,
                                  src_big=HH[:, :])
                    HHs = seg(HH[:, :], DB, TC)
                    c.inc("v", v.tensor_tensor(out=HHs, in0=HHs,
                                               in1=fbc(MU[:, :], DB, TC),
                                               op=ALU.subtract))
                    c.inc("v", v.tensor_tensor(out=HHs, in0=HHs,
                                               in1=fbc(RSTD[:, :], DB, TC), op=ALU.mult))
                    c.inc("v", v.tensor_tensor(out=HHs, in0=HHs,
                                               in1=sbc(MV[l][:, 112:120], DB, TC),
                                               op=ALU.mult))
                    c.inc("v", v.tensor_tensor(out=HHs, in0=HHs,
                                               in1=sbc(MV[l][:, 120:128], DB, TC),
                                               op=ALU.add))
                    c.inc("v", v.tensor_tensor(out=seg(XX[:, :], DB, TC),
                                               in0=seg(XX[:, :], DB, TC),
                                               in1=HHs, op=ALU.add))
                c.barrier()
        c.inc("v", v.tensor_tensor(out=seg(X[:, :], DB, TC), in0=seg(X[:, :], DB, TC),
                                   in1=seg(XX[:, :], DB, TC), op=ALU.add))
def host_prep(inputs):
    f32 = np.float32
    asc = np.ascontiguousarray
    consts = np.tile(np.array([0.0, 1.0, LNEPS, 0.5, -0.5, -1.0, C30, 127.0], f32),
                     (128, 1))

    def vec8(x):
        return asc(np.asarray(x, f32).reshape(8, 128).T)

    def vec16(x):
        return asc(np.asarray(x, f32).reshape(16, 128).T)

    avec = np.concatenate([vec8(inputs["L0_attn_a_time_mix_k"]),
                           vec8(inputs["L0_attn_a_time_mix_v"]),
                           vec8(inputs["L0_attn_a_time_mix_r"]),
                           vec8(inputs["L0_attn_a_time_decay"]),
                           vec8(inputs["L0_attn_a_time_first"])], axis=1)
    fvec = np.concatenate([asc(np.asarray(inputs["L0_ffn_b1"], f32).reshape(32, 128).T),
                           vec8(inputs["L0_ffn_b2"])], axis=1)
    shared = {
        "consts": consts,
        "identb": np.eye(128, dtype=ml_dtypes.bfloat16),
        "identf": np.eye(128, dtype=f32),
        "onesb": np.ones((128, 128), dtype=ml_dtypes.bfloat16),
        "onesf": np.ones((1, 512), dtype=f32),
        "selb": np.kron(np.eye(16), np.ones((1, 128))).astype(ml_dtypes.bfloat16),
        "wk": asc(np.asarray(inputs["L0_attn_a_W_k"], f32).T),
        "wv": asc(np.asarray(inputs["L0_attn_a_W_v"], f32).T),
        "wr": asc(np.asarray(inputs["L0_attn_a_W_r"], f32).T),
        "wo": asc(np.asarray(inputs["L0_attn_a_W_o"], f32).T),
        "avec": avec,
        "w1t": asc(np.asarray(inputs["L0_ffn_W1"], f32).T),
        "w2t": asc(np.asarray(inputs["L0_ffn_W2"], f32).T),
        "fvec": fvec,
    }
    for l in range(2):
        shared[f"inwt{l}"] = asc(np.asarray(inputs["L0_attn_b_in_proj_w"][l], f32).T)
        shared[f"outwt{l}"] = asc(np.asarray(inputs["L0_attn_b_out_proj_w"][l], f32).T)
        xp = np.zeros((DI, 128), f32)
        xp[:, 0:96] = np.asarray(inputs["L0_attn_b_x_proj_w"][l], f32).T
        shared[f"xpwt{l}"] = asc(xp)
        shared[f"dtwt{l}"] = asc(np.asarray(inputs["L0_attn_b_dt_proj_w"][l], f32).T)
        cwl = asc(np.asarray(inputs["L0_attn_b_conv_w"][l], f32)
                  .reshape(16, 128, 4).transpose(1, 0, 2).reshape(128, 64))
        mv = np.concatenate([
            cwl,
            vec16(inputs["L0_attn_b_conv_b"][l]),
            vec16(inputs["L0_attn_b_dt_proj_b"][l]),
            vec16(inputs["L0_attn_b_Dskip"][l]),
            vec8(inputs["L0_attn_b_ln_w"][l]),
            vec8(inputs["L0_attn_b_ln_b"][l]),
        ], axis=1)
        shared[f"mvec{l}"] = asc(mv)
        al = np.asarray(inputs["L0_attn_b_A_log"][l], f32)
        shared[f"alog{l}"] = asc(al.reshape(16, 128, 16).transpose(1, 0, 2).reshape(128, 256))
    x = np.asarray(inputs["L0_attn_a_x"], f32)
    in_maps = []
    for core in range(8):
        m = dict(shared)
        m["xin"] = asc(x[core * BC:(core + 1) * BC].transpose(0, 2, 1))
        in_maps.append(m)
    return in_maps


_NC_CACHE = {}


def kernel(**inputs):
    in_maps = host_prep(inputs)
    if "prog" not in _NC_CACHE:
        _NC_CACHE["prog"] = build_program()
    nc = _NC_CACHE["prog"]
    res = run_bass_kernel_spmd(nc, in_maps, core_ids=list(range(8)))
    outs = [np.asarray(res.results[core]["xout"]).transpose(0, 2, 1) for core in range(8)]
    return np.ascontiguousarray(np.concatenate(outs, axis=0).astype(np.float32))
